# version 19
# baseline (speedup 1.0000x reference)
import numpy as np
import ml_dtypes
import concourse.bass as bass
import concourse.mybir as mybir
from concourse.bass_utils import run_bass_kernel_spmd

F32 = mybir.dt.float32
BF16 = mybir.dt.bfloat16
AF = mybir.ActivationFunctionType
ALU = mybir.AluOpType
AX = mybir.AxisListType

T = 2048
D = 1024
NT = 16
KD = 8
H = 8
DV = 128
DF = 256
NCH = 16
FF = 2816
NFK = 22
QW = 704
WSL = FF // 8
LN16 = float(np.log(16.0))
QC0 = float(np.float32((4.0 / 3.0) * 2 ** 22))
QC1 = float(np.float32(2 ** 22 / 6.0))

DEBUG_TAPS = False


class SemC:
    def __init__(self, stack, nc, name):
        self.h = stack.enter_context(nc.semaphore(name))
        self.n = 0

    def inc(self, instr, k=1):
        instr.then_inc(self.h, k)
        self.n += k
        return instr

    def wait(self, eng):
        eng.wait_ge(self.h, self.n)

    def mark(self):
        return self.n

    def wait_n(self, eng, n):
        eng.wait_ge(self.h, n)


def _quant_tile(block, s, bufs, src_ap, n, dq_ap, pre=("VD",)):
    nb = n // 16
    amax, sc, rec, v, w, M, tq = (bufs[k] for k in
                                  ("amax", "sc", "rec", "v", "w", "M", "tq"))
    p = src_ap.shape[0]

    @block.vector
    def _(vector):
        for k in pre:
            s[k].wait(vector)
        src3 = src_ap.rearrange("p (nb k) -> p nb k", k=16)
        s["VD"].inc(vector.tensor_reduce(amax[0:p, 0:nb].unsqueeze(2), src3, AX.X,
                                         ALU.max, apply_absolute_value=True))
        s["VD"].wait(vector)
        s["VD"].inc(vector.tensor_scalar(sc[0:p, 0:nb], amax[0:p, 0:nb],
                                         1.0 / 6.0, 1e-12, ALU.mult, ALU.max))
        s["VD"].wait(vector)
        s["VD"].inc(vector.reciprocal(rec[0:p, 0:nb], sc[0:p, 0:nb]))
        s["VD"].wait(vector)
        src3 = src_ap.rearrange("p (nb k) -> p nb k", k=16)
        rbc = rec[0:p, 0:nb].unsqueeze(2).broadcast_to([p, nb, 16])
        s["VD"].inc(vector.tensor_tensor(
            v[0:p, 0:n].rearrange("p (nb k) -> p nb k", k=16), src3, rbc, ALU.mult))

    @block.scalar
    def _(scalar):
        s["VD"].wait(scalar)
        s["AD"].inc(scalar.activation(w[0:p, 0:n], v[0:p, 0:n], AF.Square))

    @block.vector
    def _(vector):
        s["AD"].wait(vector)
        s["VD"].inc(vector.tensor_scalar(M[0:p, 0:n], w[0:p, 0:n], QC1, QC0,
                                         ALU.mult, ALU.add))
        s["VD"].wait(vector)
        s["VD"].inc(vector.tensor_tensor(tq[0:p, 0:n], v[0:p, 0:n], M[0:p, 0:n],
                                         ALU.add))
        s["VD"].wait(vector)
        s["VD"].inc(vector.tensor_tensor(tq[0:p, 0:n], tq[0:p, 0:n], M[0:p, 0:n],
                                         ALU.subtract))
        s["VD"].wait(vector)
        sbc = sc[0:p, 0:nb].unsqueeze(2).broadcast_to([p, nb, 16])
        s["VD"].inc(vector.tensor_tensor(
            dq_ap.rearrange("p (nb k) -> p nb k", k=16),
            tq[0:p, 0:n].rearrange("p (nb k) -> p nb k", k=16), sbc, ALU.mult))




def _quant2(block, s, bufs, src_ap, n, dq_ap, pre=("VD",)):
    nb = n // 16
    amax, sc, rec, v, w, M, tq = (bufs[k] for k in
                                  ("amax", "sc", "rec", "v", "w", "M", "tq"))
    p = src_ap.shape[0]

    @block.vector
    def _(vector):
        for k in pre:
            s[k].wait(vector)
        src3 = src_ap.rearrange("p (nb k) -> p nb k", k=16)
        s["VD"].inc(vector.tensor_reduce(amax[0:p, 0:nb].unsqueeze(2), src3, AX.X,
                                         ALU.max, apply_absolute_value=True))
        s["VD"].wait(vector)
        s["VD"].inc(vector.tensor_scalar(sc[0:p, 0:nb], amax[0:p, 0:nb],
                                         1.0 / 6.0, 1e-12, ALU.mult, ALU.max))
        s["VD"].wait(vector)
        s["VD"].inc(vector.reciprocal(rec[0:p, 0:nb], sc[0:p, 0:nb]))
        s["VD"].wait(vector)
        src3 = src_ap.rearrange("p (nb k) -> p nb k", k=16)
        rbc = rec[0:p, 0:nb].unsqueeze(2).broadcast_to([p, nb, 16])
        s["VD"].inc(vector.tensor_tensor(
            v[0:p, 0:n].rearrange("p (nb k) -> p nb k", k=16), src3, rbc, ALU.mult))

    @block.gpsimd
    def _(gpsimd):
        s["VD"].wait(gpsimd)
        s["GD"].inc(gpsimd.tensor_tensor(w[0:p, 0:n], v[0:p, 0:n], v[0:p, 0:n],
                                         ALU.mult))

    @block.vector
    def _(vector):
        s["GD"].wait(vector)
        s["VD"].inc(vector.tensor_scalar(M[0:p, 0:n], w[0:p, 0:n], QC1, QC0,
                                         ALU.mult, ALU.add))

    @block.gpsimd
    def _(gpsimd):
        s["VD"].wait(gpsimd)
        s["GD"].inc(gpsimd.tensor_tensor(tq[0:p, 0:n], v[0:p, 0:n], M[0:p, 0:n],
                                         ALU.add))

    @block.vector
    def _(vector):
        s["GD"].wait(vector)
        s["VD"].inc(vector.tensor_tensor(tq[0:p, 0:n], tq[0:p, 0:n], M[0:p, 0:n],
                                         ALU.subtract))
        s["VD"].wait(vector)
        sbc = sc[0:p, 0:nb].unsqueeze(2).broadcast_to([p, nb, 16])
        s["VD"].inc(vector.tensor_tensor(
            dq_ap.rearrange("p (nb k) -> p nb k", k=16),
            tq[0:p, 0:n].rearrange("p (nb k) -> p nb k", k=16), sbc, ALU.mult))


def _barrier(block, s):
    tgt = s["PH"].n + 5
    s["PH"].n = tgt
    for engname in ("sync", "tensor", "vector", "scalar", "gpsimd"):
        def mk():
            def _(eng):
                for k in ("VD", "AD", "PD", "GD", "EV"):
                    s[k].wait(eng)
                eng.sem_inc(s["PH"].h, 1)
                eng.wait_ge(s["PH"].h, tgt)
            return _
        getattr(block, engname)(mk())


def build_nc():
    import contextlib
    stack = contextlib.ExitStack()
    nc = bass.Bass("TRN2", target_bir_lowering=False)

    x_in = nc.dram_tensor("x", [T, D], F32, kind="ExternalInput").ap()
    y_out = nc.dram_tensor("y", [T, D], F32, kind="ExternalOutput").ap()
    wq_t = nc.dram_tensor("wq_t", [H, D, 128], F32, kind="ExternalInput").ap()
    wk_t = nc.dram_tensor("wk_t", [H, D, 128], F32, kind="ExternalInput").ap()
    wv_T = nc.dram_tensor("wv_T", [D, D], F32, kind="ExternalInput").ap()
    wo_T = nc.dram_tensor("wo_T", [D, D], F32, kind="ExternalInput").ap()
    fmq_T = nc.dram_tensor("fmq_T", [128, 128], F32, kind="ExternalInput").ap()
    fmk_T = nc.dram_tensor("fmk_T", [128, 128], F32, kind="ExternalInput").ap()
    fmq_b = nc.dram_tensor("fmq_b", [128, 1], F32, kind="ExternalInput").ap()
    fmk_b = nc.dram_tensor("fmk_b", [128, 1], F32, kind="ExternalInput").ap()
    rms_c = nc.dram_tensor("rms_c", [128, 1], F32, kind="ExternalInput").ap()
    wg_sl = nc.dram_tensor("wg_sl", [WSL, D], F32, kind="ExternalInput").ap()
    wu_sl = nc.dram_tensor("wu_sl", [WSL, D], F32, kind="ExternalInput").ap()
    wd_sl = nc.dram_tensor("wd_sl", [128, FF], F32, kind="ExternalInput").ap()
    smask = nc.dram_tensor("smask", [128, 8], F32, kind="ExternalInput").ap()

    scr_xn = nc.dram_tensor("scr_xn", [T, D], BF16).ap()
    scr_xnq = nc.dram_tensor("scr_xnq", [T, D], BF16).ap()
    scr_h = nc.dram_tensor("scr_h", [T, FF], BF16).ap()
    scr_kf = nc.dram_tensor("scr_kf", [H * DF, T], BF16).ap()
    scr_qf = nc.dram_tensor("scr_qf", [H * DF, T], BF16).ap()
    cc_wg_i = nc.dram_tensor("cc_wg_i", [WSL, D], BF16).ap()
    cc_wg_o = nc.dram_tensor("cc_wg_o", [FF, D], BF16).ap()
    cc_wu_i = nc.dram_tensor("cc_wu_i", [WSL, D], BF16).ap()
    cc_wu_o = nc.dram_tensor("cc_wu_o", [FF, D], BF16).ap()
    cc_wd_i = nc.dram_tensor("cc_wd_i", [128, FF], BF16).ap()
    cc_wd_o = nc.dram_tensor("cc_wd_o", [D, FF], BF16).ap()
    cc_S_i = nc.dram_tensor("cc_S_i", [128, H * DF], BF16).ap()
    cc_S_o = nc.dram_tensor("cc_S_o", [8 * 128, H * DF], BF16).ap()

    s = {k: SemC(stack, nc, k) for k in
         ("DS", "DA", "DG", "DSK", "DSQ", "DSS", "CC", "PH", "PD", "VD", "AD", "GD", "EV")}

    block = stack.enter_context(nc.Block())

    def alloc(name, shape, dt):
        return stack.enter_context(nc.sbuf_tensor(name, shape, dt))

    ones_b = alloc("ones_b", [128, 128], BF16)
    mask4 = alloc("mask4", [128, 512], BF16)
    fmq_sb = alloc("fmq_sb", [128, 128], BF16)
    fmk_sb = alloc("fmk_sb", [128, 128], BF16)
    bcol = alloc("bcol", [128, 8], F32)
    eps_sb = alloc("eps_sb", [128, 1], F32)
    eps2_sb = alloc("eps2_sb", [128, 1], F32)
    nl2_sb = alloc("nl2_sb", [128, 1], F32)
    smask_sb = alloc("smask_sb", [128, 8], F32)
    imask = alloc("imask", [128, 512], F32)
    ident_b = alloc("ident_b", [128, 128], BF16)
    BQP, BQM, BKP, BKM, RMS, FQB, FKB = range(7)

    def bc(i):
        return bcol[:, i:i + 1]

    @block.scalar
    def _(scalar):
        s["DA"].inc(scalar.dma_start(out=bc(FQB), in_=fmq_b[:, :]), 16)
        s["DA"].inc(scalar.dma_start(out=bc(FKB), in_=fmk_b[:, :]), 16)
        s["DA"].inc(scalar.dma_start(out=bc(RMS), in_=rms_c[:, :]), 16)
        s["DA"].inc(scalar.dma_start(out=smask_sb[:], in_=smask[:, :]), 16)

    @block.gpsimd
    def _(gpsimd):
        gpsimd.memset(ones_b[:], 1.0)
        gpsimd.memset(eps_sb[:, :], 1e-5)
        gpsimd.memset(eps2_sb[:, :], 1e-5 * 2.0 ** -64)
        gpsimd.memset(nl2_sb[:, :], -32.0 * 0.6931471805599453)
        s["GD"].inc(gpsimd.iota(imask[:], pattern=[[1, 512]],
                                channel_multiplier=-1,
                                allow_small_or_imprecise_dtypes=True))
        s["DG"].inc(gpsimd.dma_start(out=fmq_sb[:], in_=fmq_T[:, :]), 16)
        s["DG"].inc(gpsimd.dma_start(out=fmk_sb[:], in_=fmk_T[:, :]), 16)

    @block.vector
    def _(vector):
        s["GD"].wait(vector)
        s["DA"].wait(vector)
        vector.tensor_scalar(mask4[:], imask[:], 0.0, None, ALU.is_ge)
        vector.tensor_scalar(ident_b[:], imask[:, 0:128], 0.0, None, ALU.is_equal)
        vector.tensor_scalar(bc(BQP), bc(FQB), -LN16, None, ALU.add)
        vector.tensor_scalar(bc(BQM), bc(FQB), -1.0, -LN16, ALU.mult, ALU.add)
        vector.tensor_copy(bc(BKP), bc(FKB))
        s["VD"].inc(vector.tensor_scalar(bc(BKM), bc(FKB), -1.0, None, ALU.mult))

    def _phase0():
        CW = 4 * D
        with (
            nc.sbuf_tensor("x_sb", [128, 2 * CW], F32) as x_sb,
            nc.sbuf_tensor("wsl_sb", [128, 3 * D + FF], F32) as wsl_sb,
            nc.sbuf_tensor("qS", [128, 5 * 320], F32) as qS,
            nc.sbuf_tensor("qB", [128, 4 * 4880], F32) as qB,
            nc.sbuf_tensor("dq_b", [128, 2 * CW], BF16) as dq_b,
            nc.sbuf_tensor("ln_st", [128, 6 * NT], F32) as ln_st,
            nc.sbuf_tensor("xn_f", [128, 2 * CW], F32) as xn_f,
            nc.sbuf_tensor("xn_b", [128, 2 * CW], BF16) as xn_b,
        ):
            QBW = 4880
            qbufs = dict(amax=qS[:, 0:320], sc=qS[:, 320:640],
                         rec=qS[:, 640:960],
                         v=qB[:, 0:QBW], w=qB[:, QBW:2 * QBW],
                         M=qB[:, 2 * QBW:3 * QBW], tq=qB[:, 3 * QBW:4 * QBW])

            ST_SUM, ST_SQ, ST_MEAN, ST_VAR, ST_RSTD, ST_NMR = (
                0, NT, 2 * NT, 3 * NT, 4 * NT, 5 * NT)

            for ck in range(NT // 4):
                cb = ck % 2
                sqs = qB[:, (1 + cb) * CW:(2 + cb) * CW]

                @block.sync
                def _(sync, ck=ck, cb=cb):
                    if ck >= 2:
                        s["VD"].wait(sync)
                    s["DS"].inc(sync.dma_start(
                        out=x_sb[:, cb * CW:(cb + 1) * CW].rearrange(
                            "p (tt d) -> p tt d", d=D),
                        in_=x_in[ck * 512:(ck + 1) * 512, :].rearrange(
                            "(tt p) d -> p tt d", p=128)), 16)

                @block.gpsimd
                def _(gpsimd, ck=ck, cb=cb, sqs=sqs):
                    s["DS"].wait(gpsimd)
                    if ck >= 2:
                        s["VD"].wait(gpsimd)
                    s["GD"].inc(gpsimd.tensor_tensor(
                        sqs, x_sb[:, cb * CW:(cb + 1) * CW],
                        x_sb[:, cb * CW:(cb + 1) * CW], ALU.mult))

                @block.vector
                def _(vector, ck=ck, cb=cb, sqs=sqs):
                    s["DS"].wait(vector)
                    for j in range(4):
                        t = ck * 4 + j
                        vector.tensor_reduce(
                            ln_st[:, ST_SUM + t:ST_SUM + t + 1].unsqueeze(2),
                            x_sb[:, cb * CW + j * D:cb * CW + (j + 1) * D].unsqueeze(1),
                            AX.X, ALU.add)
                    s["GD"].wait(vector)
                    li = None
                    for j in range(4):
                        t = ck * 4 + j
                        li = vector.tensor_reduce(
                            ln_st[:, ST_SQ + t:ST_SQ + t + 1].unsqueeze(2),
                            sqs[:, j * D:(j + 1) * D].unsqueeze(1), AX.X, ALU.add)
                    s["VD"].inc(li)

            @block.vector
            def _(vector):
                s["VD"].wait(vector)
                s["VD"].inc(vector.tensor_scalar(ln_st[:, ST_MEAN:ST_MEAN + NT],
                                                 ln_st[:, ST_SUM:ST_SUM + NT],
                                                 1.0 / D, None, ALU.mult))
                s["VD"].inc(vector.tensor_scalar(ln_st[:, ST_VAR:ST_VAR + NT],
                                                 ln_st[:, ST_SQ:ST_SQ + NT],
                                                 1.0 / D, None, ALU.mult))
                s["VD"].wait(vector)
                s["VD"].inc(vector.tensor_tensor(ln_st[:, ST_SQ:ST_SQ + NT],
                                                 ln_st[:, ST_MEAN:ST_MEAN + NT],
                                                 ln_st[:, ST_MEAN:ST_MEAN + NT], ALU.mult))
                s["VD"].wait(vector)
                s["VD"].inc(vector.tensor_tensor(ln_st[:, ST_VAR:ST_VAR + NT],
                                                 ln_st[:, ST_VAR:ST_VAR + NT],
                                                 ln_st[:, ST_SQ:ST_SQ + NT], ALU.subtract))

            @block.scalar
            def _(scalar):
                s["VD"].wait(scalar)
                scalar.activation(ln_st[:, ST_SUM:ST_SUM + NT], ln_st[:, ST_VAR:ST_VAR + NT],
                                  AF.Ln, bias=eps_sb[:, 0:1])
                s["AD"].inc(scalar.activation(ln_st[:, ST_RSTD:ST_RSTD + NT],
                                              ln_st[:, ST_SUM:ST_SUM + NT],
                                              AF.Exp, scale=-0.5))

            @block.vector
            def _(vector):
                s["AD"].wait(vector)
                s["VD"].inc(vector.scalar_tensor_tensor(
                    ln_st[:, ST_NMR:ST_NMR + NT], ln_st[:, ST_MEAN:ST_MEAN + NT], -1.0,
                    ln_st[:, ST_RSTD:ST_RSTD + NT], ALU.mult, ALU.mult))

            dsm_x = {}
            gdm_x = {}
            scp = (qS[:, 960:1280], qS[:, 1280:1600])

            def xq_p1(cki):
                cb = cki % 2
                srcx = xn_f[:, cb * CW:(cb + 1) * CW]
                NBC = CW // 16

                @block.vector
                def _(vector, cki=cki, cb=cb):
                    s["AD"].wait(vector)
                    s["DG"].wait(vector)
                    if cki >= 1:
                        s["GD"].wait_n(vector, gdm_x[cki - 1])
                    src3 = srcx.rearrange("p (nb k) -> p nb k", k=16)
                    s["VD"].inc(vector.tensor_reduce(
                        qbufs["amax"][:, 0:NBC].unsqueeze(2), src3, AX.X, ALU.max,
                        apply_absolute_value=True))
                    s["VD"].wait(vector)
                    s["VD"].inc(vector.tensor_scalar(
                        scp[cb][:, 0:NBC], qbufs["amax"][:, 0:NBC],
                        1.0 / 6.0, 1e-12, ALU.mult, ALU.max))
                    s["VD"].wait(vector)
                    s["VD"].inc(vector.reciprocal(qbufs["rec"][:, 0:NBC],
                                                  scp[cb][:, 0:NBC]))
                    s["VD"].wait(vector)
                    rbc = qbufs["rec"][:, 0:NBC].unsqueeze(2).broadcast_to(
                        [128, NBC, 16])
                    s["VD"].inc(vector.tensor_tensor(
                        qbufs["v"][:, 0:CW].rearrange("p (nb k) -> p nb k", k=16),
                        src3, rbc, ALU.mult))

                @block.gpsimd
                def _(gpsimd):
                    s["VD"].wait(gpsimd)
                    s["GD"].inc(gpsimd.tensor_tensor(
                        qbufs["w"][:, 0:CW], qbufs["v"][:, 0:CW],
                        qbufs["v"][:, 0:CW], ALU.mult))

            def xq_p2(cki):
                gmx = s["GD"].mark()

                @block.vector
                def _(vector):
                    s["GD"].wait_n(vector, gmx)
                    s["VD"].inc(vector.tensor_scalar(
                        qbufs["M"][:, 0:CW], qbufs["w"][:, 0:CW], QC1, QC0,
                        ALU.mult, ALU.add))

                @block.gpsimd
                def _(gpsimd, cki=cki):
                    s["VD"].wait(gpsimd)
                    s["GD"].inc(gpsimd.tensor_tensor(
                        qbufs["tq"][:, 0:CW], qbufs["v"][:, 0:CW],
                        qbufs["M"][:, 0:CW], ALU.add))
                    gdm_x[cki] = s["GD"].mark()

            def xq_p3(cki):
                cb = cki % 2
                gmx = s["GD"].mark()

                @block.vector
                def _(vector, cki=cki, cb=cb):
                    s["GD"].wait_n(vector, gmx)
                    if cki >= 2:
                        s["DS"].wait_n(vector, dsm_x[cki - 1])
                    s["VD"].inc(vector.tensor_tensor(
                        qbufs["tq"][:, 0:CW], qbufs["tq"][:, 0:CW],
                        qbufs["M"][:, 0:CW], ALU.subtract))
                    s["VD"].wait(vector)
                    sbc = scp[cb][:, 0:CW // 16].unsqueeze(2).broadcast_to(
                        [128, CW // 16, 16])
                    s["VD"].inc(vector.tensor_tensor(
                        dq_b[:, cb * CW:(cb + 1) * CW].rearrange(
                            "p (nb k) -> p nb k", k=16),
                        qbufs["tq"][:, 0:CW].rearrange("p (nb k) -> p nb k", k=16),
                        sbc, ALU.mult))

                @block.sync
                def _(sync, cki=cki, cb=cb):
                    s["VD"].wait(sync)
                    s["DS"].inc(sync.dma_start(
                        out=scr_xnq[cki * 512:(cki + 1) * 512, :].rearrange(
                            "(tt p) d -> p tt d", p=128),
                        in_=dq_b[:, cb * CW:(cb + 1) * CW].rearrange(
                            "p (tt d) -> p tt d", d=D)), 16)
                dsm_x[cki] = s["DS"].mark()

            for cki in range(NT // 4):
                cb = cki % 2

                @block.sync
                def _(sync, cki=cki, cb=cb):
                    s["AD"].wait(sync)
                    s["DS"].inc(sync.dma_start(
                        out=x_sb[:, cb * CW:(cb + 1) * CW].rearrange(
                            "p (tt d) -> p tt d", d=D),
                        in_=x_in[cki * 512:(cki + 1) * 512, :].rearrange(
                            "(tt p) d -> p tt d", p=128)), 16)

                @block.scalar
                def _(scalar, cki=cki, cb=cb):
                    s["DS"].wait(scalar)
                    s["VD"].wait(scalar)
                    for j in range(4):
                        t = cki * 4 + j
                        scalar.activation(xn_f[:, cb * CW + j * D:cb * CW + (j + 1) * D],
                                          x_sb[:, cb * CW + j * D:cb * CW + (j + 1) * D],
                                          AF.Identity,
                                          bias=ln_st[:, ST_NMR + t:ST_NMR + t + 1],
                                          scale=ln_st[:, ST_RSTD + t:ST_RSTD + t + 1])
                    for j in range(4):
                        t = cki * 4 + j
                        li = scalar.activation(
                            xn_b[:, cb * CW + j * D:cb * CW + (j + 1) * D],
                            x_sb[:, cb * CW + j * D:cb * CW + (j + 1) * D],
                            AF.Identity, bias=ln_st[:, ST_NMR + t:ST_NMR + t + 1],
                            scale=ln_st[:, ST_RSTD + t:ST_RSTD + t + 1])
                    s["AD"].inc(li)

                @block.sync
                def _(sync, cki=cki, cb=cb):
                    s["AD"].wait(sync)
                    s["DS"].inc(sync.dma_start(
                        out=scr_xn[cki * 512:(cki + 1) * 512, :].rearrange(
                            "(tt p) d -> p tt d", p=128),
                        in_=xn_b[:, cb * CW:(cb + 1) * CW].rearrange(
                            "p (tt d) -> p tt d", d=D)), 16)

                if cki >= 1:
                    xq_p1(cki - 1)
                if cki >= 2:
                    xq_p3(cki - 2)
                if cki >= 1:
                    xq_p2(cki - 1)

            xq_p1(NT // 4 - 1)
            xq_p3(NT // 4 - 2)
            xq_p2(NT // 4 - 1)
            xq_p3(NT // 4 - 1)

            @block.scalar
            def _(scalar):
                s["DA"].inc(scalar.dma_start(out=wsl_sb[:, 0:D], in_=wg_sl[0:128, :]), 16)
                s["DA"].inc(scalar.dma_start(out=wsl_sb[:, D:2 * D], in_=wu_sl[0:128, :]), 16)
                s["DA"].inc(scalar.dma_start(out=wsl_sb[:, 2 * D:3 * D],
                                             in_=wd_sl[0:128, 0:D]), 16)
                s["DA"].inc(scalar.dma_start(out=wsl_sb[:, 3 * D:3 * D + FF - D],
                                             in_=wd_sl[0:128, D:FF]), 16)

            @block.vector
            def _(vector):
                s["DA"].wait(vector)

            _quant2(block, s, qbufs, wsl_sb[:, 0:2 * D + FF], 2 * D + FF,
                    dq_b[:, 0:2 * D + FF])

            @block.gpsimd
            def _(gpsimd):
                s["VD"].wait(gpsimd)
                s["DG"].inc(gpsimd.dma_start(out=cc_wg_i[0:128, :],
                                             in_=dq_b[:, 0:D]), 16)
                s["DG"].inc(gpsimd.dma_start(out=cc_wu_i[0:128, :],
                                             in_=dq_b[:, D:2 * D]), 16)
                s["DG"].inc(gpsimd.dma_start(out=cc_wd_i[0:128, :],
                                             in_=dq_b[:, 2 * D:2 * D + FF]), 16)

            @block.scalar
            def _(scalar):
                s["VD"].wait(scalar)
                s["DG"].wait(scalar)
                s["DA"].wait(scalar)
                s["DA"].inc(scalar.dma_start(out=wsl_sb[:, 0:D], in_=wg_sl[128:256, :]), 16)
                s["DA"].inc(scalar.dma_start(out=wsl_sb[:, D:2 * D],
                                             in_=wu_sl[128:256, :]), 16)
                s["DA"].inc(scalar.dma_start(out=wsl_sb[0:96, 2 * D:3 * D],
                                             in_=wg_sl[256:352, :]), 16)
                s["DA"].inc(scalar.dma_start(out=wsl_sb[0:96, 3 * D:4 * D],
                                             in_=wu_sl[256:352, :]), 16)

            @block.vector
            def _(vector):
                s["DA"].wait(vector)

            _quant2(block, s, qbufs, wsl_sb[:, 0:2 * D], 2 * D, dq_b[:, 0:2 * D])

            @block.gpsimd
            def _(gpsimd):
                s["VD"].wait(gpsimd)
                s["DG"].inc(gpsimd.dma_start(out=cc_wg_i[128:256, :],
                                             in_=dq_b[:, 0:D]), 16)
                s["DG"].inc(gpsimd.dma_start(out=cc_wu_i[128:256, :],
                                             in_=dq_b[:, D:2 * D]), 16)

            _quant2(block, s, qbufs, wsl_sb[0:96, 2 * D:4 * D], 2 * D,
                    dq_b[0:96, 2 * D:4 * D])

            @block.gpsimd
            def _(gpsimd):
                s["VD"].wait(gpsimd)
                s["DG"].inc(gpsimd.dma_start(out=cc_wg_i[256:352, :],
                                             in_=dq_b[0:96, 2 * D:3 * D]), 16)
                s["DG"].inc(gpsimd.dma_start(out=cc_wu_i[256:352, :],
                                             in_=dq_b[0:96, 3 * D:4 * D]), 16)

            @block.gpsimd
            def _(gpsimd):
                s["DG"].wait(gpsimd)
                for ci, co in ((cc_wg_i, cc_wg_o), (cc_wu_i, cc_wu_o), (cc_wd_i, cc_wd_o)):
                    s["CC"].inc(gpsimd.collective_compute(
                        "AllGather", ALU.bypass, replica_groups=[list(range(8))],
                        ins=[ci[:, :].opt()], outs=[co[:, :].opt()]))


            @block.sync
            def _(sync):
                s["DS"].wait(sync)

            _barrier(block, s)

    _phase0()

    scr_o = nc.dram_tensor("scr_o", [128, H * T], BF16, kind="Internal").ap()

    def _phase2():
        with (
            nc.sbuf_tensor("xnT", [128, KD * T], BF16) as xnT,
            nc.sbuf_tensor("v_sb", [128, NT * D], BF16) as v_sb,
            nc.sbuf_tensor("wv_sb", [128, KD * D], BF16) as wv_sb,
            nc.sbuf_tensor("wqk", [128, 4 * D], BF16) as wqk,
            nc.sbuf_tensor("qkraw", [128, 2 * T], BF16) as qkraw,
            nc.sbuf_tensor("feat", [128, 4 * T], BF16) as feat,
            nc.sbuf_tensor("kfT", [128, NCH * DF], BF16) as kfT,
            nc.sbuf_tensor("attnT", [128, NCH * 128], BF16) as attnT,
            nc.sbuf_tensor("Sx", [128, DF + H * DF], BF16) as Sx,
            nc.sbuf_tensor("o_full", [128, H * T], BF16) as o_full,
            nc.sbuf_tensor("echo2", [128, 1], F32) as echo2,
            nc.psum_tensor("pp", [128, 3072], F32) as pp,
            nc.psum_tensor("pt", [128, 2048], BF16) as pt,
        ):
            wqk_r = ((wqk[:, 0:D], wqk[:, D:2 * D]),
                     (wqk[:, 2 * D:3 * D], wqk[:, 3 * D:4 * D]))
            qraw, kraw = qkraw[:, 0:T], qkraw[:, T:2 * T]
            qfp, qfm = feat[:, 0:T], feat[:, T:2 * T]
            kfp, kfm = feat[:, 2 * T:3 * T], feat[:, 3 * T:4 * T]
            S_sb = Sx[:, 0:DF]
            Sstage = Sx[:, DF:DF + H * DF]
            pv = (pp[:, 0:1024], pp[:, 1024:2048])
            pS = pp[:, 0:512]
            pqy = (pp[:, 1024:1536], pp[:, 1536:2048])
            pA = (pp[:, 2048:2560], pp[:, 2560:3072])
            pO = pp[:, 512:1024]

            @block.sync
            def _(sync):
                for k in range(KD):
                    s["DS"].inc(sync.dma_start(out=xnT[:, k * T:(k + 1) * T],
                                               in_=scr_xn[:, k * 128:(k + 1) * 128],
                                               transpose=True), 16)

            @block.gpsimd
            def _(gpsimd):
                for k in range(KD):
                    s["DG"].inc(gpsimd.dma_start(out=wv_sb[:, k * D:(k + 1) * D],
                                                 in_=wv_T[k * 128:(k + 1) * 128, :]), 16)

            vm2 = []
            for t in range(NT):
                pvb = t % 2

                @block.tensor
                def _(tensor, t=t, pvb=pvb):
                    if t == 0:
                        s["DS"].wait(tensor)
                        s["DG"].wait(tensor)
                    if t >= 1:
                        s["EV"].wait_n(tensor, vm2[t - 1])
                    for k in range(KD):
                        lhs = xnT[:, k * T + t * 128:k * T + (t + 1) * 128]
                        tensor.matmul(pv[pvb][:, 0:512], lhs, wv_sb[:, k * D:k * D + 512],
                                      start=(k == 0), stop=(k == KD - 1),
                                      skip_group_check=True)
                        mi = tensor.matmul(pv[pvb][:, 512:1024], lhs,
                                           wv_sb[:, k * D + 512:(k + 1) * D],
                                           start=(k == 0), stop=(k == KD - 1),
                                           skip_group_check=True)
                    s["PD"].inc(mi)

                @block.vector
                def _(vector, t=t, pvb=pvb):
                    s["PD"].wait(vector)
                    s["EV"].inc(vector.tensor_copy(v_sb[:, t * D:(t + 1) * D],
                                                   pv[pvb][:, :]))
                vm2.append(s["EV"].mark())

            pqm = [None] * (16 * H)
            emk = [None] * H
            dskm = [None] * NCH
            aam_all = []
            om_all = []
            scm_all = []
            ktm_all = []
            qrm = []
            for h in range(H):
                wqm_sb, wkm_sb = wqk_r[h % 2]

                @block.gpsimd
                def _(gpsimd, h=h):
                    if h >= 2:
                        s["PD"].wait_n(gpsimd, qrm[h - 1])
                    s["DG"].inc(gpsimd.dma_start(
                        out=wqk_r[h % 2][0].rearrange("p (k c) -> p k c", c=128),
                        in_=wq_t[h].rearrange("(k p) c -> p k c", p=128)), 16)
                    s["DG"].inc(gpsimd.dma_start(
                        out=wqk_r[h % 2][1].rearrange("p (k c) -> p k c", c=128),
                        in_=wk_t[h].rearrange("(k p) c -> p k c", p=128)), 16)

                for wi, (which, wm, rawbuf) in enumerate(
                        (("q", wqm_sb, qraw), ("k", wkm_sb, kraw))):
                    for tc in range(4):
                        pb = tc % 2
                        G2 = 16 * h + wi * 4 + tc

                        @block.tensor
                        def _(tensor, h=h, which=which, wm=wm, tc=tc, pb=pb, G2=G2):
                            if which == "q" and tc == 0:
                                s["DG"].wait(tensor)
                                if h > 0:
                                    s["EV"].wait_n(tensor, ktm_all[-1])
                            if G2 < 1:
                                s["EV"].wait(tensor)
                            else:
                                s["EV"].wait_n(tensor, pqm[G2 - 1])
                            for k in range(KD):
                                mi = tensor.matmul(
                                    pqy[pb], wm[:, k * 128:(k + 1) * 128],
                                    xnT[:, k * T + tc * 512:k * T + (tc + 1) * 512],
                                    start=(k == 0), stop=(k == KD - 1),
                                    skip_group_check=True)
                            s["PD"].inc(mi)

                        @block.vector
                        def _(vector, rawbuf=rawbuf, tc=tc, pb=pb):
                            s["PD"].wait(vector)
                            s["EV"].inc(vector.tensor_copy(
                                rawbuf[:, tc * 512:(tc + 1) * 512], pqy[pb]))
                        pqm[G2] = s["EV"].mark()
                qrm.append(s["PD"].mark())

                for wi, (which, rawbuf, fT, bpi, bmi, fp_, fm_) in enumerate((
                        ("q", qraw, fmq_sb, BQP, BQM, qfp, qfm),
                        ("k", kraw, fmk_sb, BKP, BKM, kfp, kfm))):
                    for tc in range(4):
                        pb = tc % 2
                        G2 = 16 * h + 8 + wi * 4 + tc

                        @block.tensor
                        def _(tensor, which=which, rawbuf=rawbuf, fT=fT, tc=tc,
                              pb=pb, G2=G2):
                            s["EV"].wait_n(tensor, pqm[G2 - 1])
                            if which == "q" and tc == 0:
                                s["DSQ"].wait(tensor)
                                s["DSK"].wait(tensor)
                            mi = tensor.matmul(pqy[pb], fT[:, :],
                                               rawbuf[:, tc * 512:(tc + 1) * 512],
                                               start=True, stop=True,
                                               skip_group_check=True)
                            s["PD"].inc(mi)

                        @block.scalar
                        def _(scalar, tc=tc, pb=pb, bpi=bpi, bmi=bmi, fp_=fp_, fm_=fm_):
                            s["PD"].wait(scalar)
                            scalar.activation(fp_[:, tc * 512:(tc + 1) * 512], pqy[pb],
                                              AF.Exp, bias=bc(bpi), scale=1.0)
                            s["EV"].inc(scalar.activation(
                                fm_[:, tc * 512:(tc + 1) * 512], pqy[pb],
                                AF.Exp, bias=bc(bmi), scale=-1.0))
                        pqm[G2] = s["EV"].mark()

                @block.scalar
                def _(scalar):
                    s["EV"].inc(scalar.activation(echo2[:, :], ones_b[:, 0:1], AF.Exp))
                emk[h] = s["EV"].mark()

                @block.sync
                def _(sync, h=h):
                    s["EV"].wait(sync)
                    s["PD"].wait(sync)
                    s["DSQ"].inc(sync.dma_start(out=scr_qf[h * DF:h * DF + 128, :],
                                                in_=qfp), 16)
                    s["DSQ"].inc(sync.dma_start(out=scr_qf[h * DF + 128:(h + 1) * DF, :],
                                                in_=qfm), 16)

                for b in range(4):
                    pob = 0
                    for i in range(4):
                        c = 4 * b + i
                        nq = (4 - i) * 128
                        q0 = (4 * b + i) * 128

                        @block.tensor
                        def _(tensor, h=h, b=b, i=i, c=c, nq=nq, q0=q0):
                            ai = len(aam_all)
                            if ai >= 2:
                                s["EV"].wait_n(tensor, aam_all[ai - 2])
                            if b == 0 and i == 0:
                                s["EV"].wait_n(tensor, emk[h])
                            pb = i % 2
                            tensor.matmul(pA[pb][:, 0:nq], kfp[:, c * 128:(c + 1) * 128],
                                          qfp[:, q0:q0 + nq], start=True, stop=False,
                                          skip_group_check=True)
                            mi = tensor.matmul(pA[pb][:, 0:nq],
                                               kfm[:, c * 128:(c + 1) * 128],
                                               qfm[:, q0:q0 + nq], start=False,
                                               stop=True, skip_group_check=True)
                            s["PD"].inc(mi)

                        @block.vector
                        def _(vector, i=i, nq=nq):
                            s["PD"].wait(vector)
                            pb = i % 2
                            s["EV"].inc(vector.tensor_tensor(
                                attnT[:, i * 512:i * 512 + nq],
                                pA[pb][:, 0:nq], mask4[:, 0:nq], ALU.mult))
                        aam_all.append(s["EV"].mark())

                    for i in range(4):
                        c = 4 * b + i
                        pt0 = (c % 2) * 1024

                        @block.tensor
                        def _(tensor, h=h, c=c, pt0=pt0):
                            ki = len(ktm_all)
                            if ki >= 2:
                                s["EV"].wait_n(tensor, ktm_all[ki - 2])
                            tensor.matmul(pt[:, pt0:pt0 + 128],
                                          kfp[:, c * 128:(c + 1) * 128], ident_b[:, :],
                                          start=True, stop=False, is_transpose=True,
                                          skip_group_check=True)
                            mi = tensor.matmul(pt[:, pt0 + 128:pt0 + 256],
                                               kfm[:, c * 128:(c + 1) * 128],
                                               ident_b[:, :],
                                               start=False, stop=True,
                                               is_transpose=True,
                                               skip_group_check=True)
                            s["PD"].inc(mi)

                        @block.vector
                        def _(vector, c=c, pt0=pt0):
                            s["PD"].wait(vector)
                            s["EV"].inc(vector.tensor_copy(
                                kfT[:, c * DF:(c + 1) * DF], pt[:, pt0:pt0 + 256]))
                        ktm_all.append(s["EV"].mark())

                    @block.tensor
                    def _(tensor, h=h, b=b, pob=pob):
                        s["EV"].wait_n(tensor, aam_all[-1])
                        if b >= 1:
                            s["EV"].wait_n(tensor, om_all[-1])
                        elif h > 0:
                            s["EV"].wait_n(tensor, om_all[-1])
                        for i in range(4):
                            c = 4 * b + i
                            nq = (4 - i) * 128
                            vsl = v_sb[:, c * D + h * 128:c * D + (h + 1) * 128]
                            mi = tensor.matmul(pO[:, i * 128:512], vsl,
                                               attnT[:, i * 512:i * 512 + nq],
                                               start=(i == 0),
                                               stop=(b == 0 and i == 3),
                                               skip_group_check=True)
                        if b > 0:
                            tensor.matmul(pO[:, 0:512], S_sb[:, 0:128],
                                          qfp[:, b * 512:(b + 1) * 512],
                                          start=False, stop=False,
                                          skip_group_check=True)
                            mi = tensor.matmul(pO[:, 0:512], S_sb[:, 128:256],
                                               qfm[:, b * 512:(b + 1) * 512],
                                               start=False, stop=True,
                                               skip_group_check=True)
                        s["EV"].wait_n(tensor, ktm_all[-1])
                        if h > 0 and b == 0:
                            s["EV"].wait_n(tensor, scm_all[-1])
                        for i in range(4):
                            c = 4 * b + i
                            vsl = v_sb[:, c * D + h * 128:c * D + (h + 1) * 128]
                            tensor.matmul(pS[:, 0:128], kfT[:, c * DF:c * DF + 128],
                                          vsl, start=(b == 0 and i == 0), stop=False,
                                          skip_group_check=True)
                            mi = tensor.matmul(pS[:, 128:256],
                                               kfT[:, c * DF + 128:c * DF + 256], vsl,
                                               start=False,
                                               stop=(b == 3 and i == 3),
                                               skip_group_check=True)
                        s["PD"].inc(mi)

                    @block.vector
                    def _(vector, h=h, b=b):
                        s["PD"].wait(vector)
                        s["EV"].inc(vector.tensor_copy(
                            o_full[:, h * T + b * 512:h * T + (b + 1) * 512],
                            pO[:, 0:512]))
                        om_all.append(s["EV"].mark())
                        if b < 3:
                            s["EV"].inc(vector.tensor_copy(S_sb[:, :], pS[:, 0:256]))
                        else:
                            s["EV"].inc(vector.tensor_copy(
                                Sstage[:, h * DF:(h + 1) * DF], pS[:, 0:256]))
                        scm_all.append(s["EV"].mark())

            @block.sync
            def _(sync):
                s["EV"].wait(sync)
                s["DSS"].inc(sync.dma_start(out=cc_S_i[:, :], in_=Sstage), 16)
                s["DS"].inc(sync.dma_start(out=scr_o[:, :], in_=o_full[:, :]), 16)
                s["DSQ"].wait(sync)
                s["DSK"].wait(sync)
                s["DG"].wait(sync)
                s["DS"].wait(sync)

            @block.gpsimd
            def _(gpsimd):
                s["DSS"].wait(gpsimd)
                s["CC"].inc(gpsimd.collective_compute(
                    "AllGather", ALU.bypass, replica_groups=[list(range(8))],
                    ins=[cc_S_i[:, :].opt()], outs=[cc_S_o[:, :].opt()]))

            _barrier(block, s)

    _phase2()

    def _phase1():
        NSL = FF // 256
        with (
            nc.sbuf_tensor("xnqT", [128, KD * T], BF16) as xnqT,
            nc.sbuf_tensor("wgT", [128, KD * FF], BF16) as wgT,
            nc.sbuf_tensor("wuT", [128, KD * FF], BF16) as wuT,
            nc.sbuf_tensor("sg", [128, 4 * 256], F32) as sg,
            nc.sbuf_tensor("h_sb", [128, 2 * FF], F32) as h_sb,
            nc.sbuf_tensor("fS", [128, 4 * (FF // 16)], F32) as fS,
            nc.sbuf_tensor("fB", [128, 3 * FF], F32) as fB,
            nc.sbuf_tensor("fW", [128, FF], BF16) as fW,
            nc.sbuf_tensor("hq_b", [128, 2 * FF], BF16) as hq_b,
            nc.psum_tensor("pf", [128, 4096], F32) as pf,
        ):
            NB = FF // 16
            fqb = dict(amax=fS[:, 0:NB], sc=fS[:, NB:2 * NB],
                       rec=fS[:, 2 * NB:3 * NB],
                       v=fB[:, 0:FF], w=fW[:, 0:FF],
                       M=fB[:, FF:2 * FF], tq=fB[:, 2 * FF:3 * FF])

            @block.sync
            def _(sync):
                for k in range(KD):
                    s["DS"].inc(sync.dma_start(out=xnqT[:, k * T:(k + 1) * T],
                                               in_=scr_xnq[:, k * 128:(k + 1) * 128],
                                               transpose=True), 16)
                sync.wait_ge(s["CC"].h, 1)
                for k in range(KD):
                    s["DS"].inc(sync.dma_start(out=wgT[:, k * FF:(k + 1) * FF],
                                               in_=cc_wg_o[:, k * 128:(k + 1) * 128],
                                               transpose=True), 16)
                sync.wait_ge(s["CC"].h, 2)
                for k in range(KD):
                    s["DS"].inc(sync.dma_start(out=wuT[:, k * FF:(k + 1) * FF],
                                               in_=cc_wu_o[:, k * 128:(k + 1) * 128],
                                               transpose=True), 16)

            sim1 = []
            hmm1 = []
            dsm1 = {}
            vdm1 = {}

            def q_p1(tt):
                hb = tt % 2
                srcq = h_sb[:, hb * FF:(hb + 1) * FF]

                @block.vector
                def _(vector):
                    s["VD"].wait(vector)
                    src3 = srcq.rearrange("p (nb k) -> p nb k", k=16)
                    s["VD"].inc(vector.tensor_reduce(
                        fqb["amax"][:, 0:NB].unsqueeze(2), src3, AX.X, ALU.max,
                        apply_absolute_value=True))
                    s["VD"].wait(vector)
                    s["VD"].inc(vector.tensor_scalar(
                        fqb["sc"][:, 0:NB], fqb["amax"][:, 0:NB],
                        1.0 / 6.0, 1e-12, ALU.mult, ALU.max))
                    s["VD"].wait(vector)
                    s["VD"].inc(vector.reciprocal(fqb["rec"][:, 0:NB],
                                                  fqb["sc"][:, 0:NB]))
                    s["VD"].wait(vector)
                    rbc = fqb["rec"][:, 0:NB].unsqueeze(2).broadcast_to([128, NB, 16])
                    s["VD"].inc(vector.tensor_tensor(
                        fqb["v"][:, 0:FF].rearrange("p (nb k) -> p nb k", k=16),
                        src3, rbc, ALU.mult))

                @block.gpsimd
                def _(gpsimd):
                    s["VD"].wait(gpsimd)
                    s["GD"].inc(gpsimd.tensor_tensor(
                        fqb["w"][:, 0:FF], fqb["v"][:, 0:FF], fqb["v"][:, 0:FF],
                        ALU.mult))

            def q_p2(tt):
                gm = s["GD"].mark()

                @block.vector
                def _(vector):
                    s["GD"].wait_n(vector, gm)
                    s["VD"].inc(vector.tensor_scalar(
                        fqb["M"][:, 0:FF], fqb["w"][:, 0:FF], QC1, QC0,
                        ALU.mult, ALU.add))

                @block.gpsimd
                def _(gpsimd):
                    s["VD"].wait(gpsimd)
                    s["GD"].inc(gpsimd.tensor_tensor(
                        fqb["tq"][:, 0:FF], fqb["v"][:, 0:FF], fqb["M"][:, 0:FF],
                        ALU.add))

            def q_p3(tt):
                hb = tt % 2
                gm = s["GD"].mark()

                @block.vector
                def _(vector, tt=tt, hb=hb):
                    s["GD"].wait_n(vector, gm)
                    if tt >= 2:
                        s["DS"].wait_n(vector, dsm1[tt - 1])
                    s["VD"].inc(vector.tensor_tensor(
                        fqb["tq"][:, 0:FF], fqb["tq"][:, 0:FF], fqb["M"][:, 0:FF],
                        ALU.subtract))
                    s["VD"].wait(vector)
                    sbc = fqb["sc"][:, 0:NB].unsqueeze(2).broadcast_to([128, NB, 16])
                    s["VD"].inc(vector.tensor_tensor(
                        hq_b[:, hb * FF:(hb + 1) * FF].rearrange(
                            "p (nb k) -> p nb k", k=16),
                        fqb["tq"][:, 0:FF].rearrange("p (nb k) -> p nb k", k=16),
                        sbc, ALU.mult))
                vdm1[tt] = s["VD"].mark()

            def q_stash(tt):
                hb = tt % 2

                @block.sync
                def _(sync, tt=tt, hb=hb):
                    s["VD"].wait_n(sync, vdm1[tt])
                    s["DS"].inc(sync.dma_start(out=scr_h[tt * 128:(tt + 1) * 128, :],
                                               in_=hq_b[:, hb * FF:(hb + 1) * FF]), 16)
                dsm1[tt] = s["DS"].mark()

            for t in range(NT):
                hb = t % 2
                for sl in range(NSL):
                    S1 = t * NSL + sl
                    bk = S1 % 8
                    rb = S1 % 4
                    c0 = bk * 512

                    @block.tensor
                    def _(tensor, t=t, sl=sl, S1=S1, c0=c0):
                        if S1 == 0:
                            s["DS"].wait(tensor)
                        if S1 >= 8:
                            s["EV"].wait_n(tensor, sim1[S1 - 7])
                            s["VD"].wait_n(tensor, hmm1[S1 - 8])
                        n0 = sl * 256
                        for k in range(KD):
                            lhs = xnqT[:, k * T + t * 128:k * T + (t + 1) * 128]
                            tensor.matmul(pf[:, c0:c0 + 256], lhs,
                                          wgT[:, k * FF + n0:k * FF + n0 + 256],
                                          start=(k == 0), stop=False,
                                          skip_group_check=True)
                            mi = tensor.matmul(pf[:, c0 + 256:c0 + 512], lhs,
                                               wuT[:, k * FF + n0:k * FF + n0 + 256],
                                               start=False, stop=(k == KD - 1),
                                               skip_group_check=True)
                        s["PD"].inc(mi)

                    @block.scalar
                    def _(scalar, S1=S1, rb=rb, c0=c0):
                        s["PD"].wait(scalar)
                        if S1 >= 4:
                            s["VD"].wait_n(scalar, hmm1[S1 - 4])
                        s["EV"].inc(scalar.activation(sg[:, rb * 256:(rb + 1) * 256],
                                                      pf[:, c0:c0 + 256], AF.Silu))
                    sim1.append(s["EV"].mark())

                    @block.vector
                    def _(vector, t=t, sl=sl, S1=S1, rb=rb, c0=c0, hb=hb):
                        s["PD"].wait(vector)
                        s["EV"].wait_n(vector, sim1[S1])
                        s["VD"].inc(vector.tensor_tensor(
                            h_sb[:, hb * FF + sl * 256:hb * FF + (sl + 1) * 256],
                            pf[:, c0 + 256:c0 + 512],
                            sg[:, rb * 256:(rb + 1) * 256], ALU.mult))
                    hmm1.append(s["VD"].mark())

                    if t >= 1:
                        if sl == 1:
                            q_p1(t - 1)
                        elif sl == 4:
                            q_p2(t - 1)
                        elif sl == 8:
                            q_p3(t - 1)
                        elif sl == 9:
                            q_stash(t - 1)

            q_p1(NT - 1)
            q_p2(NT - 1)
            q_p3(NT - 1)
            q_stash(NT - 1)

            @block.sync
            def _(sync):
                s["DS"].wait(sync)

            _barrier(block, s)

    _phase1()


    g_sb = stack.enter_context(nc.sbuf_tensor("g_sb", [128, NT * D], BF16))
    Sg = stack.enter_context(nc.sbuf_tensor("Sg", [128, 2 * H * DF], BF16))
    S_pre = stack.enter_context(nc.sbuf_tensor("S_pre", [128, H * DF], BF16))
    woT = stack.enter_context(nc.sbuf_tensor("woT", [128, KD * D], BF16))

    def _phase4_prework():
        for r in range(8):
            rb = r % 2

            @block.sync
            def _(sync, r=r, rb=rb):
                if r == 0:
                    sync.wait_ge(s["CC"].h, 4)
                if r >= 2:
                    s["VD"].wait(sync)
                s["DSS"].inc(sync.dma_start(
                    out=Sg[:, rb * H * DF:(rb + 1) * H * DF],
                    in_=cc_S_o[r * 128:(r + 1) * 128, :]), 16)

            @block.vector
            def _(vector, r=r, rb=rb):
                s["DSS"].wait(vector)
                s["VD"].wait(vector)
                if r == 0:
                    li = vector.tensor_scalar(S_pre[:, :], Sg[:, 0:H * DF],
                                              smask_sb[:, 0:1], None, ALU.mult)
                else:
                    li = vector.scalar_tensor_tensor(
                        S_pre[:, :], Sg[:, rb * H * DF:(rb + 1) * H * DF],
                        smask_sb[:, r:r + 1], S_pre[:, :], ALU.mult, ALU.add)
                s["VD"].inc(li)

        @block.gpsimd
        def _(gpsimd):
            for k in range(KD):
                s["DG"].inc(gpsimd.dma_start(out=woT[:, k * D:(k + 1) * D],
                                             in_=wo_T[k * 128:(k + 1) * 128, :]), 16)

        @block.vector
        def _(vector):
            s["DG"].wait(vector)
            li = None
            for k in range(KD):
                li = vector.tensor_scalar(woT[:, k * D:(k + 1) * D],
                                          woT[:, k * D:(k + 1) * D],
                                          bc(RMS), None, ALU.mult)
            s["VD"].inc(li)

    def _phase3():
        with (
            nc.sbuf_tensor("wdT", [128, NFK * D], BF16) as wdT,
            nc.sbuf_tensor("hqT", [128, 2 * NFK * 512], BF16) as hqT,
            nc.psum_tensor("pD", [128, 2048], F32) as pDm,
        ):
            pD = (pDm[:, 0:1024], pDm[:, 1024:2048])

            @block.sync
            def _(sync):
                sync.wait_ge(s["CC"].h, 3)
                for n in range(NFK):
                    s["DS"].inc(sync.dma_start(out=wdT[:, n * D:(n + 1) * D],
                                               in_=cc_wd_o[:, n * 128:(n + 1) * 128],
                                               transpose=True), 16)

            p3m = []
            pdm3 = []
            for tb in range(4):
                hb = tb % 2

                @block.sync
                def _(sync, tb=tb, hb=hb):
                    if tb >= 2:
                        s["PD"].wait_n(sync, pdm3[tb - 2])
                    for n in range(NFK):
                        s["DA"].inc(sync.dma_start(
                            out=hqT[:, (hb * NFK + n) * 512:(hb * NFK + n + 1) * 512],
                            in_=scr_h[tb * 512:(tb + 1) * 512, n * 128:(n + 1) * 128],
                            transpose=True), 16)

                for ti in range(4):
                    t = tb * 4 + ti
                    pb = t % 2

                    @block.tensor
                    def _(tensor, hb=hb, ti=ti, t=t, pb=pb):
                        if t == 0:
                            s["DS"].wait(tensor)
                        if ti == 0:
                            s["DA"].wait(tensor)
                        if t >= 1:
                            s["EV"].wait_n(tensor, p3m[t - 1])
                        for n in range(NFK):
                            lhs = hqT[:, (hb * NFK + n) * 512 + ti * 128:
                                      (hb * NFK + n) * 512 + (ti + 1) * 128]
                            tensor.matmul(pD[pb][:, 0:512], lhs, wdT[:, n * D:n * D + 512],
                                          start=(n == 0), stop=(n == NFK - 1),
                                          skip_group_check=True)
                            mi = tensor.matmul(pD[pb][:, 512:1024], lhs,
                                               wdT[:, n * D + 512:(n + 1) * D],
                                               start=(n == 0), stop=(n == NFK - 1),
                                               skip_group_check=True)
                        s["PD"].inc(mi)

                    @block.scalar
                    def _(scalar, t=t, pb=pb):
                        s["PD"].wait(scalar)
                        s["EV"].inc(scalar.activation(g_sb[:, t * D:(t + 1) * D],
                                                      pD[pb][:, :], AF.Silu))
                    p3m.append(s["EV"].mark())
                pdm3.append(s["PD"].mark())

            _barrier(block, s)

    _phase4_prework()
    _phase3()

    s_sb = stack.enter_context(nc.sbuf_tensor("s_sb", [128, NT * D], BF16))

    def _phase4():
        with (
            nc.sbuf_tensor("qf2", [128, 2 * T], BF16) as qf2,
            nc.sbuf_tensor("o2", [128, T], BF16) as o2,
            nc.sbuf_tensor("lsum", [128, T], F32) as lsum,
            nc.sbuf_tensor("rstd", [128, T], BF16) as rstd,
            nc.sbuf_tensor("o_nrm", [128, H * T], BF16) as o_nrm,
            nc.sbuf_tensor("o_full4", [128, H * T], BF16) as o_full,
            nc.sbuf_tensor("x2", [128, 2 * D], F32) as x2,
            nc.sbuf_tensor("sgp", [128, 2 * D], F32) as sgp,
            nc.psum_tensor("pX", [128, 4096], F32) as pX,
        ):
            qfp2, qfm2 = qf2[:, 0:T], qf2[:, T:2 * T]
            pC = (pX[:, 0:512], pX[:, 512:1024])
            pR = (pX[:, 1024:1536], pX[:, 1536:2048])
            pW = (pX[:, 2048:3072], pX[:, 3072:4096])

            @block.sync
            def _(sync):
                s["DS"].inc(sync.dma_start(out=o_full[:, :], in_=scr_o[:, :]), 16)

            cpm = []
            for h in range(H):
                @block.sync
                def _(sync, h=h):
                    if h >= 1:
                        s["PD"].wait(sync)
                    s["DS"].inc(sync.dma_start(out=qfp2,
                                               in_=scr_qf[h * DF:h * DF + 128, :]), 16)
                    s["DS"].inc(sync.dma_start(out=qfm2,
                                               in_=scr_qf[h * DF + 128:(h + 1) * DF, :]), 16)

                for tc in range(4):
                    pb = tc % 2

                    @block.tensor
                    def _(tensor, h=h, tc=tc, pb=pb):
                        if tc == 0:
                            s["DS"].wait(tensor)
                            if h == 0:
                                s["VD"].wait(tensor)
                        if h * 4 + tc >= 2:
                            s["EV"].wait_n(tensor, cpm[h * 4 + tc - 2])
                        tensor.matmul(pC[pb], S_pre[:, h * DF:h * DF + 128],
                                      qfp2[:, tc * 512:(tc + 1) * 512],
                                      start=True, stop=False, skip_group_check=True)
                        mi = tensor.matmul(pC[pb], S_pre[:, h * DF + 128:(h + 1) * DF],
                                           qfm2[:, tc * 512:(tc + 1) * 512],
                                           start=False, stop=True, skip_group_check=True)
                        s["PD"].inc(mi)

                    @block.vector
                    def _(vector, h=h, tc=tc, pb=pb):
                        s["PD"].wait(vector)
                        if h == 0 and tc == 0:
                            s["DS"].wait(vector)
                        sl = slice(h * T + tc * 512, h * T + (tc + 1) * 512)
                        s["EV"].inc(vector.tensor_tensor(o_full[:, sl], o_full[:, sl],
                                                         pC[pb], ALU.add))
                    cpm.append(s["EV"].mark())

                @block.vector
                def _(vector, h=h):
                    s["EV"].wait(vector)
                    s["PD"].wait(vector)
                    s["EV"].inc(vector.scalar_tensor_tensor(
                        o2[:, :], o_full[:, h * T:(h + 1) * T], 2.0 ** -16,
                        o_full[:, h * T:(h + 1) * T], ALU.mult, ALU.mult))

                for tc in range(4):
                    pb = tc % 2

                    @block.tensor
                    def _(tensor, tc=tc, pb=pb):
                        s["EV"].wait(tensor)
                        mi = tensor.matmul(pR[pb], ones_b[:, :],
                                           o2[:, tc * 512:(tc + 1) * 512],
                                           start=True, stop=True, skip_group_check=True)
                        s["PD"].inc(mi)

                    @block.scalar
                    def _(scalar, tc=tc, pb=pb):
                        s["PD"].wait(scalar)
                        s["EV"].inc(scalar.activation(lsum[:, tc * 512:(tc + 1) * 512],
                                                      pR[pb], AF.Ln, bias=eps2_sb[:, 0:1],
                                                      scale=2.0 ** -48 / DV))

                @block.scalar
                def _(scalar, h=h):
                    li = None
                    for tc in range(4):
                        li = scalar.activation(rstd[:, tc * 512:(tc + 1) * 512],
                                               lsum[:, tc * 512:(tc + 1) * 512],
                                               AF.Exp, scale=-0.5, bias=nl2_sb[:, 0:1])
                    s["EV"].inc(li)

                @block.vector
                def _(vector, h=h):
                    s["EV"].wait(vector)
                    s["EV"].inc(vector.tensor_tensor(o_nrm[:, h * T:(h + 1) * T],
                                                     o_full[:, h * T:(h + 1) * T],
                                                     rstd[:, :], ALU.mult))

            wom = []
            vm5 = []
            ssm5 = []

            def emit_y(t):
                pb5 = t % 2

                @block.sync
                def _(sync, t=t, pb5=pb5):
                    if t >= 2:
                        s["VD"].wait_n(sync, vm5[t - 2])
                    s["DS"].inc(sync.dma_start(out=x2[:, pb5 * D:(pb5 + 1) * D],
                                               in_=x_in[t * 128:(t + 1) * 128, :]), 16)

                @block.vector
                def _(vector, t=t, pb5=pb5):
                    s["DS"].wait(vector)
                    s["EV"].wait_n(vector, wom[t])
                    if t >= 2:
                        s["DSS"].wait_n(vector, ssm5[t - 2])
                    s["VD"].inc(vector.tensor_tensor(sgp[:, pb5 * D:(pb5 + 1) * D],
                                                     s_sb[:, t * D:(t + 1) * D],
                                                     g_sb[:, t * D:(t + 1) * D],
                                                     ALU.mult))
                    s["VD"].wait(vector)
                    s["VD"].inc(vector.tensor_tensor(sgp[:, pb5 * D:(pb5 + 1) * D],
                                                     x2[:, pb5 * D:(pb5 + 1) * D],
                                                     sgp[:, pb5 * D:(pb5 + 1) * D],
                                                     ALU.add))
                vm5.append(s["VD"].mark())

                @block.sync
                def _(sync, t=t, pb5=pb5):
                    s["VD"].wait_n(sync, vm5[t])
                    s["DSS"].inc(sync.dma_start(out=y_out[t * 128:(t + 1) * 128, :],
                                                in_=sgp[:, pb5 * D:(pb5 + 1) * D]), 16)
                ssm5.append(s["DSS"].mark())

            for t in range(NT):
                pb = t % 2

                @block.tensor
                def _(tensor, t=t, pb=pb):
                    if t == 0:
                        s["VD"].wait(tensor)
                    if t < 1:
                        s["EV"].wait(tensor)
                    elif t >= 2:
                        s["EV"].wait_n(tensor, wom[t - 2])
                    for k in range(KD):
                        lhs = o_nrm[:, k * T + t * 128:k * T + (t + 1) * 128]
                        tensor.matmul(pW[pb][:, 0:512], lhs, woT[:, k * D:k * D + 512],
                                      start=(k == 0), stop=(k == KD - 1),
                                      skip_group_check=True)
                        mi = tensor.matmul(pW[pb][:, 512:1024], lhs,
                                           woT[:, k * D + 512:(k + 1) * D],
                                           start=(k == 0), stop=(k == KD - 1),
                                           skip_group_check=True)
                    s["PD"].inc(mi)

                @block.vector
                def _(vector, t=t, pb=pb):
                    s["PD"].wait(vector)
                    s["EV"].inc(vector.tensor_copy(s_sb[:, t * D:(t + 1) * D],
                                                   pW[pb][:, :]))
                wom.append(s["EV"].mark())
                emit_y(t)

            @block.sync
            def _(sync):
                s["DSS"].wait(sync)

    _phase4()


    if DEBUG_TAPS:
        taps = {
            "dbg_xn": (scr_xn, [T, D]), "dbg_xnq": (scr_xnq, [T, D]),
            "dbg_h": (scr_h, [T, FF]), "dbg_qf": (scr_qf, [H * DF, T]),
            "dbg_kf": (scr_kf, [H * DF, T]),
            "dbg_wg": (cc_wg_o, [FF, D]), "dbg_wu": (cc_wu_o, [FF, D]),
            "dbg_wd": (cc_wd_o, [D, FF]),
            "dbg_Si": (cc_S_i, [128, H * DF]), "dbg_So": (cc_S_o, [8 * 128, H * DF]),
        }
        taps["dbg_ofull"] = (scr_o, [128, H * T])
        dbg_outs = {k: nc.dram_tensor(k, shp, BF16, kind="ExternalOutput").ap()
                    for k, (_, shp) in taps.items()}
        sb_taps = {"dbg_g": (g_sb, [128, NT * D]), "dbg_s": (s_sb, [128, NT * D])}
        for k, (_, shp) in sb_taps.items():
            dbg_outs[k] = nc.dram_tensor(k, shp, BF16, kind="ExternalOutput").ap()

        @block.sync
        def _(sync):
            for k, (src_t, shp) in taps.items():
                s["DS"].inc(sync.dma_start(out=dbg_outs[k][:, :], in_=src_t[:, :]), 16)
            for k, (src_t, shp) in sb_taps.items():
                s["DS"].inc(sync.dma_start(out=dbg_outs[k][:, :], in_=src_t[:, :]), 16)
            s["DS"].wait(sync)

    stack.close()
    return nc


def kernel(L0_attn_x, L0_attn_Wq, L0_attn_Wk, L0_attn_Wv, L0_attn_Wo,
           L0_attn_fmq_w, L0_attn_fmq_b, L0_attn_fmk_w, L0_attn_fmk_b,
           L0_attn_rms_w, L0_ffn_w_gate, L0_ffn_w_up, L0_ffn_w_down):
    f32 = np.float32
    x = np.ascontiguousarray(np.asarray(L0_attn_x, dtype=f32))
    Wq = np.asarray(L0_attn_Wq, dtype=f32)
    Wk = np.asarray(L0_attn_Wk, dtype=f32)
    Wv = np.asarray(L0_attn_Wv, dtype=f32)
    Wo = np.asarray(L0_attn_Wo, dtype=f32)
    fmq_w = np.asarray(L0_attn_fmq_w, dtype=f32)
    fmk_w = np.asarray(L0_attn_fmk_w, dtype=f32)
    fmq_b = np.asarray(L0_attn_fmq_b, dtype=f32)
    fmk_b = np.asarray(L0_attn_fmk_b, dtype=f32)
    rms_w = np.asarray(L0_attn_rms_w, dtype=f32)
    w_gate = np.asarray(L0_ffn_w_gate, dtype=f32)
    w_up = np.asarray(L0_ffn_w_up, dtype=f32)
    w_down = np.asarray(L0_ffn_w_down, dtype=f32)

    wq_t = np.ascontiguousarray(Wq.T.reshape(D, H, 128).transpose(1, 0, 2))
    wk_t = np.ascontiguousarray(Wk.T.reshape(D, H, 128).transpose(1, 0, 2))
    wv_Tc = np.ascontiguousarray(Wv.T)
    wo_Tc = np.ascontiguousarray(Wo.T)
    fmq_Tc = np.ascontiguousarray(fmq_w.T)
    fmk_Tc = np.ascontiguousarray(fmk_w.T)
    fmq_bc = np.ascontiguousarray(fmq_b.reshape(128, 1))
    fmk_bc = np.ascontiguousarray(fmk_b.reshape(128, 1))
    rms_cc = np.ascontiguousarray(rms_w.reshape(128, 1))

    nc = build_nc()

    in_maps = []
    for c in range(8):
        b, q = c // 4, c % 4
        msk = np.zeros((128, 8), dtype=np.float32)
        msk[:, 4 * b:4 * b + q] = 1.0
        in_maps.append({
            "x": np.ascontiguousarray(x[b, q * T:(q + 1) * T, :]),
            "wq_t": wq_t, "wk_t": wk_t, "wv_T": wv_Tc, "wo_T": wo_Tc,
            "fmq_T": fmq_Tc, "fmk_T": fmk_Tc, "fmq_b": fmq_bc, "fmk_b": fmk_bc,
            "rms_c": rms_cc,
            "wg_sl": np.ascontiguousarray(w_gate[c * WSL:(c + 1) * WSL, :]),
            "wu_sl": np.ascontiguousarray(w_up[c * WSL:(c + 1) * WSL, :]),
            "wd_sl": np.ascontiguousarray(w_down[c * 128:(c + 1) * 128, :]),
            "smask": msk,
        })

    res = run_bass_kernel_spmd(nc, in_maps, core_ids=list(range(8)))
    globals()["_last_results"] = res.results

    out = np.empty((2, 8192, D), dtype=f32)
    for c in range(8):
        b, q = c // 4, c % 4
        out[b, q * T:(q + 1) * T, :] = np.asarray(res.results[c]["y"])
    return out


# version 20
# speedup vs baseline: 1.0042x; 1.0042x over previous
import numpy as np
import ml_dtypes
import concourse.bass as bass
import concourse.mybir as mybir
from concourse.bass_utils import run_bass_kernel_spmd

F32 = mybir.dt.float32
BF16 = mybir.dt.bfloat16
AF = mybir.ActivationFunctionType
ALU = mybir.AluOpType
AX = mybir.AxisListType

T = 2048
D = 1024
NT = 16
KD = 8
H = 8
DV = 128
DF = 256
NCH = 16
FF = 2816
NFK = 22
QW = 704
WSL = FF // 8
LN16 = float(np.log(16.0))
QC0 = float(np.float32((4.0 / 3.0) * 2 ** 22))
QC1 = float(np.float32(2 ** 22 / 6.0))

DEBUG_TAPS = False


class SemC:
    def __init__(self, stack, nc, name):
        self.h = stack.enter_context(nc.semaphore(name))
        self.n = 0

    def inc(self, instr, k=1):
        instr.then_inc(self.h, k)
        self.n += k
        return instr

    def wait(self, eng):
        eng.wait_ge(self.h, self.n)

    def mark(self):
        return self.n

    def wait_n(self, eng, n):
        eng.wait_ge(self.h, n)


def _quant_tile(block, s, bufs, src_ap, n, dq_ap, pre=("VD",)):
    nb = n // 16
    amax, sc, rec, v, w, M, tq = (bufs[k] for k in
                                  ("amax", "sc", "rec", "v", "w", "M", "tq"))
    p = src_ap.shape[0]

    @block.vector
    def _(vector):
        for k in pre:
            s[k].wait(vector)
        src3 = src_ap.rearrange("p (nb k) -> p nb k", k=16)
        s["VD"].inc(vector.tensor_reduce(amax[0:p, 0:nb].unsqueeze(2), src3, AX.X,
                                         ALU.max, apply_absolute_value=True))
        s["VD"].wait(vector)
        s["VD"].inc(vector.tensor_scalar(sc[0:p, 0:nb], amax[0:p, 0:nb],
                                         1.0 / 6.0, 1e-12, ALU.mult, ALU.max))
        s["VD"].wait(vector)
        s["VD"].inc(vector.reciprocal(rec[0:p, 0:nb], sc[0:p, 0:nb]))
        s["VD"].wait(vector)
        src3 = src_ap.rearrange("p (nb k) -> p nb k", k=16)
        rbc = rec[0:p, 0:nb].unsqueeze(2).broadcast_to([p, nb, 16])
        s["VD"].inc(vector.tensor_tensor(
            v[0:p, 0:n].rearrange("p (nb k) -> p nb k", k=16), src3, rbc, ALU.mult))

    @block.scalar
    def _(scalar):
        s["VD"].wait(scalar)
        s["AD"].inc(scalar.activation(w[0:p, 0:n], v[0:p, 0:n], AF.Square))

    @block.vector
    def _(vector):
        s["AD"].wait(vector)
        s["VD"].inc(vector.tensor_scalar(M[0:p, 0:n], w[0:p, 0:n], QC1, QC0,
                                         ALU.mult, ALU.add))
        s["VD"].wait(vector)
        s["VD"].inc(vector.tensor_tensor(tq[0:p, 0:n], v[0:p, 0:n], M[0:p, 0:n],
                                         ALU.add))
        s["VD"].wait(vector)
        s["VD"].inc(vector.tensor_tensor(tq[0:p, 0:n], tq[0:p, 0:n], M[0:p, 0:n],
                                         ALU.subtract))
        s["VD"].wait(vector)
        sbc = sc[0:p, 0:nb].unsqueeze(2).broadcast_to([p, nb, 16])
        s["VD"].inc(vector.tensor_tensor(
            dq_ap.rearrange("p (nb k) -> p nb k", k=16),
            tq[0:p, 0:n].rearrange("p (nb k) -> p nb k", k=16), sbc, ALU.mult))




def _quant2(block, s, bufs, src_ap, n, dq_ap, pre=("VD",)):
    nb = n // 16
    amax, sc, rec, v, w, M, tq = (bufs[k] for k in
                                  ("amax", "sc", "rec", "v", "w", "M", "tq"))
    p = src_ap.shape[0]

    @block.vector
    def _(vector):
        for k in pre:
            s[k].wait(vector)
        src3 = src_ap.rearrange("p (nb k) -> p nb k", k=16)
        s["VD"].inc(vector.tensor_reduce(amax[0:p, 0:nb].unsqueeze(2), src3, AX.X,
                                         ALU.max, apply_absolute_value=True))
        s["VD"].wait(vector)
        s["VD"].inc(vector.tensor_scalar(sc[0:p, 0:nb], amax[0:p, 0:nb],
                                         1.0 / 6.0, 1e-12, ALU.mult, ALU.max))
        s["VD"].wait(vector)
        s["VD"].inc(vector.reciprocal(rec[0:p, 0:nb], sc[0:p, 0:nb]))
        s["VD"].wait(vector)
        src3 = src_ap.rearrange("p (nb k) -> p nb k", k=16)
        rbc = rec[0:p, 0:nb].unsqueeze(2).broadcast_to([p, nb, 16])
        s["VD"].inc(vector.tensor_tensor(
            v[0:p, 0:n].rearrange("p (nb k) -> p nb k", k=16), src3, rbc, ALU.mult))

    @block.gpsimd
    def _(gpsimd):
        s["VD"].wait(gpsimd)
        s["GD"].inc(gpsimd.tensor_tensor(w[0:p, 0:n], v[0:p, 0:n], v[0:p, 0:n],
                                         ALU.mult))

    @block.vector
    def _(vector):
        s["GD"].wait(vector)
        s["VD"].inc(vector.tensor_scalar(M[0:p, 0:n], w[0:p, 0:n], QC1, QC0,
                                         ALU.mult, ALU.add))

    @block.gpsimd
    def _(gpsimd):
        s["VD"].wait(gpsimd)
        s["GD"].inc(gpsimd.tensor_tensor(tq[0:p, 0:n], v[0:p, 0:n], M[0:p, 0:n],
                                         ALU.add))

    @block.vector
    def _(vector):
        s["GD"].wait(vector)
        s["VD"].inc(vector.tensor_tensor(tq[0:p, 0:n], tq[0:p, 0:n], M[0:p, 0:n],
                                         ALU.subtract))
        s["VD"].wait(vector)
        sbc = sc[0:p, 0:nb].unsqueeze(2).broadcast_to([p, nb, 16])
        s["VD"].inc(vector.tensor_tensor(
            dq_ap.rearrange("p (nb k) -> p nb k", k=16),
            tq[0:p, 0:n].rearrange("p (nb k) -> p nb k", k=16), sbc, ALU.mult))


def _barrier(block, s):
    tgt = s["PH"].n + 5
    s["PH"].n = tgt
    for engname in ("sync", "tensor", "vector", "scalar", "gpsimd"):
        def mk():
            def _(eng):
                for k in ("VD", "AD", "PD", "GD", "EV"):
                    s[k].wait(eng)
                eng.sem_inc(s["PH"].h, 1)
                eng.wait_ge(s["PH"].h, tgt)
            return _
        getattr(block, engname)(mk())


def build_nc():
    import contextlib
    stack = contextlib.ExitStack()
    nc = bass.Bass("TRN2", target_bir_lowering=False)

    x_in = nc.dram_tensor("x", [T, D], F32, kind="ExternalInput").ap()
    y_out = nc.dram_tensor("y", [T, D], F32, kind="ExternalOutput").ap()
    wq_t = nc.dram_tensor("wq_t", [H, D, 128], F32, kind="ExternalInput").ap()
    wk_t = nc.dram_tensor("wk_t", [H, D, 128], F32, kind="ExternalInput").ap()
    wv_T = nc.dram_tensor("wv_T", [D, D], F32, kind="ExternalInput").ap()
    wo_T = nc.dram_tensor("wo_T", [D, D], F32, kind="ExternalInput").ap()
    fmq_T = nc.dram_tensor("fmq_T", [128, 128], F32, kind="ExternalInput").ap()
    fmk_T = nc.dram_tensor("fmk_T", [128, 128], F32, kind="ExternalInput").ap()
    fmq_b = nc.dram_tensor("fmq_b", [128, 1], F32, kind="ExternalInput").ap()
    fmk_b = nc.dram_tensor("fmk_b", [128, 1], F32, kind="ExternalInput").ap()
    rms_c = nc.dram_tensor("rms_c", [128, 1], F32, kind="ExternalInput").ap()
    wg_sl = nc.dram_tensor("wg_sl", [WSL, D], F32, kind="ExternalInput").ap()
    wu_sl = nc.dram_tensor("wu_sl", [WSL, D], F32, kind="ExternalInput").ap()
    wd_sl = nc.dram_tensor("wd_sl", [128, FF], F32, kind="ExternalInput").ap()
    smask = nc.dram_tensor("smask", [128, 8], F32, kind="ExternalInput").ap()

    scr_xn = nc.dram_tensor("scr_xn", [T, D], BF16).ap()
    scr_xnq = nc.dram_tensor("scr_xnq", [T, D], BF16).ap()
    scr_h = nc.dram_tensor("scr_h", [T, FF], BF16).ap()
    scr_kf = nc.dram_tensor("scr_kf", [H * DF, T], BF16).ap()
    scr_qf = nc.dram_tensor("scr_qf", [H * DF, T], BF16).ap()
    cc_wg_i = nc.dram_tensor("cc_wg_i", [WSL, D], BF16).ap()
    cc_wg_o = nc.dram_tensor("cc_wg_o", [FF, D], BF16).ap()
    cc_wu_i = nc.dram_tensor("cc_wu_i", [WSL, D], BF16).ap()
    cc_wu_o = nc.dram_tensor("cc_wu_o", [FF, D], BF16).ap()
    cc_wd_i = nc.dram_tensor("cc_wd_i", [128, FF], BF16).ap()
    cc_wd_o = nc.dram_tensor("cc_wd_o", [D, FF], BF16).ap()
    cc_S_i = nc.dram_tensor("cc_S_i", [128, H * DF], BF16).ap()
    cc_S_o = nc.dram_tensor("cc_S_o", [8 * 128, H * DF], BF16).ap()

    s = {k: SemC(stack, nc, k) for k in
         ("DS", "DA", "DG", "DSK", "DSQ", "DSS", "CC", "PH", "PD", "VD", "AD", "GD", "EV")}

    block = stack.enter_context(nc.Block())

    def alloc(name, shape, dt):
        return stack.enter_context(nc.sbuf_tensor(name, shape, dt))

    ones_b = alloc("ones_b", [128, 128], BF16)
    mask4 = alloc("mask4", [128, 512], BF16)
    fmq_sb = alloc("fmq_sb", [128, 128], BF16)
    fmk_sb = alloc("fmk_sb", [128, 128], BF16)
    bcol = alloc("bcol", [128, 8], F32)
    eps_sb = alloc("eps_sb", [128, 1], F32)
    eps2_sb = alloc("eps2_sb", [128, 1], F32)
    nl2_sb = alloc("nl2_sb", [128, 1], F32)
    smask_sb = alloc("smask_sb", [128, 8], F32)
    imask = alloc("imask", [128, 512], F32)
    ident_b = alloc("ident_b", [128, 128], BF16)
    BQP, BQM, BKP, BKM, RMS, FQB, FKB = range(7)

    def bc(i):
        return bcol[:, i:i + 1]

    @block.scalar
    def _(scalar):
        s["DA"].inc(scalar.dma_start(out=bc(FQB), in_=fmq_b[:, :]), 16)
        s["DA"].inc(scalar.dma_start(out=bc(FKB), in_=fmk_b[:, :]), 16)
        s["DA"].inc(scalar.dma_start(out=bc(RMS), in_=rms_c[:, :]), 16)
        s["DA"].inc(scalar.dma_start(out=smask_sb[:], in_=smask[:, :]), 16)

    @block.gpsimd
    def _(gpsimd):
        gpsimd.memset(ones_b[:], 1.0)
        gpsimd.memset(eps_sb[:, :], 1e-5)
        gpsimd.memset(eps2_sb[:, :], 1e-5 * 2.0 ** -64)
        gpsimd.memset(nl2_sb[:, :], -32.0 * 0.6931471805599453)
        s["GD"].inc(gpsimd.iota(imask[:], pattern=[[1, 512]],
                                channel_multiplier=-1,
                                allow_small_or_imprecise_dtypes=True))
        s["DG"].inc(gpsimd.dma_start(out=fmq_sb[:], in_=fmq_T[:, :]), 16)
        s["DG"].inc(gpsimd.dma_start(out=fmk_sb[:], in_=fmk_T[:, :]), 16)

    @block.vector
    def _(vector):
        s["GD"].wait(vector)
        s["DA"].wait(vector)
        vector.tensor_scalar(mask4[:], imask[:], 0.0, None, ALU.is_ge)
        vector.tensor_scalar(ident_b[:], imask[:, 0:128], 0.0, None, ALU.is_equal)
        vector.tensor_scalar(bc(BQP), bc(FQB), -LN16, None, ALU.add)
        vector.tensor_scalar(bc(BQM), bc(FQB), -1.0, -LN16, ALU.mult, ALU.add)
        vector.tensor_copy(bc(BKP), bc(FKB))
        s["VD"].inc(vector.tensor_scalar(bc(BKM), bc(FKB), -1.0, None, ALU.mult))

    def _phase0():
        CW = 4 * D
        with (
            nc.sbuf_tensor("x_sb", [128, 2 * CW], F32) as x_sb,
            nc.sbuf_tensor("wsl_sb", [128, 3 * D + FF], F32) as wsl_sb,
            nc.sbuf_tensor("qS", [128, 3 * 320], F32) as qS,
            nc.sbuf_tensor("qB", [128, 4 * 4880], F32) as qB,
            nc.sbuf_tensor("dq_b", [128, 2 * CW], BF16) as dq_b,
            nc.sbuf_tensor("ln_st", [128, 6 * NT], F32) as ln_st,
            nc.sbuf_tensor("xn_f", [128, 2 * CW], F32) as xn_f,
            nc.sbuf_tensor("xn_b", [128, 2 * CW], BF16) as xn_b,
        ):
            QBW = 4880
            qbufs = dict(amax=qS[:, 0:320], sc=qS[:, 320:640],
                         rec=qS[:, 640:960],
                         v=qB[:, 0:QBW], w=qB[:, QBW:2 * QBW],
                         M=qB[:, 2 * QBW:3 * QBW], tq=qB[:, 3 * QBW:4 * QBW])

            ST_SUM, ST_SQ, ST_MEAN, ST_VAR, ST_RSTD, ST_NMR = (
                0, NT, 2 * NT, 3 * NT, 4 * NT, 5 * NT)

            for ck in range(NT // 4):
                cb = ck % 2
                sqs = qB[:, (1 + cb) * CW:(2 + cb) * CW]

                @block.sync
                def _(sync, ck=ck, cb=cb):
                    if ck >= 2:
                        s["VD"].wait(sync)
                    s["DS"].inc(sync.dma_start(
                        out=x_sb[:, cb * CW:(cb + 1) * CW].rearrange(
                            "p (tt d) -> p tt d", d=D),
                        in_=x_in[ck * 512:(ck + 1) * 512, :].rearrange(
                            "(tt p) d -> p tt d", p=128)), 16)

                @block.gpsimd
                def _(gpsimd, ck=ck, cb=cb, sqs=sqs):
                    s["DS"].wait(gpsimd)
                    if ck >= 2:
                        s["VD"].wait(gpsimd)
                    s["GD"].inc(gpsimd.tensor_tensor(
                        sqs, x_sb[:, cb * CW:(cb + 1) * CW],
                        x_sb[:, cb * CW:(cb + 1) * CW], ALU.mult))

                @block.vector
                def _(vector, ck=ck, cb=cb, sqs=sqs):
                    s["DS"].wait(vector)
                    for j in range(4):
                        t = ck * 4 + j
                        vector.tensor_reduce(
                            ln_st[:, ST_SUM + t:ST_SUM + t + 1].unsqueeze(2),
                            x_sb[:, cb * CW + j * D:cb * CW + (j + 1) * D].unsqueeze(1),
                            AX.X, ALU.add)
                    s["GD"].wait(vector)
                    li = None
                    for j in range(4):
                        t = ck * 4 + j
                        li = vector.tensor_reduce(
                            ln_st[:, ST_SQ + t:ST_SQ + t + 1].unsqueeze(2),
                            sqs[:, j * D:(j + 1) * D].unsqueeze(1), AX.X, ALU.add)
                    s["VD"].inc(li)

            @block.vector
            def _(vector):
                s["VD"].wait(vector)
                s["VD"].inc(vector.tensor_scalar(ln_st[:, ST_MEAN:ST_MEAN + NT],
                                                 ln_st[:, ST_SUM:ST_SUM + NT],
                                                 1.0 / D, None, ALU.mult))
                s["VD"].inc(vector.tensor_scalar(ln_st[:, ST_VAR:ST_VAR + NT],
                                                 ln_st[:, ST_SQ:ST_SQ + NT],
                                                 1.0 / D, None, ALU.mult))
                s["VD"].wait(vector)
                s["VD"].inc(vector.tensor_tensor(ln_st[:, ST_SQ:ST_SQ + NT],
                                                 ln_st[:, ST_MEAN:ST_MEAN + NT],
                                                 ln_st[:, ST_MEAN:ST_MEAN + NT], ALU.mult))
                s["VD"].wait(vector)
                s["VD"].inc(vector.tensor_tensor(ln_st[:, ST_VAR:ST_VAR + NT],
                                                 ln_st[:, ST_VAR:ST_VAR + NT],
                                                 ln_st[:, ST_SQ:ST_SQ + NT], ALU.subtract))

            @block.scalar
            def _(scalar):
                s["VD"].wait(scalar)
                scalar.activation(ln_st[:, ST_SUM:ST_SUM + NT], ln_st[:, ST_VAR:ST_VAR + NT],
                                  AF.Ln, bias=eps_sb[:, 0:1])
                s["AD"].inc(scalar.activation(ln_st[:, ST_RSTD:ST_RSTD + NT],
                                              ln_st[:, ST_SUM:ST_SUM + NT],
                                              AF.Exp, scale=-0.5))

            @block.vector
            def _(vector):
                s["AD"].wait(vector)
                s["VD"].inc(vector.scalar_tensor_tensor(
                    ln_st[:, ST_NMR:ST_NMR + NT], ln_st[:, ST_MEAN:ST_MEAN + NT], -1.0,
                    ln_st[:, ST_RSTD:ST_RSTD + NT], ALU.mult, ALU.mult))

            def emit_xnq(cki):
                cb = cki % 2
                _quant2(block, s, qbufs, xn_f[:, cb * CW:(cb + 1) * CW], CW,
                        dq_b[:, cb * CW:(cb + 1) * CW], pre=("VD", "AD", "DS", "DG"))

                @block.sync
                def _(sync, cki=cki, cb=cb):
                    s["VD"].wait(sync)
                    s["DS"].inc(sync.dma_start(
                        out=scr_xnq[cki * 512:(cki + 1) * 512, :].rearrange(
                            "(tt p) d -> p tt d", p=128),
                        in_=dq_b[:, cb * CW:(cb + 1) * CW].rearrange(
                            "p (tt d) -> p tt d", d=D)), 16)

            for cki in range(NT // 4):
                cb = cki % 2

                @block.sync
                def _(sync, cki=cki, cb=cb):
                    s["AD"].wait(sync)
                    s["DS"].inc(sync.dma_start(
                        out=x_sb[:, cb * CW:(cb + 1) * CW].rearrange(
                            "p (tt d) -> p tt d", d=D),
                        in_=x_in[cki * 512:(cki + 1) * 512, :].rearrange(
                            "(tt p) d -> p tt d", p=128)), 16)

                @block.scalar
                def _(scalar, cki=cki, cb=cb):
                    s["DS"].wait(scalar)
                    s["VD"].wait(scalar)
                    for j in range(4):
                        t = cki * 4 + j
                        scalar.activation(xn_f[:, cb * CW + j * D:cb * CW + (j + 1) * D],
                                          x_sb[:, cb * CW + j * D:cb * CW + (j + 1) * D],
                                          AF.Identity,
                                          bias=ln_st[:, ST_NMR + t:ST_NMR + t + 1],
                                          scale=ln_st[:, ST_RSTD + t:ST_RSTD + t + 1])
                    for j in range(4):
                        t = cki * 4 + j
                        li = scalar.activation(
                            xn_b[:, cb * CW + j * D:cb * CW + (j + 1) * D],
                            x_sb[:, cb * CW + j * D:cb * CW + (j + 1) * D],
                            AF.Identity, bias=ln_st[:, ST_NMR + t:ST_NMR + t + 1],
                            scale=ln_st[:, ST_RSTD + t:ST_RSTD + t + 1])
                    s["AD"].inc(li)

                @block.sync
                def _(sync, cki=cki, cb=cb):
                    s["AD"].wait(sync)
                    s["DS"].inc(sync.dma_start(
                        out=scr_xn[cki * 512:(cki + 1) * 512, :].rearrange(
                            "(tt p) d -> p tt d", p=128),
                        in_=xn_b[:, cb * CW:(cb + 1) * CW].rearrange(
                            "p (tt d) -> p tt d", d=D)), 16)

                if cki >= 1:
                    emit_xnq(cki - 1)

            emit_xnq(NT // 4 - 1)

            @block.scalar
            def _(scalar):
                s["DA"].inc(scalar.dma_start(out=wsl_sb[:, 0:D], in_=wg_sl[0:128, :]), 16)
                s["DA"].inc(scalar.dma_start(out=wsl_sb[:, D:2 * D], in_=wu_sl[0:128, :]), 16)
                s["DA"].inc(scalar.dma_start(out=wsl_sb[:, 2 * D:3 * D],
                                             in_=wd_sl[0:128, 0:D]), 16)
                s["DA"].inc(scalar.dma_start(out=wsl_sb[:, 3 * D:3 * D + FF - D],
                                             in_=wd_sl[0:128, D:FF]), 16)

            @block.vector
            def _(vector):
                s["DA"].wait(vector)

            _quant2(block, s, qbufs, wsl_sb[:, 0:2 * D + FF], 2 * D + FF,
                    dq_b[:, 0:2 * D + FF])

            @block.gpsimd
            def _(gpsimd):
                s["VD"].wait(gpsimd)
                s["DG"].inc(gpsimd.dma_start(out=cc_wg_i[0:128, :],
                                             in_=dq_b[:, 0:D]), 16)
                s["DG"].inc(gpsimd.dma_start(out=cc_wu_i[0:128, :],
                                             in_=dq_b[:, D:2 * D]), 16)
                s["DG"].inc(gpsimd.dma_start(out=cc_wd_i[0:128, :],
                                             in_=dq_b[:, 2 * D:2 * D + FF]), 16)

            @block.scalar
            def _(scalar):
                s["VD"].wait(scalar)
                s["DG"].wait(scalar)
                s["DA"].wait(scalar)
                s["DA"].inc(scalar.dma_start(out=wsl_sb[:, 0:D], in_=wg_sl[128:256, :]), 16)
                s["DA"].inc(scalar.dma_start(out=wsl_sb[:, D:2 * D],
                                             in_=wu_sl[128:256, :]), 16)
                s["DA"].inc(scalar.dma_start(out=wsl_sb[0:96, 2 * D:3 * D],
                                             in_=wg_sl[256:352, :]), 16)
                s["DA"].inc(scalar.dma_start(out=wsl_sb[0:96, 3 * D:4 * D],
                                             in_=wu_sl[256:352, :]), 16)

            @block.vector
            def _(vector):
                s["DA"].wait(vector)

            _quant2(block, s, qbufs, wsl_sb[:, 0:2 * D], 2 * D, dq_b[:, 0:2 * D])

            @block.gpsimd
            def _(gpsimd):
                s["VD"].wait(gpsimd)
                s["DG"].inc(gpsimd.dma_start(out=cc_wg_i[128:256, :],
                                             in_=dq_b[:, 0:D]), 16)
                s["DG"].inc(gpsimd.dma_start(out=cc_wu_i[128:256, :],
                                             in_=dq_b[:, D:2 * D]), 16)

            _quant2(block, s, qbufs, wsl_sb[0:96, 2 * D:4 * D], 2 * D,
                    dq_b[0:96, 2 * D:4 * D])

            @block.gpsimd
            def _(gpsimd):
                s["VD"].wait(gpsimd)
                s["DG"].inc(gpsimd.dma_start(out=cc_wg_i[256:352, :],
                                             in_=dq_b[0:96, 2 * D:3 * D]), 16)
                s["DG"].inc(gpsimd.dma_start(out=cc_wu_i[256:352, :],
                                             in_=dq_b[0:96, 3 * D:4 * D]), 16)

            @block.gpsimd
            def _(gpsimd):
                s["DG"].wait(gpsimd)
                for ci, co in ((cc_wg_i, cc_wg_o), (cc_wu_i, cc_wu_o), (cc_wd_i, cc_wd_o)):
                    s["CC"].inc(gpsimd.collective_compute(
                        "AllGather", ALU.bypass, replica_groups=[list(range(8))],
                        ins=[ci[:, :].opt()], outs=[co[:, :].opt()]))


            @block.sync
            def _(sync):
                s["DS"].wait(sync)

            _barrier(block, s)

    _phase0()

    scr_o = nc.dram_tensor("scr_o", [128, H * T], BF16, kind="Internal").ap()

    def _phase2():
        with (
            nc.sbuf_tensor("xnT", [128, KD * T], BF16) as xnT,
            nc.sbuf_tensor("v_sb", [128, NT * D], BF16) as v_sb,
            nc.sbuf_tensor("wv_sb", [128, KD * D], BF16) as wv_sb,
            nc.sbuf_tensor("wqk", [128, 4 * D], BF16) as wqk,
            nc.sbuf_tensor("qkraw", [128, 2 * T], BF16) as qkraw,
            nc.sbuf_tensor("feat", [128, 4 * T], BF16) as feat,
            nc.sbuf_tensor("kfT", [128, NCH * DF], BF16) as kfT,
            nc.sbuf_tensor("attnT", [128, NCH * 128], BF16) as attnT,
            nc.sbuf_tensor("Sx", [128, DF + H * DF], BF16) as Sx,
            nc.sbuf_tensor("o_full", [128, H * T], BF16) as o_full,
            nc.sbuf_tensor("echo2", [128, 1], F32) as echo2,
            nc.psum_tensor("pp", [128, 3072], F32) as pp,
            nc.psum_tensor("pt", [128, 2048], BF16) as pt,
        ):
            wqk_r = ((wqk[:, 0:D], wqk[:, D:2 * D]),
                     (wqk[:, 2 * D:3 * D], wqk[:, 3 * D:4 * D]))
            qraw, kraw = qkraw[:, 0:T], qkraw[:, T:2 * T]
            qfp, qfm = feat[:, 0:T], feat[:, T:2 * T]
            kfp, kfm = feat[:, 2 * T:3 * T], feat[:, 3 * T:4 * T]
            S_sb = Sx[:, 0:DF]
            Sstage = Sx[:, DF:DF + H * DF]
            pv = (pp[:, 0:1024], pp[:, 1024:2048], pp[:, 2048:3072])
            pS = pp[:, 0:512]
            pqy = (pp[:, 1024:1536], pp[:, 1536:2048])
            pA = (pp[:, 2048:2560], pp[:, 2560:3072])
            pO = pp[:, 512:1024]

            @block.sync
            def _(sync):
                for k in range(KD):
                    s["DS"].inc(sync.dma_start(out=xnT[:, k * T:(k + 1) * T],
                                               in_=scr_xn[:, k * 128:(k + 1) * 128],
                                               transpose=True), 16)

            @block.gpsimd
            def _(gpsimd):
                for k in range(KD):
                    s["DG"].inc(gpsimd.dma_start(out=wv_sb[:, k * D:(k + 1) * D],
                                                 in_=wv_T[k * 128:(k + 1) * 128, :]), 16)

            vm2 = []
            for t in range(NT):
                pvb = t % 2

                @block.tensor
                def _(tensor, t=t, pvb=pvb):
                    if t == 0:
                        s["DS"].wait(tensor)
                        s["DG"].wait(tensor)
                    if t >= 1:
                        s["EV"].wait_n(tensor, vm2[t - 1])
                    for k in range(KD):
                        lhs = xnT[:, k * T + t * 128:k * T + (t + 1) * 128]
                        tensor.matmul(pv[pvb][:, 0:512], lhs, wv_sb[:, k * D:k * D + 512],
                                      start=(k == 0), stop=(k == KD - 1),
                                      skip_group_check=True)
                        mi = tensor.matmul(pv[pvb][:, 512:1024], lhs,
                                           wv_sb[:, k * D + 512:(k + 1) * D],
                                           start=(k == 0), stop=(k == KD - 1),
                                           skip_group_check=True)
                    s["PD"].inc(mi)

                @block.vector
                def _(vector, t=t, pvb=pvb):
                    s["PD"].wait(vector)
                    s["EV"].inc(vector.tensor_copy(v_sb[:, t * D:(t + 1) * D],
                                                   pv[pvb][:, :]))
                vm2.append(s["EV"].mark())

            pqm = [None] * (16 * H)
            emk = [None] * H
            dskm = [None] * NCH
            aam_all = []
            om_all = []
            scm_all = []
            ktm_all = []
            qrm = []
            for h in range(H):
                wqm_sb, wkm_sb = wqk_r[h % 2]

                @block.gpsimd
                def _(gpsimd, h=h):
                    if h >= 2:
                        s["PD"].wait_n(gpsimd, qrm[h - 1])
                    s["DG"].inc(gpsimd.dma_start(
                        out=wqk_r[h % 2][0].rearrange("p (k c) -> p k c", c=128),
                        in_=wq_t[h].rearrange("(k p) c -> p k c", p=128)), 16)
                    s["DG"].inc(gpsimd.dma_start(
                        out=wqk_r[h % 2][1].rearrange("p (k c) -> p k c", c=128),
                        in_=wk_t[h].rearrange("(k p) c -> p k c", p=128)), 16)

                for wi, (which, wm, rawbuf) in enumerate(
                        (("q", wqm_sb, qraw), ("k", wkm_sb, kraw))):
                    for tc in range(4):
                        pb = tc % 2
                        G2 = 16 * h + wi * 4 + tc

                        @block.tensor
                        def _(tensor, h=h, which=which, wm=wm, tc=tc, pb=pb, G2=G2):
                            if which == "q" and tc == 0:
                                s["DG"].wait(tensor)
                                if h > 0:
                                    s["EV"].wait_n(tensor, ktm_all[-1])
                            if G2 < 1:
                                s["EV"].wait(tensor)
                            else:
                                s["EV"].wait_n(tensor, pqm[G2 - 1])
                            for k in range(KD):
                                mi = tensor.matmul(
                                    pqy[pb], wm[:, k * 128:(k + 1) * 128],
                                    xnT[:, k * T + tc * 512:k * T + (tc + 1) * 512],
                                    start=(k == 0), stop=(k == KD - 1),
                                    skip_group_check=True)
                            s["PD"].inc(mi)

                        @block.vector
                        def _(vector, rawbuf=rawbuf, tc=tc, pb=pb):
                            s["PD"].wait(vector)
                            s["EV"].inc(vector.tensor_copy(
                                rawbuf[:, tc * 512:(tc + 1) * 512], pqy[pb]))
                        pqm[G2] = s["EV"].mark()
                qrm.append(s["PD"].mark())

                for wi, (which, rawbuf, fT, bpi, bmi, fp_, fm_) in enumerate((
                        ("q", qraw, fmq_sb, BQP, BQM, qfp, qfm),
                        ("k", kraw, fmk_sb, BKP, BKM, kfp, kfm))):
                    for tc in range(4):
                        pb = tc % 2
                        G2 = 16 * h + 8 + wi * 4 + tc

                        @block.tensor
                        def _(tensor, which=which, rawbuf=rawbuf, fT=fT, tc=tc,
                              pb=pb, G2=G2):
                            s["EV"].wait_n(tensor, pqm[G2 - 1])
                            if which == "q" and tc == 0:
                                s["DSQ"].wait(tensor)
                                s["DSK"].wait(tensor)
                            mi = tensor.matmul(pqy[pb], fT[:, :],
                                               rawbuf[:, tc * 512:(tc + 1) * 512],
                                               start=True, stop=True,
                                               skip_group_check=True)
                            s["PD"].inc(mi)

                        @block.scalar
                        def _(scalar, tc=tc, pb=pb, bpi=bpi, bmi=bmi, fp_=fp_, fm_=fm_):
                            s["PD"].wait(scalar)
                            scalar.activation(fp_[:, tc * 512:(tc + 1) * 512], pqy[pb],
                                              AF.Exp, bias=bc(bpi), scale=1.0)
                            s["EV"].inc(scalar.activation(
                                fm_[:, tc * 512:(tc + 1) * 512], pqy[pb],
                                AF.Exp, bias=bc(bmi), scale=-1.0))
                        pqm[G2] = s["EV"].mark()

                @block.scalar
                def _(scalar):
                    s["EV"].inc(scalar.activation(echo2[:, :], ones_b[:, 0:1], AF.Exp))
                emk[h] = s["EV"].mark()

                @block.sync
                def _(sync, h=h):
                    s["EV"].wait(sync)
                    s["PD"].wait(sync)
                    s["DSQ"].inc(sync.dma_start(out=scr_qf[h * DF:h * DF + 128, :],
                                                in_=qfp), 16)
                    s["DSQ"].inc(sync.dma_start(out=scr_qf[h * DF + 128:(h + 1) * DF, :],
                                                in_=qfm), 16)

                for b in range(4):
                    pob = 0
                    for i in range(4):
                        c = 4 * b + i
                        nq = (4 - i) * 128
                        q0 = (4 * b + i) * 128

                        @block.tensor
                        def _(tensor, h=h, b=b, i=i, c=c, nq=nq, q0=q0):
                            ai = len(aam_all)
                            if ai >= 2:
                                s["EV"].wait_n(tensor, aam_all[ai - 2])
                            if b == 0 and i == 0:
                                s["EV"].wait_n(tensor, emk[h])
                            pb = i % 2
                            tensor.matmul(pA[pb][:, 0:nq], kfp[:, c * 128:(c + 1) * 128],
                                          qfp[:, q0:q0 + nq], start=True, stop=False,
                                          skip_group_check=True)
                            mi = tensor.matmul(pA[pb][:, 0:nq],
                                               kfm[:, c * 128:(c + 1) * 128],
                                               qfm[:, q0:q0 + nq], start=False,
                                               stop=True, skip_group_check=True)
                            s["PD"].inc(mi)

                        @block.vector
                        def _(vector, i=i, nq=nq):
                            s["PD"].wait(vector)
                            pb = i % 2
                            s["EV"].inc(vector.tensor_tensor(
                                attnT[:, i * 512:i * 512 + nq],
                                pA[pb][:, 0:nq], mask4[:, 0:nq], ALU.mult))
                        aam_all.append(s["EV"].mark())

                    for i in range(4):
                        c = 4 * b + i
                        pt0 = (c % 2) * 1024

                        @block.tensor
                        def _(tensor, h=h, c=c, pt0=pt0):
                            ki = len(ktm_all)
                            if ki >= 2:
                                s["EV"].wait_n(tensor, ktm_all[ki - 2])
                            tensor.matmul(pt[:, pt0:pt0 + 128],
                                          kfp[:, c * 128:(c + 1) * 128], ident_b[:, :],
                                          start=True, stop=False, is_transpose=True,
                                          skip_group_check=True)
                            mi = tensor.matmul(pt[:, pt0 + 128:pt0 + 256],
                                               kfm[:, c * 128:(c + 1) * 128],
                                               ident_b[:, :],
                                               start=False, stop=True,
                                               is_transpose=True,
                                               skip_group_check=True)
                            s["PD"].inc(mi)

                        @block.vector
                        def _(vector, c=c, pt0=pt0):
                            s["PD"].wait(vector)
                            s["EV"].inc(vector.tensor_copy(
                                kfT[:, c * DF:(c + 1) * DF], pt[:, pt0:pt0 + 256]))
                        ktm_all.append(s["EV"].mark())

                    @block.tensor
                    def _(tensor, h=h, b=b, pob=pob):
                        s["EV"].wait_n(tensor, aam_all[-1])
                        if b >= 1:
                            s["EV"].wait_n(tensor, om_all[-1])
                        elif h > 0:
                            s["EV"].wait_n(tensor, om_all[-1])
                        for i in range(4):
                            c = 4 * b + i
                            nq = (4 - i) * 128
                            vsl = v_sb[:, c * D + h * 128:c * D + (h + 1) * 128]
                            mi = tensor.matmul(pO[:, i * 128:512], vsl,
                                               attnT[:, i * 512:i * 512 + nq],
                                               start=(i == 0),
                                               stop=(b == 0 and i == 3),
                                               skip_group_check=True)
                        if b > 0:
                            tensor.matmul(pO[:, 0:512], S_sb[:, 0:128],
                                          qfp[:, b * 512:(b + 1) * 512],
                                          start=False, stop=False,
                                          skip_group_check=True)
                            mi = tensor.matmul(pO[:, 0:512], S_sb[:, 128:256],
                                               qfm[:, b * 512:(b + 1) * 512],
                                               start=False, stop=True,
                                               skip_group_check=True)
                        s["EV"].wait_n(tensor, ktm_all[-1])
                        if h > 0 and b == 0:
                            s["EV"].wait_n(tensor, scm_all[-1])
                        for i in range(4):
                            c = 4 * b + i
                            vsl = v_sb[:, c * D + h * 128:c * D + (h + 1) * 128]
                            tensor.matmul(pS[:, 0:128], kfT[:, c * DF:c * DF + 128],
                                          vsl, start=(b == 0 and i == 0), stop=False,
                                          skip_group_check=True)
                            mi = tensor.matmul(pS[:, 128:256],
                                               kfT[:, c * DF + 128:c * DF + 256], vsl,
                                               start=False,
                                               stop=(b == 3 and i == 3),
                                               skip_group_check=True)
                        s["PD"].inc(mi)

                    @block.vector
                    def _(vector, h=h, b=b):
                        s["PD"].wait(vector)
                        s["EV"].inc(vector.tensor_copy(
                            o_full[:, h * T + b * 512:h * T + (b + 1) * 512],
                            pO[:, 0:512]))
                        om_all.append(s["EV"].mark())
                        if b < 3:
                            s["EV"].inc(vector.tensor_copy(S_sb[:, :], pS[:, 0:256]))
                        else:
                            s["EV"].inc(vector.tensor_copy(
                                Sstage[:, h * DF:(h + 1) * DF], pS[:, 0:256]))
                        scm_all.append(s["EV"].mark())

            @block.sync
            def _(sync):
                s["EV"].wait(sync)
                s["DSS"].inc(sync.dma_start(out=cc_S_i[:, :], in_=Sstage), 16)
                s["DS"].inc(sync.dma_start(out=scr_o[:, :], in_=o_full[:, :]), 16)
                s["DSQ"].wait(sync)
                s["DSK"].wait(sync)
                s["DG"].wait(sync)
                s["DS"].wait(sync)

            @block.gpsimd
            def _(gpsimd):
                s["DSS"].wait(gpsimd)
                s["CC"].inc(gpsimd.collective_compute(
                    "AllGather", ALU.bypass, replica_groups=[list(range(8))],
                    ins=[cc_S_i[:, :].opt()], outs=[cc_S_o[:, :].opt()]))

            _barrier(block, s)

    _phase2()

    def _phase1():
        NSL = FF // 256
        with (
            nc.sbuf_tensor("xnqT", [128, KD * T], BF16) as xnqT,
            nc.sbuf_tensor("wgT", [128, KD * FF], BF16) as wgT,
            nc.sbuf_tensor("wuT", [128, KD * FF], BF16) as wuT,
            nc.sbuf_tensor("sg", [128, 4 * 256], F32) as sg,
            nc.sbuf_tensor("h_sb", [128, 2 * FF], F32) as h_sb,
            nc.sbuf_tensor("fS", [128, 4 * (FF // 16)], F32) as fS,
            nc.sbuf_tensor("fB", [128, 3 * FF], F32) as fB,
            nc.sbuf_tensor("fW", [128, FF], BF16) as fW,
            nc.sbuf_tensor("hq_b", [128, 2 * FF], BF16) as hq_b,
            nc.psum_tensor("pf", [128, 4096], F32) as pf,
        ):
            NB = FF // 16
            fqb = dict(amax=fS[:, 0:NB], sc=fS[:, NB:2 * NB],
                       rec=fS[:, 2 * NB:3 * NB],
                       v=fB[:, 0:FF], w=fW[:, 0:FF],
                       M=fB[:, FF:2 * FF], tq=fB[:, 2 * FF:3 * FF])

            @block.sync
            def _(sync):
                for k in range(KD):
                    s["DS"].inc(sync.dma_start(out=xnqT[:, k * T:(k + 1) * T],
                                               in_=scr_xnq[:, k * 128:(k + 1) * 128],
                                               transpose=True), 16)
                sync.wait_ge(s["CC"].h, 1)
                for k in range(KD):
                    s["DS"].inc(sync.dma_start(out=wgT[:, k * FF:(k + 1) * FF],
                                               in_=cc_wg_o[:, k * 128:(k + 1) * 128],
                                               transpose=True), 16)
                sync.wait_ge(s["CC"].h, 2)
                for k in range(KD):
                    s["DS"].inc(sync.dma_start(out=wuT[:, k * FF:(k + 1) * FF],
                                               in_=cc_wu_o[:, k * 128:(k + 1) * 128],
                                               transpose=True), 16)

            sim1 = []
            hmm1 = []
            dsm1 = {}
            vdm1 = {}

            def q_p1(tt):
                hb = tt % 2
                srcq = h_sb[:, hb * FF:(hb + 1) * FF]

                @block.vector
                def _(vector):
                    s["VD"].wait(vector)
                    src3 = srcq.rearrange("p (nb k) -> p nb k", k=16)
                    s["VD"].inc(vector.tensor_reduce(
                        fqb["amax"][:, 0:NB].unsqueeze(2), src3, AX.X, ALU.max,
                        apply_absolute_value=True))
                    s["VD"].wait(vector)
                    s["VD"].inc(vector.tensor_scalar(
                        fqb["sc"][:, 0:NB], fqb["amax"][:, 0:NB],
                        1.0 / 6.0, 1e-12, ALU.mult, ALU.max))
                    s["VD"].wait(vector)
                    s["VD"].inc(vector.reciprocal(fqb["rec"][:, 0:NB],
                                                  fqb["sc"][:, 0:NB]))
                    s["VD"].wait(vector)
                    rbc = fqb["rec"][:, 0:NB].unsqueeze(2).broadcast_to([128, NB, 16])
                    s["VD"].inc(vector.tensor_tensor(
                        fqb["v"][:, 0:FF].rearrange("p (nb k) -> p nb k", k=16),
                        src3, rbc, ALU.mult))

                @block.gpsimd
                def _(gpsimd):
                    s["VD"].wait(gpsimd)
                    s["GD"].inc(gpsimd.tensor_tensor(
                        fqb["w"][:, 0:FF], fqb["v"][:, 0:FF], fqb["v"][:, 0:FF],
                        ALU.mult))

            def q_p2(tt):
                gm = s["GD"].mark()

                @block.vector
                def _(vector):
                    s["GD"].wait_n(vector, gm)
                    s["VD"].inc(vector.tensor_scalar(
                        fqb["M"][:, 0:FF], fqb["w"][:, 0:FF], QC1, QC0,
                        ALU.mult, ALU.add))

                @block.gpsimd
                def _(gpsimd):
                    s["VD"].wait(gpsimd)
                    s["GD"].inc(gpsimd.tensor_tensor(
                        fqb["tq"][:, 0:FF], fqb["v"][:, 0:FF], fqb["M"][:, 0:FF],
                        ALU.add))

            def q_p3(tt):
                hb = tt % 2
                gm = s["GD"].mark()

                @block.vector
                def _(vector, tt=tt, hb=hb):
                    s["GD"].wait_n(vector, gm)
                    if tt >= 2:
                        s["DS"].wait_n(vector, dsm1[tt - 1])
                    s["VD"].inc(vector.tensor_tensor(
                        fqb["tq"][:, 0:FF], fqb["tq"][:, 0:FF], fqb["M"][:, 0:FF],
                        ALU.subtract))
                    s["VD"].wait(vector)
                    sbc = fqb["sc"][:, 0:NB].unsqueeze(2).broadcast_to([128, NB, 16])
                    s["VD"].inc(vector.tensor_tensor(
                        hq_b[:, hb * FF:(hb + 1) * FF].rearrange(
                            "p (nb k) -> p nb k", k=16),
                        fqb["tq"][:, 0:FF].rearrange("p (nb k) -> p nb k", k=16),
                        sbc, ALU.mult))
                vdm1[tt] = s["VD"].mark()

            def q_stash(tt):
                hb = tt % 2

                @block.sync
                def _(sync, tt=tt, hb=hb):
                    s["VD"].wait_n(sync, vdm1[tt])
                    s["DS"].inc(sync.dma_start(out=scr_h[tt * 128:(tt + 1) * 128, :],
                                               in_=hq_b[:, hb * FF:(hb + 1) * FF]), 16)
                dsm1[tt] = s["DS"].mark()

            for t in range(NT):
                hb = t % 2
                for sl in range(NSL):
                    S1 = t * NSL + sl
                    bk = S1 % 8
                    rb = S1 % 4
                    c0 = bk * 512

                    @block.tensor
                    def _(tensor, t=t, sl=sl, S1=S1, c0=c0):
                        if S1 == 0:
                            s["DS"].wait(tensor)
                        if S1 >= 8:
                            s["EV"].wait_n(tensor, sim1[S1 - 7])
                            s["VD"].wait_n(tensor, hmm1[S1 - 8])
                        n0 = sl * 256
                        for k in range(KD):
                            lhs = xnqT[:, k * T + t * 128:k * T + (t + 1) * 128]
                            tensor.matmul(pf[:, c0:c0 + 256], lhs,
                                          wgT[:, k * FF + n0:k * FF + n0 + 256],
                                          start=(k == 0), stop=False,
                                          skip_group_check=True)
                            mi = tensor.matmul(pf[:, c0 + 256:c0 + 512], lhs,
                                               wuT[:, k * FF + n0:k * FF + n0 + 256],
                                               start=False, stop=(k == KD - 1),
                                               skip_group_check=True)
                        s["PD"].inc(mi)

                    @block.scalar
                    def _(scalar, S1=S1, rb=rb, c0=c0):
                        s["PD"].wait(scalar)
                        if S1 >= 4:
                            s["VD"].wait_n(scalar, hmm1[S1 - 4])
                        s["EV"].inc(scalar.activation(sg[:, rb * 256:(rb + 1) * 256],
                                                      pf[:, c0:c0 + 256], AF.Silu))
                    sim1.append(s["EV"].mark())

                    @block.vector
                    def _(vector, t=t, sl=sl, S1=S1, rb=rb, c0=c0, hb=hb):
                        s["PD"].wait(vector)
                        s["EV"].wait_n(vector, sim1[S1])
                        s["VD"].inc(vector.tensor_tensor(
                            h_sb[:, hb * FF + sl * 256:hb * FF + (sl + 1) * 256],
                            pf[:, c0 + 256:c0 + 512],
                            sg[:, rb * 256:(rb + 1) * 256], ALU.mult))
                    hmm1.append(s["VD"].mark())

                    if t >= 1:
                        if sl == 1:
                            q_p1(t - 1)
                        elif sl == 4:
                            q_p2(t - 1)
                        elif sl == 8:
                            q_p3(t - 1)
                        elif sl == 9:
                            q_stash(t - 1)

            q_p1(NT - 1)
            q_p2(NT - 1)
            q_p3(NT - 1)
            q_stash(NT - 1)

            @block.sync
            def _(sync):
                s["DS"].wait(sync)

            _barrier(block, s)

    _phase1()


    g_sb = stack.enter_context(nc.sbuf_tensor("g_sb", [128, NT * D], BF16))
    Sg = stack.enter_context(nc.sbuf_tensor("Sg", [128, 2 * H * DF], BF16))
    S_pre = stack.enter_context(nc.sbuf_tensor("S_pre", [128, H * DF], BF16))
    woT = stack.enter_context(nc.sbuf_tensor("woT", [128, KD * D], BF16))

    def _phase4_prework():
        for r in range(8):
            rb = r % 2

            @block.sync
            def _(sync, r=r, rb=rb):
                if r == 0:
                    sync.wait_ge(s["CC"].h, 4)
                if r >= 2:
                    s["VD"].wait(sync)
                s["DSS"].inc(sync.dma_start(
                    out=Sg[:, rb * H * DF:(rb + 1) * H * DF],
                    in_=cc_S_o[r * 128:(r + 1) * 128, :]), 16)

            @block.vector
            def _(vector, r=r, rb=rb):
                s["DSS"].wait(vector)
                s["VD"].wait(vector)
                if r == 0:
                    li = vector.tensor_scalar(S_pre[:, :], Sg[:, 0:H * DF],
                                              smask_sb[:, 0:1], None, ALU.mult)
                else:
                    li = vector.scalar_tensor_tensor(
                        S_pre[:, :], Sg[:, rb * H * DF:(rb + 1) * H * DF],
                        smask_sb[:, r:r + 1], S_pre[:, :], ALU.mult, ALU.add)
                s["VD"].inc(li)

        @block.gpsimd
        def _(gpsimd):
            for k in range(KD):
                s["DG"].inc(gpsimd.dma_start(out=woT[:, k * D:(k + 1) * D],
                                             in_=wo_T[k * 128:(k + 1) * 128, :]), 16)

        @block.vector
        def _(vector):
            s["DG"].wait(vector)
            li = None
            for k in range(KD):
                li = vector.tensor_scalar(woT[:, k * D:(k + 1) * D],
                                          woT[:, k * D:(k + 1) * D],
                                          bc(RMS), None, ALU.mult)
            s["VD"].inc(li)

    def _phase3():
        with (
            nc.sbuf_tensor("wdT", [128, NFK * D], BF16) as wdT,
            nc.sbuf_tensor("hqT", [128, 2 * NFK * 512], BF16) as hqT,
            nc.psum_tensor("pD", [128, 2048], F32) as pDm,
        ):
            pD = (pDm[:, 0:1024], pDm[:, 1024:2048])

            @block.sync
            def _(sync):
                sync.wait_ge(s["CC"].h, 3)
                for n in range(NFK):
                    s["DS"].inc(sync.dma_start(out=wdT[:, n * D:(n + 1) * D],
                                               in_=cc_wd_o[:, n * 128:(n + 1) * 128],
                                               transpose=True), 16)

            p3m = []
            pdm3 = []
            for tb in range(4):
                hb = tb % 2

                @block.sync
                def _(sync, tb=tb, hb=hb):
                    if tb >= 2:
                        s["PD"].wait_n(sync, pdm3[tb - 2])
                    for n in range(NFK):
                        s["DA"].inc(sync.dma_start(
                            out=hqT[:, (hb * NFK + n) * 512:(hb * NFK + n + 1) * 512],
                            in_=scr_h[tb * 512:(tb + 1) * 512, n * 128:(n + 1) * 128],
                            transpose=True), 16)

                for ti in range(4):
                    t = tb * 4 + ti
                    pb = t % 2

                    @block.tensor
                    def _(tensor, hb=hb, ti=ti, t=t, pb=pb):
                        if t == 0:
                            s["DS"].wait(tensor)
                        if ti == 0:
                            s["DA"].wait(tensor)
                        if t >= 1:
                            s["EV"].wait_n(tensor, p3m[t - 1])
                        for n in range(NFK):
                            lhs = hqT[:, (hb * NFK + n) * 512 + ti * 128:
                                      (hb * NFK + n) * 512 + (ti + 1) * 128]
                            tensor.matmul(pD[pb][:, 0:512], lhs, wdT[:, n * D:n * D + 512],
                                          start=(n == 0), stop=(n == NFK - 1),
                                          skip_group_check=True)
                            mi = tensor.matmul(pD[pb][:, 512:1024], lhs,
                                               wdT[:, n * D + 512:(n + 1) * D],
                                               start=(n == 0), stop=(n == NFK - 1),
                                               skip_group_check=True)
                        s["PD"].inc(mi)

                    @block.scalar
                    def _(scalar, t=t, pb=pb):
                        s["PD"].wait(scalar)
                        s["EV"].inc(scalar.activation(g_sb[:, t * D:(t + 1) * D],
                                                      pD[pb][:, :], AF.Silu))
                    p3m.append(s["EV"].mark())
                pdm3.append(s["PD"].mark())

            _barrier(block, s)

    _phase4_prework()
    _phase3()

    s_sb = stack.enter_context(nc.sbuf_tensor("s_sb", [128, NT * D], BF16))

    def _phase4():
        with (
            nc.sbuf_tensor("qf2", [128, 2 * T], BF16) as qf2,
            nc.sbuf_tensor("o2", [128, T], BF16) as o2,
            nc.sbuf_tensor("lsum", [128, T], F32) as lsum,
            nc.sbuf_tensor("rstd", [128, T], BF16) as rstd,
            nc.sbuf_tensor("o_nrm", [128, H * T], BF16) as o_nrm,
            nc.sbuf_tensor("o_full4", [128, H * T], BF16) as o_full,
            nc.sbuf_tensor("x2", [128, 2 * D], F32) as x2,
            nc.sbuf_tensor("sgp", [128, 2 * D], F32) as sgp,
            nc.psum_tensor("pX", [128, 4096], F32) as pX,
        ):
            qfp2, qfm2 = qf2[:, 0:T], qf2[:, T:2 * T]
            pC = (pX[:, 0:512], pX[:, 512:1024])
            pR = (pX[:, 1024:1536], pX[:, 1536:2048])
            pW = (pX[:, 2048:3072], pX[:, 3072:4096])

            @block.sync
            def _(sync):
                s["DS"].inc(sync.dma_start(out=o_full[:, :], in_=scr_o[:, :]), 16)

            cpm = []
            for h in range(H):
                @block.sync
                def _(sync, h=h):
                    if h >= 1:
                        s["PD"].wait(sync)
                    s["DS"].inc(sync.dma_start(out=qfp2,
                                               in_=scr_qf[h * DF:h * DF + 128, :]), 16)
                    s["DS"].inc(sync.dma_start(out=qfm2,
                                               in_=scr_qf[h * DF + 128:(h + 1) * DF, :]), 16)

                for tc in range(4):
                    pb = tc % 2

                    @block.tensor
                    def _(tensor, h=h, tc=tc, pb=pb):
                        if tc == 0:
                            s["DS"].wait(tensor)
                            if h == 0:
                                s["VD"].wait(tensor)
                        if h * 4 + tc >= 2:
                            s["EV"].wait_n(tensor, cpm[h * 4 + tc - 2])
                        tensor.matmul(pC[pb], S_pre[:, h * DF:h * DF + 128],
                                      qfp2[:, tc * 512:(tc + 1) * 512],
                                      start=True, stop=False, skip_group_check=True)
                        mi = tensor.matmul(pC[pb], S_pre[:, h * DF + 128:(h + 1) * DF],
                                           qfm2[:, tc * 512:(tc + 1) * 512],
                                           start=False, stop=True, skip_group_check=True)
                        s["PD"].inc(mi)

                    @block.vector
                    def _(vector, h=h, tc=tc, pb=pb):
                        s["PD"].wait(vector)
                        if h == 0 and tc == 0:
                            s["DS"].wait(vector)
                        sl = slice(h * T + tc * 512, h * T + (tc + 1) * 512)
                        s["EV"].inc(vector.tensor_tensor(o_full[:, sl], o_full[:, sl],
                                                         pC[pb], ALU.add))
                    cpm.append(s["EV"].mark())

                @block.vector
                def _(vector, h=h):
                    s["EV"].wait(vector)
                    s["PD"].wait(vector)
                    s["EV"].inc(vector.scalar_tensor_tensor(
                        o2[:, :], o_full[:, h * T:(h + 1) * T], 2.0 ** -16,
                        o_full[:, h * T:(h + 1) * T], ALU.mult, ALU.mult))

                for tc in range(4):
                    pb = tc % 2

                    @block.tensor
                    def _(tensor, tc=tc, pb=pb):
                        s["EV"].wait(tensor)
                        mi = tensor.matmul(pR[pb], ones_b[:, :],
                                           o2[:, tc * 512:(tc + 1) * 512],
                                           start=True, stop=True, skip_group_check=True)
                        s["PD"].inc(mi)

                    @block.scalar
                    def _(scalar, tc=tc, pb=pb):
                        s["PD"].wait(scalar)
                        s["EV"].inc(scalar.activation(lsum[:, tc * 512:(tc + 1) * 512],
                                                      pR[pb], AF.Ln, bias=eps2_sb[:, 0:1],
                                                      scale=2.0 ** -48 / DV))

                @block.scalar
                def _(scalar, h=h):
                    li = None
                    for tc in range(4):
                        li = scalar.activation(rstd[:, tc * 512:(tc + 1) * 512],
                                               lsum[:, tc * 512:(tc + 1) * 512],
                                               AF.Exp, scale=-0.5, bias=nl2_sb[:, 0:1])
                    s["EV"].inc(li)

                @block.vector
                def _(vector, h=h):
                    s["EV"].wait(vector)
                    s["EV"].inc(vector.tensor_tensor(o_nrm[:, h * T:(h + 1) * T],
                                                     o_full[:, h * T:(h + 1) * T],
                                                     rstd[:, :], ALU.mult))

            wom = []
            vm5 = []
            ssm5 = []

            def emit_y(t):
                pb5 = t % 2

                @block.sync
                def _(sync, t=t, pb5=pb5):
                    if t >= 2:
                        s["VD"].wait_n(sync, vm5[t - 2])
                    s["DS"].inc(sync.dma_start(out=x2[:, pb5 * D:(pb5 + 1) * D],
                                               in_=x_in[t * 128:(t + 1) * 128, :]), 16)

                @block.vector
                def _(vector, t=t, pb5=pb5):
                    s["DS"].wait(vector)
                    s["EV"].wait_n(vector, wom[t])
                    if t >= 2:
                        s["DSS"].wait_n(vector, ssm5[t - 2])
                    s["VD"].inc(vector.tensor_tensor(sgp[:, pb5 * D:(pb5 + 1) * D],
                                                     s_sb[:, t * D:(t + 1) * D],
                                                     g_sb[:, t * D:(t + 1) * D],
                                                     ALU.mult))
                    s["VD"].wait(vector)
                    s["VD"].inc(vector.tensor_tensor(sgp[:, pb5 * D:(pb5 + 1) * D],
                                                     x2[:, pb5 * D:(pb5 + 1) * D],
                                                     sgp[:, pb5 * D:(pb5 + 1) * D],
                                                     ALU.add))
                vm5.append(s["VD"].mark())

                @block.sync
                def _(sync, t=t, pb5=pb5):
                    s["VD"].wait_n(sync, vm5[t])
                    s["DSS"].inc(sync.dma_start(out=y_out[t * 128:(t + 1) * 128, :],
                                                in_=sgp[:, pb5 * D:(pb5 + 1) * D]), 16)
                ssm5.append(s["DSS"].mark())

            for t in range(NT):
                pb = t % 2

                @block.tensor
                def _(tensor, t=t, pb=pb):
                    if t == 0:
                        s["VD"].wait(tensor)
                    if t < 1:
                        s["EV"].wait(tensor)
                    elif t >= 2:
                        s["EV"].wait_n(tensor, wom[t - 2])
                    for k in range(KD):
                        lhs = o_nrm[:, k * T + t * 128:k * T + (t + 1) * 128]
                        tensor.matmul(pW[pb][:, 0:512], lhs, woT[:, k * D:k * D + 512],
                                      start=(k == 0), stop=(k == KD - 1),
                                      skip_group_check=True)
                        mi = tensor.matmul(pW[pb][:, 512:1024], lhs,
                                           woT[:, k * D + 512:(k + 1) * D],
                                           start=(k == 0), stop=(k == KD - 1),
                                           skip_group_check=True)
                    s["PD"].inc(mi)

                @block.vector
                def _(vector, t=t, pb=pb):
                    s["PD"].wait(vector)
                    s["EV"].inc(vector.tensor_copy(s_sb[:, t * D:(t + 1) * D],
                                                   pW[pb][:, :]))
                wom.append(s["EV"].mark())
                emit_y(t)

            @block.sync
            def _(sync):
                s["DSS"].wait(sync)

    _phase4()


    if DEBUG_TAPS:
        taps = {
            "dbg_xn": (scr_xn, [T, D]), "dbg_xnq": (scr_xnq, [T, D]),
            "dbg_h": (scr_h, [T, FF]), "dbg_qf": (scr_qf, [H * DF, T]),
            "dbg_kf": (scr_kf, [H * DF, T]),
            "dbg_wg": (cc_wg_o, [FF, D]), "dbg_wu": (cc_wu_o, [FF, D]),
            "dbg_wd": (cc_wd_o, [D, FF]),
            "dbg_Si": (cc_S_i, [128, H * DF]), "dbg_So": (cc_S_o, [8 * 128, H * DF]),
        }
        taps["dbg_ofull"] = (scr_o, [128, H * T])
        dbg_outs = {k: nc.dram_tensor(k, shp, BF16, kind="ExternalOutput").ap()
                    for k, (_, shp) in taps.items()}
        sb_taps = {"dbg_g": (g_sb, [128, NT * D]), "dbg_s": (s_sb, [128, NT * D])}
        for k, (_, shp) in sb_taps.items():
            dbg_outs[k] = nc.dram_tensor(k, shp, BF16, kind="ExternalOutput").ap()

        @block.sync
        def _(sync):
            for k, (src_t, shp) in taps.items():
                s["DS"].inc(sync.dma_start(out=dbg_outs[k][:, :], in_=src_t[:, :]), 16)
            for k, (src_t, shp) in sb_taps.items():
                s["DS"].inc(sync.dma_start(out=dbg_outs[k][:, :], in_=src_t[:, :]), 16)
            s["DS"].wait(sync)

    stack.close()
    return nc


def kernel(L0_attn_x, L0_attn_Wq, L0_attn_Wk, L0_attn_Wv, L0_attn_Wo,
           L0_attn_fmq_w, L0_attn_fmq_b, L0_attn_fmk_w, L0_attn_fmk_b,
           L0_attn_rms_w, L0_ffn_w_gate, L0_ffn_w_up, L0_ffn_w_down):
    f32 = np.float32
    x = np.ascontiguousarray(np.asarray(L0_attn_x, dtype=f32))
    Wq = np.asarray(L0_attn_Wq, dtype=f32)
    Wk = np.asarray(L0_attn_Wk, dtype=f32)
    Wv = np.asarray(L0_attn_Wv, dtype=f32)
    Wo = np.asarray(L0_attn_Wo, dtype=f32)
    fmq_w = np.asarray(L0_attn_fmq_w, dtype=f32)
    fmk_w = np.asarray(L0_attn_fmk_w, dtype=f32)
    fmq_b = np.asarray(L0_attn_fmq_b, dtype=f32)
    fmk_b = np.asarray(L0_attn_fmk_b, dtype=f32)
    rms_w = np.asarray(L0_attn_rms_w, dtype=f32)
    w_gate = np.asarray(L0_ffn_w_gate, dtype=f32)
    w_up = np.asarray(L0_ffn_w_up, dtype=f32)
    w_down = np.asarray(L0_ffn_w_down, dtype=f32)

    wq_t = np.ascontiguousarray(Wq.T.reshape(D, H, 128).transpose(1, 0, 2))
    wk_t = np.ascontiguousarray(Wk.T.reshape(D, H, 128).transpose(1, 0, 2))
    wv_Tc = np.ascontiguousarray(Wv.T)
    wo_Tc = np.ascontiguousarray(Wo.T)
    fmq_Tc = np.ascontiguousarray(fmq_w.T)
    fmk_Tc = np.ascontiguousarray(fmk_w.T)
    fmq_bc = np.ascontiguousarray(fmq_b.reshape(128, 1))
    fmk_bc = np.ascontiguousarray(fmk_b.reshape(128, 1))
    rms_cc = np.ascontiguousarray(rms_w.reshape(128, 1))

    nc = build_nc()

    in_maps = []
    for c in range(8):
        b, q = c // 4, c % 4
        msk = np.zeros((128, 8), dtype=np.float32)
        msk[:, 4 * b:4 * b + q] = 1.0
        in_maps.append({
            "x": np.ascontiguousarray(x[b, q * T:(q + 1) * T, :]),
            "wq_t": wq_t, "wk_t": wk_t, "wv_T": wv_Tc, "wo_T": wo_Tc,
            "fmq_T": fmq_Tc, "fmk_T": fmk_Tc, "fmq_b": fmq_bc, "fmk_b": fmk_bc,
            "rms_c": rms_cc,
            "wg_sl": np.ascontiguousarray(w_gate[c * WSL:(c + 1) * WSL, :]),
            "wu_sl": np.ascontiguousarray(w_up[c * WSL:(c + 1) * WSL, :]),
            "wd_sl": np.ascontiguousarray(w_down[c * 128:(c + 1) * 128, :]),
            "smask": msk,
        })

    res = run_bass_kernel_spmd(nc, in_maps, core_ids=list(range(8)))
    globals()["_last_results"] = res.results

    out = np.empty((2, 8192, D), dtype=f32)
    for c in range(8):
        b, q = c // 4, c % 4
        out[b, q * T:(q + 1) * T, :] = np.asarray(res.results[c]["y"])
    return out


# version 21 (final)
# speedup vs baseline: 1.0236x; 1.0192x over previous
import numpy as np
import ml_dtypes
import concourse.bass as bass
import concourse.mybir as mybir
from concourse.bass_utils import run_bass_kernel_spmd

F32 = mybir.dt.float32
BF16 = mybir.dt.bfloat16
AF = mybir.ActivationFunctionType
ALU = mybir.AluOpType
AX = mybir.AxisListType

T = 2048
D = 1024
NT = 16
KD = 8
H = 8
DV = 128
DF = 256
NCH = 16
FF = 2816
NFK = 22
QW = 704
WSL = FF // 8
LN16 = float(np.log(16.0))
QC0 = float(np.float32((4.0 / 3.0) * 2 ** 22))
QC1 = float(np.float32(2 ** 22 / 6.0))

DEBUG_TAPS = False


class SemC:
    def __init__(self, stack, nc, name):
        self.h = stack.enter_context(nc.semaphore(name))
        self.n = 0

    def inc(self, instr, k=1):
        instr.then_inc(self.h, k)
        self.n += k
        return instr

    def wait(self, eng):
        eng.wait_ge(self.h, self.n)

    def mark(self):
        return self.n

    def wait_n(self, eng, n):
        eng.wait_ge(self.h, n)


def _quant_tile(block, s, bufs, src_ap, n, dq_ap, pre=("VD",)):
    nb = n // 16
    amax, sc, rec, v, w, M, tq = (bufs[k] for k in
                                  ("amax", "sc", "rec", "v", "w", "M", "tq"))
    p = src_ap.shape[0]

    @block.vector
    def _(vector):
        for k in pre:
            s[k].wait(vector)
        src3 = src_ap.rearrange("p (nb k) -> p nb k", k=16)
        s["VD"].inc(vector.tensor_reduce(amax[0:p, 0:nb].unsqueeze(2), src3, AX.X,
                                         ALU.max, apply_absolute_value=True))
        s["VD"].wait(vector)
        s["VD"].inc(vector.tensor_scalar(sc[0:p, 0:nb], amax[0:p, 0:nb],
                                         1.0 / 6.0, 1e-12, ALU.mult, ALU.max))
        s["VD"].wait(vector)
        s["VD"].inc(vector.reciprocal(rec[0:p, 0:nb], sc[0:p, 0:nb]))
        s["VD"].wait(vector)
        src3 = src_ap.rearrange("p (nb k) -> p nb k", k=16)
        rbc = rec[0:p, 0:nb].unsqueeze(2).broadcast_to([p, nb, 16])
        s["VD"].inc(vector.tensor_tensor(
            v[0:p, 0:n].rearrange("p (nb k) -> p nb k", k=16), src3, rbc, ALU.mult))

    @block.scalar
    def _(scalar):
        s["VD"].wait(scalar)
        s["AD"].inc(scalar.activation(w[0:p, 0:n], v[0:p, 0:n], AF.Square))

    @block.vector
    def _(vector):
        s["AD"].wait(vector)
        s["VD"].inc(vector.tensor_scalar(M[0:p, 0:n], w[0:p, 0:n], QC1, QC0,
                                         ALU.mult, ALU.add))
        s["VD"].wait(vector)
        s["VD"].inc(vector.tensor_tensor(tq[0:p, 0:n], v[0:p, 0:n], M[0:p, 0:n],
                                         ALU.add))
        s["VD"].wait(vector)
        s["VD"].inc(vector.tensor_tensor(tq[0:p, 0:n], tq[0:p, 0:n], M[0:p, 0:n],
                                         ALU.subtract))
        s["VD"].wait(vector)
        sbc = sc[0:p, 0:nb].unsqueeze(2).broadcast_to([p, nb, 16])
        s["VD"].inc(vector.tensor_tensor(
            dq_ap.rearrange("p (nb k) -> p nb k", k=16),
            tq[0:p, 0:n].rearrange("p (nb k) -> p nb k", k=16), sbc, ALU.mult))




def _quant2(block, s, bufs, src_ap, n, dq_ap, pre=("VD",)):
    nb = n // 16
    amax, sc, rec, v, w, M, tq = (bufs[k] for k in
                                  ("amax", "sc", "rec", "v", "w", "M", "tq"))
    p = src_ap.shape[0]

    @block.vector
    def _(vector):
        for k in pre:
            s[k].wait(vector)
        src3 = src_ap.rearrange("p (nb k) -> p nb k", k=16)
        s["VD"].inc(vector.tensor_reduce(amax[0:p, 0:nb].unsqueeze(2), src3, AX.X,
                                         ALU.max, apply_absolute_value=True))
        s["VD"].wait(vector)
        s["VD"].inc(vector.tensor_scalar(sc[0:p, 0:nb], amax[0:p, 0:nb],
                                         1.0 / 6.0, 1e-12, ALU.mult, ALU.max))
        s["VD"].wait(vector)
        s["VD"].inc(vector.reciprocal(rec[0:p, 0:nb], sc[0:p, 0:nb]))
        s["VD"].wait(vector)
        src3 = src_ap.rearrange("p (nb k) -> p nb k", k=16)
        rbc = rec[0:p, 0:nb].unsqueeze(2).broadcast_to([p, nb, 16])
        s["VD"].inc(vector.tensor_tensor(
            v[0:p, 0:n].rearrange("p (nb k) -> p nb k", k=16), src3, rbc, ALU.mult))

    @block.gpsimd
    def _(gpsimd):
        s["VD"].wait(gpsimd)
        s["GD"].inc(gpsimd.tensor_tensor(w[0:p, 0:n], v[0:p, 0:n], v[0:p, 0:n],
                                         ALU.mult))

    @block.vector
    def _(vector):
        s["GD"].wait(vector)
        s["VD"].inc(vector.tensor_scalar(M[0:p, 0:n], w[0:p, 0:n], QC1, QC0,
                                         ALU.mult, ALU.add))

    @block.gpsimd
    def _(gpsimd):
        s["VD"].wait(gpsimd)
        s["GD"].inc(gpsimd.tensor_tensor(tq[0:p, 0:n], v[0:p, 0:n], M[0:p, 0:n],
                                         ALU.add))

    @block.vector
    def _(vector):
        s["GD"].wait(vector)
        s["VD"].inc(vector.tensor_tensor(tq[0:p, 0:n], tq[0:p, 0:n], M[0:p, 0:n],
                                         ALU.subtract))
        s["VD"].wait(vector)
        sbc = sc[0:p, 0:nb].unsqueeze(2).broadcast_to([p, nb, 16])
        s["VD"].inc(vector.tensor_tensor(
            dq_ap.rearrange("p (nb k) -> p nb k", k=16),
            tq[0:p, 0:n].rearrange("p (nb k) -> p nb k", k=16), sbc, ALU.mult))


def _barrier(block, s):
    tgt = s["PH"].n + 5
    s["PH"].n = tgt
    for engname in ("sync", "tensor", "vector", "scalar", "gpsimd"):
        def mk():
            def _(eng):
                for k in ("VD", "AD", "PD", "GD", "EV"):
                    s[k].wait(eng)
                eng.sem_inc(s["PH"].h, 1)
                eng.wait_ge(s["PH"].h, tgt)
            return _
        getattr(block, engname)(mk())


def build_nc():
    import contextlib
    stack = contextlib.ExitStack()
    nc = bass.Bass("TRN2", target_bir_lowering=False)

    x_in = nc.dram_tensor("x", [T, D], F32, kind="ExternalInput").ap()
    y_out = nc.dram_tensor("y", [T, D], F32, kind="ExternalOutput").ap()
    wq_t = nc.dram_tensor("wq_t", [H, D, 128], F32, kind="ExternalInput").ap()
    wk_t = nc.dram_tensor("wk_t", [H, D, 128], F32, kind="ExternalInput").ap()
    wv_T = nc.dram_tensor("wv_T", [D, D], F32, kind="ExternalInput").ap()
    wo_T = nc.dram_tensor("wo_T", [D, D], F32, kind="ExternalInput").ap()
    fmq_T = nc.dram_tensor("fmq_T", [128, 128], F32, kind="ExternalInput").ap()
    fmk_T = nc.dram_tensor("fmk_T", [128, 128], F32, kind="ExternalInput").ap()
    fmq_b = nc.dram_tensor("fmq_b", [128, 1], F32, kind="ExternalInput").ap()
    fmk_b = nc.dram_tensor("fmk_b", [128, 1], F32, kind="ExternalInput").ap()
    rms_c = nc.dram_tensor("rms_c", [128, 1], F32, kind="ExternalInput").ap()
    wg_sl = nc.dram_tensor("wg_sl", [WSL, D], F32, kind="ExternalInput").ap()
    wu_sl = nc.dram_tensor("wu_sl", [WSL, D], F32, kind="ExternalInput").ap()
    wd_sl = nc.dram_tensor("wd_sl", [128, FF], F32, kind="ExternalInput").ap()
    smask = nc.dram_tensor("smask", [128, 8], F32, kind="ExternalInput").ap()

    scr_xn = nc.dram_tensor("scr_xn", [T, D], BF16).ap()
    scr_xnq = nc.dram_tensor("scr_xnq", [T, D], BF16).ap()
    scr_h = nc.dram_tensor("scr_h", [T, FF], BF16).ap()
    scr_kf = nc.dram_tensor("scr_kf", [H * DF, T], BF16).ap()
    scr_qf = nc.dram_tensor("scr_qf", [H * DF, T], BF16).ap()
    cc_wg_i = nc.dram_tensor("cc_wg_i", [WSL, D], BF16).ap()
    cc_wg_o = nc.dram_tensor("cc_wg_o", [FF, D], BF16).ap()
    cc_wu_i = nc.dram_tensor("cc_wu_i", [WSL, D], BF16).ap()
    cc_wu_o = nc.dram_tensor("cc_wu_o", [FF, D], BF16).ap()
    cc_wd_i = nc.dram_tensor("cc_wd_i", [128, FF], BF16).ap()
    cc_wd_o = nc.dram_tensor("cc_wd_o", [D, FF], BF16).ap()
    cc_S_i = nc.dram_tensor("cc_S_i", [128, H * DF], BF16).ap()
    cc_S_o = nc.dram_tensor("cc_S_o", [8 * 128, H * DF], BF16).ap()

    s = {k: SemC(stack, nc, k) for k in
         ("DS", "DA", "DG", "DSK", "DSQ", "DSS", "CC", "PH", "PD", "VD", "AD", "GD", "EV")}

    block = stack.enter_context(nc.Block())

    def alloc(name, shape, dt):
        return stack.enter_context(nc.sbuf_tensor(name, shape, dt))

    ones_b = alloc("ones_b", [128, 128], BF16)
    mask4 = alloc("mask4", [128, 512], BF16)
    fmq_sb = alloc("fmq_sb", [128, 128], BF16)
    fmk_sb = alloc("fmk_sb", [128, 128], BF16)
    bcol = alloc("bcol", [128, 8], F32)
    eps_sb = alloc("eps_sb", [128, 1], F32)
    eps2_sb = alloc("eps2_sb", [128, 1], F32)
    nl2_sb = alloc("nl2_sb", [128, 1], F32)
    smask_sb = alloc("smask_sb", [128, 8], F32)
    imask = alloc("imask", [128, 512], F32)
    ident_b = alloc("ident_b", [128, 128], BF16)
    BQP, BQM, BKP, BKM, RMS, FQB, FKB = range(7)

    def bc(i):
        return bcol[:, i:i + 1]

    @block.scalar
    def _(scalar):
        s["DA"].inc(scalar.dma_start(out=bc(FQB), in_=fmq_b[:, :]), 16)
        s["DA"].inc(scalar.dma_start(out=bc(FKB), in_=fmk_b[:, :]), 16)
        s["DA"].inc(scalar.dma_start(out=bc(RMS), in_=rms_c[:, :]), 16)
        s["DA"].inc(scalar.dma_start(out=smask_sb[:], in_=smask[:, :]), 16)

    @block.gpsimd
    def _(gpsimd):
        gpsimd.memset(ones_b[:], 1.0)
        gpsimd.memset(eps_sb[:, :], 1e-5)
        gpsimd.memset(eps2_sb[:, :], 1e-5 * 2.0 ** -64)
        gpsimd.memset(nl2_sb[:, :], -32.0 * 0.6931471805599453)
        s["GD"].inc(gpsimd.iota(imask[:], pattern=[[1, 512]],
                                channel_multiplier=-1,
                                allow_small_or_imprecise_dtypes=True))
        s["DG"].inc(gpsimd.dma_start(out=fmq_sb[:], in_=fmq_T[:, :]), 16)
        s["DG"].inc(gpsimd.dma_start(out=fmk_sb[:], in_=fmk_T[:, :]), 16)

    @block.vector
    def _(vector):
        s["GD"].wait(vector)
        s["DA"].wait(vector)
        vector.tensor_scalar(mask4[:], imask[:], 0.0, None, ALU.is_ge)
        vector.tensor_scalar(ident_b[:], imask[:, 0:128], 0.0, None, ALU.is_equal)
        vector.tensor_scalar(bc(BQP), bc(FQB), -LN16, None, ALU.add)
        vector.tensor_scalar(bc(BQM), bc(FQB), -1.0, -LN16, ALU.mult, ALU.add)
        vector.tensor_copy(bc(BKP), bc(FKB))
        s["VD"].inc(vector.tensor_scalar(bc(BKM), bc(FKB), -1.0, None, ALU.mult))

    def _phase0():
        CW = 4 * D
        with (
            nc.sbuf_tensor("x_sb", [128, 2 * CW], F32) as x_sb,
            nc.sbuf_tensor("wsl_sb", [128, 3 * D + FF], F32) as wsl_sb,
            nc.sbuf_tensor("qS", [128, 3 * 320], F32) as qS,
            nc.sbuf_tensor("qB", [128, 4 * 4880], F32) as qB,
            nc.sbuf_tensor("dq_b", [128, 2 * CW], BF16) as dq_b,
            nc.sbuf_tensor("ln_st", [128, 6 * NT], F32) as ln_st,
            nc.sbuf_tensor("xn_f", [128, 2 * CW], F32) as xn_f,
            nc.sbuf_tensor("xn_b", [128, 2 * CW], BF16) as xn_b,
        ):
            QBW = 4880
            qbufs = dict(amax=qS[:, 0:320], sc=qS[:, 320:640],
                         rec=qS[:, 640:960],
                         v=qB[:, 0:QBW], w=qB[:, QBW:2 * QBW],
                         M=qB[:, 2 * QBW:3 * QBW], tq=qB[:, 3 * QBW:4 * QBW])

            ST_SUM, ST_SQ, ST_MEAN, ST_VAR, ST_RSTD, ST_NMR = (
                0, NT, 2 * NT, 3 * NT, 4 * NT, 5 * NT)

            for ck in range(NT // 4):
                cb = ck % 2
                sqs = qB[:, (1 + cb) * CW:(2 + cb) * CW]

                @block.sync
                def _(sync, ck=ck, cb=cb):
                    if ck >= 2:
                        s["VD"].wait(sync)
                    s["DS"].inc(sync.dma_start(
                        out=x_sb[:, cb * CW:(cb + 1) * CW].rearrange(
                            "p (tt d) -> p tt d", d=D),
                        in_=x_in[ck * 512:(ck + 1) * 512, :].rearrange(
                            "(tt p) d -> p tt d", p=128)), 16)

                @block.gpsimd
                def _(gpsimd, ck=ck, cb=cb, sqs=sqs):
                    s["DS"].wait(gpsimd)
                    if ck >= 2:
                        s["VD"].wait(gpsimd)
                    s["GD"].inc(gpsimd.tensor_tensor(
                        sqs, x_sb[:, cb * CW:(cb + 1) * CW],
                        x_sb[:, cb * CW:(cb + 1) * CW], ALU.mult))

                @block.vector
                def _(vector, ck=ck, cb=cb, sqs=sqs):
                    s["DS"].wait(vector)
                    for j in range(4):
                        t = ck * 4 + j
                        vector.tensor_reduce(
                            ln_st[:, ST_SUM + t:ST_SUM + t + 1].unsqueeze(2),
                            x_sb[:, cb * CW + j * D:cb * CW + (j + 1) * D].unsqueeze(1),
                            AX.X, ALU.add)
                    s["GD"].wait(vector)
                    li = None
                    for j in range(4):
                        t = ck * 4 + j
                        li = vector.tensor_reduce(
                            ln_st[:, ST_SQ + t:ST_SQ + t + 1].unsqueeze(2),
                            sqs[:, j * D:(j + 1) * D].unsqueeze(1), AX.X, ALU.add)
                    s["VD"].inc(li)

            @block.vector
            def _(vector):
                s["VD"].wait(vector)
                s["VD"].inc(vector.tensor_scalar(ln_st[:, ST_MEAN:ST_MEAN + NT],
                                                 ln_st[:, ST_SUM:ST_SUM + NT],
                                                 1.0 / D, None, ALU.mult))
                s["VD"].inc(vector.tensor_scalar(ln_st[:, ST_VAR:ST_VAR + NT],
                                                 ln_st[:, ST_SQ:ST_SQ + NT],
                                                 1.0 / D, None, ALU.mult))
                s["VD"].wait(vector)
                s["VD"].inc(vector.tensor_tensor(ln_st[:, ST_SQ:ST_SQ + NT],
                                                 ln_st[:, ST_MEAN:ST_MEAN + NT],
                                                 ln_st[:, ST_MEAN:ST_MEAN + NT], ALU.mult))
                s["VD"].wait(vector)
                s["VD"].inc(vector.tensor_tensor(ln_st[:, ST_VAR:ST_VAR + NT],
                                                 ln_st[:, ST_VAR:ST_VAR + NT],
                                                 ln_st[:, ST_SQ:ST_SQ + NT], ALU.subtract))

            @block.scalar
            def _(scalar):
                s["VD"].wait(scalar)
                scalar.activation(ln_st[:, ST_SUM:ST_SUM + NT], ln_st[:, ST_VAR:ST_VAR + NT],
                                  AF.Ln, bias=eps_sb[:, 0:1])
                s["AD"].inc(scalar.activation(ln_st[:, ST_RSTD:ST_RSTD + NT],
                                              ln_st[:, ST_SUM:ST_SUM + NT],
                                              AF.Exp, scale=-0.5))

            @block.vector
            def _(vector):
                s["AD"].wait(vector)
                s["VD"].inc(vector.scalar_tensor_tensor(
                    ln_st[:, ST_NMR:ST_NMR + NT], ln_st[:, ST_MEAN:ST_MEAN + NT], -1.0,
                    ln_st[:, ST_RSTD:ST_RSTD + NT], ALU.mult, ALU.mult))

            def emit_xnq(cki):
                cb = cki % 2
                _quant2(block, s, qbufs, xn_f[:, cb * CW:(cb + 1) * CW], CW,
                        dq_b[:, cb * CW:(cb + 1) * CW], pre=("VD", "AD", "DS", "DG"))

                @block.sync
                def _(sync, cki=cki, cb=cb):
                    s["VD"].wait(sync)
                    s["DS"].inc(sync.dma_start(
                        out=scr_xnq[cki * 512:(cki + 1) * 512, :].rearrange(
                            "(tt p) d -> p tt d", p=128),
                        in_=dq_b[:, cb * CW:(cb + 1) * CW].rearrange(
                            "p (tt d) -> p tt d", d=D)), 16)

            for cki in range(NT // 4):
                cb = cki % 2

                @block.sync
                def _(sync, cki=cki, cb=cb):
                    s["AD"].wait(sync)
                    s["DS"].inc(sync.dma_start(
                        out=x_sb[:, cb * CW:(cb + 1) * CW].rearrange(
                            "p (tt d) -> p tt d", d=D),
                        in_=x_in[cki * 512:(cki + 1) * 512, :].rearrange(
                            "(tt p) d -> p tt d", p=128)), 16)

                @block.scalar
                def _(scalar, cki=cki, cb=cb):
                    s["DS"].wait(scalar)
                    s["VD"].wait(scalar)
                    for j in range(4):
                        t = cki * 4 + j
                        scalar.activation(xn_f[:, cb * CW + j * D:cb * CW + (j + 1) * D],
                                          x_sb[:, cb * CW + j * D:cb * CW + (j + 1) * D],
                                          AF.Identity,
                                          bias=ln_st[:, ST_NMR + t:ST_NMR + t + 1],
                                          scale=ln_st[:, ST_RSTD + t:ST_RSTD + t + 1])
                    for j in range(4):
                        t = cki * 4 + j
                        li = scalar.activation(
                            xn_b[:, cb * CW + j * D:cb * CW + (j + 1) * D],
                            x_sb[:, cb * CW + j * D:cb * CW + (j + 1) * D],
                            AF.Identity, bias=ln_st[:, ST_NMR + t:ST_NMR + t + 1],
                            scale=ln_st[:, ST_RSTD + t:ST_RSTD + t + 1])
                    s["AD"].inc(li)

                @block.sync
                def _(sync, cki=cki, cb=cb):
                    s["AD"].wait(sync)
                    s["DS"].inc(sync.dma_start(
                        out=scr_xn[cki * 512:(cki + 1) * 512, :].rearrange(
                            "(tt p) d -> p tt d", p=128),
                        in_=xn_b[:, cb * CW:(cb + 1) * CW].rearrange(
                            "p (tt d) -> p tt d", d=D)), 16)

                if cki >= 1:
                    emit_xnq(cki - 1)

            emit_xnq(NT // 4 - 1)

            @block.scalar
            def _(scalar):
                s["DA"].inc(scalar.dma_start(out=wsl_sb[:, 0:D], in_=wg_sl[0:128, :]), 16)
                s["DA"].inc(scalar.dma_start(out=wsl_sb[:, D:2 * D], in_=wu_sl[0:128, :]), 16)
                s["DA"].inc(scalar.dma_start(out=wsl_sb[:, 2 * D:3 * D],
                                             in_=wd_sl[0:128, 0:D]), 16)
                s["DA"].inc(scalar.dma_start(out=wsl_sb[:, 3 * D:3 * D + FF - D],
                                             in_=wd_sl[0:128, D:FF]), 16)

            @block.vector
            def _(vector):
                s["DA"].wait(vector)

            _quant2(block, s, qbufs, wsl_sb[:, 0:2 * D + FF], 2 * D + FF,
                    dq_b[:, 0:2 * D + FF])

            @block.gpsimd
            def _(gpsimd):
                s["VD"].wait(gpsimd)
                s["DG"].inc(gpsimd.dma_start(out=cc_wg_i[0:128, :],
                                             in_=dq_b[:, 0:D]), 16)
                s["DG"].inc(gpsimd.dma_start(out=cc_wu_i[0:128, :],
                                             in_=dq_b[:, D:2 * D]), 16)
                s["DG"].inc(gpsimd.dma_start(out=cc_wd_i[0:128, :],
                                             in_=dq_b[:, 2 * D:2 * D + FF]), 16)

            @block.scalar
            def _(scalar):
                s["VD"].wait(scalar)
                s["DG"].wait(scalar)
                s["DA"].wait(scalar)
                s["DA"].inc(scalar.dma_start(out=wsl_sb[:, 0:D], in_=wg_sl[128:256, :]), 16)
                s["DA"].inc(scalar.dma_start(out=wsl_sb[:, D:2 * D],
                                             in_=wu_sl[128:256, :]), 16)
                s["DA"].inc(scalar.dma_start(out=wsl_sb[0:96, 2 * D:3 * D],
                                             in_=wg_sl[256:352, :]), 16)
                s["DA"].inc(scalar.dma_start(out=wsl_sb[0:96, 3 * D:4 * D],
                                             in_=wu_sl[256:352, :]), 16)

            @block.vector
            def _(vector):
                s["DA"].wait(vector)

            _quant2(block, s, qbufs, wsl_sb[:, 0:2 * D], 2 * D, dq_b[:, 0:2 * D])

            @block.gpsimd
            def _(gpsimd):
                s["VD"].wait(gpsimd)
                s["DG"].inc(gpsimd.dma_start(out=cc_wg_i[128:256, :],
                                             in_=dq_b[:, 0:D]), 16)
                s["DG"].inc(gpsimd.dma_start(out=cc_wu_i[128:256, :],
                                             in_=dq_b[:, D:2 * D]), 16)

            _quant2(block, s, qbufs, wsl_sb[0:96, 2 * D:4 * D], 2 * D,
                    dq_b[0:96, 2 * D:4 * D])

            @block.gpsimd
            def _(gpsimd):
                s["VD"].wait(gpsimd)
                s["DG"].inc(gpsimd.dma_start(out=cc_wg_i[256:352, :],
                                             in_=dq_b[0:96, 2 * D:3 * D]), 16)
                s["DG"].inc(gpsimd.dma_start(out=cc_wu_i[256:352, :],
                                             in_=dq_b[0:96, 3 * D:4 * D]), 16)

            @block.gpsimd
            def _(gpsimd):
                s["DG"].wait(gpsimd)
                for ci, co in ((cc_wg_i, cc_wg_o), (cc_wu_i, cc_wu_o), (cc_wd_i, cc_wd_o)):
                    s["CC"].inc(gpsimd.collective_compute(
                        "AllGather", ALU.bypass, replica_groups=[list(range(8))],
                        ins=[ci[:, :].opt()], outs=[co[:, :].opt()]))


            @block.sync
            def _(sync):
                s["DS"].wait(sync)

            _barrier(block, s)

    _phase0()

    scr_o = nc.dram_tensor("scr_o", [128, H * T], BF16, kind="Internal").ap()

    def _phase2():
        with (
            nc.sbuf_tensor("xnT", [128, KD * T], BF16) as xnT,
            nc.sbuf_tensor("v_sb", [128, NT * D], BF16) as v_sb,
            nc.sbuf_tensor("wv_sb", [128, KD * D], BF16) as wv_sb,
            nc.sbuf_tensor("wqk", [128, 4 * D], BF16) as wqk,
            nc.sbuf_tensor("qkraw", [128, 2 * T], BF16) as qkraw,
            nc.sbuf_tensor("feat", [128, 4 * T], BF16) as feat,
            nc.sbuf_tensor("kfT", [128, NCH * DF], BF16) as kfT,
            nc.sbuf_tensor("attnT", [128, NCH * 128], BF16) as attnT,
            nc.sbuf_tensor("Sx", [128, DF + H * DF], BF16) as Sx,
            nc.sbuf_tensor("o_full", [128, H * T], BF16) as o_full,
            nc.sbuf_tensor("echo2", [128, 1], F32) as echo2,
            nc.psum_tensor("pp", [128, 3072], F32) as pp,
            nc.psum_tensor("pt", [128, 2048], BF16) as pt,
        ):
            wqk_r = ((wqk[:, 0:D], wqk[:, D:2 * D]),
                     (wqk[:, 2 * D:3 * D], wqk[:, 3 * D:4 * D]))
            qraw, kraw = qkraw[:, 0:T], qkraw[:, T:2 * T]
            qfp, qfm = feat[:, 0:T], feat[:, T:2 * T]
            kfp, kfm = feat[:, 2 * T:3 * T], feat[:, 3 * T:4 * T]
            S_sb = Sx[:, 0:DF]
            Sstage = Sx[:, DF:DF + H * DF]
            pv = (pp[:, 0:1024], pp[:, 1024:2048], pp[:, 2048:3072])
            pS = pp[:, 0:512]
            pqy = (pp[:, 1024:1536], pp[:, 1536:2048])
            pA = (pp[:, 2048:2560], pp[:, 2560:3072])
            pO = pp[:, 512:1024]

            @block.sync
            def _(sync):
                for k in range(KD):
                    s["DS"].inc(sync.dma_start(out=xnT[:, k * T:(k + 1) * T],
                                               in_=scr_xn[:, k * 128:(k + 1) * 128],
                                               transpose=True), 16)

            @block.gpsimd
            def _(gpsimd):
                for k in range(KD):
                    s["DG"].inc(gpsimd.dma_start(out=wv_sb[:, k * D:(k + 1) * D],
                                                 in_=wv_T[k * 128:(k + 1) * 128, :]), 16)

            vm2 = []
            for t in range(NT):
                pvb = t % 2

                @block.tensor
                def _(tensor, t=t, pvb=pvb):
                    if t == 0:
                        s["DS"].wait(tensor)
                        s["DG"].wait(tensor)
                    if t >= 1:
                        s["EV"].wait_n(tensor, vm2[t - 1])
                    for k in range(KD):
                        lhs = xnT[:, k * T + t * 128:k * T + (t + 1) * 128]
                        tensor.matmul(pv[pvb][:, 0:512], lhs, wv_sb[:, k * D:k * D + 512],
                                      start=(k == 0), stop=(k == KD - 1),
                                      skip_group_check=True)
                        mi = tensor.matmul(pv[pvb][:, 512:1024], lhs,
                                           wv_sb[:, k * D + 512:(k + 1) * D],
                                           start=(k == 0), stop=(k == KD - 1),
                                           skip_group_check=True)
                    s["PD"].inc(mi)

                @block.vector
                def _(vector, t=t, pvb=pvb):
                    s["PD"].wait(vector)
                    s["EV"].inc(vector.tensor_copy(v_sb[:, t * D:(t + 1) * D],
                                                   pv[pvb][:, :]))
                vm2.append(s["EV"].mark())

            pqm = [None] * (16 * H)
            emk = [None] * H
            dskm = [None] * NCH
            aam_all = []
            om_all = []
            scm_all = []
            ktm_all = []
            qrm = []
            for h in range(H):
                wqm_sb, wkm_sb = wqk_r[h % 2]

                @block.gpsimd
                def _(gpsimd, h=h):
                    if h >= 2:
                        s["PD"].wait_n(gpsimd, qrm[h - 1])
                    s["DG"].inc(gpsimd.dma_start(
                        out=wqk_r[h % 2][0].rearrange("p (k c) -> p k c", c=128),
                        in_=wq_t[h].rearrange("(k p) c -> p k c", p=128)), 16)
                    s["DG"].inc(gpsimd.dma_start(
                        out=wqk_r[h % 2][1].rearrange("p (k c) -> p k c", c=128),
                        in_=wk_t[h].rearrange("(k p) c -> p k c", p=128)), 16)

                for wi, (which, wm, rawbuf) in enumerate(
                        (("q", wqm_sb, qraw), ("k", wkm_sb, kraw))):
                    for tc in range(4):
                        pb = tc % 2
                        G2 = 16 * h + wi * 4 + tc

                        @block.tensor
                        def _(tensor, h=h, which=which, wm=wm, tc=tc, pb=pb, G2=G2):
                            if which == "q" and tc == 0:
                                s["DG"].wait(tensor)
                                if h > 0:
                                    s["EV"].wait_n(tensor, ktm_all[-1])
                            if G2 < 1:
                                s["EV"].wait(tensor)
                            else:
                                s["EV"].wait_n(tensor, pqm[G2 - 1])
                            for k in range(KD):
                                mi = tensor.matmul(
                                    pqy[pb], wm[:, k * 128:(k + 1) * 128],
                                    xnT[:, k * T + tc * 512:k * T + (tc + 1) * 512],
                                    start=(k == 0), stop=(k == KD - 1),
                                    skip_group_check=True)
                            s["PD"].inc(mi)

                        @block.vector
                        def _(vector, rawbuf=rawbuf, tc=tc, pb=pb):
                            s["PD"].wait(vector)
                            s["EV"].inc(vector.tensor_copy(
                                rawbuf[:, tc * 512:(tc + 1) * 512], pqy[pb]))
                        pqm[G2] = s["EV"].mark()
                qrm.append(s["PD"].mark())

                for wi, (which, rawbuf, fT, bpi, bmi, fp_, fm_) in enumerate((
                        ("q", qraw, fmq_sb, BQP, BQM, qfp, qfm),
                        ("k", kraw, fmk_sb, BKP, BKM, kfp, kfm))):
                    for tc in range(4):
                        pb = tc % 2
                        G2 = 16 * h + 8 + wi * 4 + tc

                        @block.tensor
                        def _(tensor, which=which, rawbuf=rawbuf, fT=fT, tc=tc,
                              pb=pb, G2=G2):
                            s["EV"].wait_n(tensor, pqm[G2 - 1])
                            if which == "q" and tc == 0:
                                s["DSQ"].wait(tensor)
                                s["DSK"].wait(tensor)
                            mi = tensor.matmul(pqy[pb], fT[:, :],
                                               rawbuf[:, tc * 512:(tc + 1) * 512],
                                               start=True, stop=True,
                                               skip_group_check=True)
                            s["PD"].inc(mi)

                        @block.scalar
                        def _(scalar, tc=tc, pb=pb, bpi=bpi, bmi=bmi, fp_=fp_, fm_=fm_):
                            s["PD"].wait(scalar)
                            scalar.activation(fp_[:, tc * 512:(tc + 1) * 512], pqy[pb],
                                              AF.Exp, bias=bc(bpi), scale=1.0)
                            s["EV"].inc(scalar.activation(
                                fm_[:, tc * 512:(tc + 1) * 512], pqy[pb],
                                AF.Exp, bias=bc(bmi), scale=-1.0))
                        pqm[G2] = s["EV"].mark()

                @block.scalar
                def _(scalar):
                    s["EV"].inc(scalar.activation(echo2[:, :], ones_b[:, 0:1], AF.Exp))
                emk[h] = s["EV"].mark()

                @block.sync
                def _(sync, h=h):
                    s["EV"].wait(sync)
                    s["PD"].wait(sync)
                    s["DSQ"].inc(sync.dma_start(out=scr_qf[h * DF:h * DF + 128, :],
                                                in_=qfp), 16)
                    s["DSQ"].inc(sync.dma_start(out=scr_qf[h * DF + 128:(h + 1) * DF, :],
                                                in_=qfm), 16)

                for b in range(4):
                    pob = 0
                    for i in range(4):
                        c = 4 * b + i
                        nq = (4 - i) * 128
                        q0 = (4 * b + i) * 128

                        @block.tensor
                        def _(tensor, h=h, b=b, i=i, c=c, nq=nq, q0=q0):
                            ai = len(aam_all)
                            if ai >= 2:
                                s["EV"].wait_n(tensor, aam_all[ai - 2])
                            if b == 0 and i == 0:
                                s["EV"].wait_n(tensor, emk[h])
                            pb = i % 2
                            tensor.matmul(pA[pb][:, 0:nq], kfp[:, c * 128:(c + 1) * 128],
                                          qfp[:, q0:q0 + nq], start=True, stop=False,
                                          skip_group_check=True)
                            mi = tensor.matmul(pA[pb][:, 0:nq],
                                               kfm[:, c * 128:(c + 1) * 128],
                                               qfm[:, q0:q0 + nq], start=False,
                                               stop=True, skip_group_check=True)
                            s["PD"].inc(mi)

                        @block.vector
                        def _(vector, i=i, nq=nq):
                            s["PD"].wait(vector)
                            pb = i % 2
                            s["EV"].inc(vector.tensor_tensor(
                                attnT[:, i * 512:i * 512 + nq],
                                pA[pb][:, 0:nq], mask4[:, 0:nq], ALU.mult))
                        aam_all.append(s["EV"].mark())

                    for i in range(4):
                        c = 4 * b + i
                        pt0 = (c % 2) * 1024

                        @block.tensor
                        def _(tensor, h=h, c=c, pt0=pt0):
                            ki = len(ktm_all)
                            if ki >= 2:
                                s["EV"].wait_n(tensor, ktm_all[ki - 2])
                            tensor.matmul(pt[:, pt0:pt0 + 128],
                                          kfp[:, c * 128:(c + 1) * 128], ident_b[:, :],
                                          start=True, stop=False, is_transpose=True,
                                          skip_group_check=True)
                            mi = tensor.matmul(pt[:, pt0 + 128:pt0 + 256],
                                               kfm[:, c * 128:(c + 1) * 128],
                                               ident_b[:, :],
                                               start=False, stop=True,
                                               is_transpose=True,
                                               skip_group_check=True)
                            s["PD"].inc(mi)

                        @block.vector
                        def _(vector, c=c, pt0=pt0):
                            s["PD"].wait(vector)
                            s["EV"].inc(vector.tensor_copy(
                                kfT[:, c * DF:(c + 1) * DF], pt[:, pt0:pt0 + 256]))
                        ktm_all.append(s["EV"].mark())

                    @block.tensor
                    def _(tensor, h=h, b=b, pob=pob):
                        s["EV"].wait_n(tensor, aam_all[-1])
                        if b >= 1:
                            s["EV"].wait_n(tensor, om_all[-1])
                        elif h > 0:
                            s["EV"].wait_n(tensor, om_all[-1])
                        for i in range(4):
                            c = 4 * b + i
                            nq = (4 - i) * 128
                            vsl = v_sb[:, c * D + h * 128:c * D + (h + 1) * 128]
                            mi = tensor.matmul(pO[:, i * 128:512], vsl,
                                               attnT[:, i * 512:i * 512 + nq],
                                               start=(i == 0),
                                               stop=(b == 0 and i == 3),
                                               skip_group_check=True)
                        if b > 0:
                            tensor.matmul(pO[:, 0:512], S_sb[:, 0:128],
                                          qfp[:, b * 512:(b + 1) * 512],
                                          start=False, stop=False,
                                          skip_group_check=True)
                            mi = tensor.matmul(pO[:, 0:512], S_sb[:, 128:256],
                                               qfm[:, b * 512:(b + 1) * 512],
                                               start=False, stop=True,
                                               skip_group_check=True)
                        s["EV"].wait_n(tensor, ktm_all[-1])
                        if h > 0 and b == 0:
                            s["EV"].wait_n(tensor, scm_all[-1])
                        for i in range(4):
                            c = 4 * b + i
                            vsl = v_sb[:, c * D + h * 128:c * D + (h + 1) * 128]
                            tensor.matmul(pS[:, 0:128], kfT[:, c * DF:c * DF + 128],
                                          vsl, start=(b == 0 and i == 0), stop=False,
                                          skip_group_check=True)
                            mi = tensor.matmul(pS[:, 128:256],
                                               kfT[:, c * DF + 128:c * DF + 256], vsl,
                                               start=False,
                                               stop=(b == 3 and i == 3),
                                               skip_group_check=True)
                        s["PD"].inc(mi)

                    @block.vector
                    def _(vector, h=h, b=b):
                        s["PD"].wait(vector)
                        s["EV"].inc(vector.tensor_copy(
                            o_full[:, h * T + b * 512:h * T + (b + 1) * 512],
                            pO[:, 0:512]))
                        om_all.append(s["EV"].mark())
                        if b < 3:
                            s["EV"].inc(vector.tensor_copy(S_sb[:, :], pS[:, 0:256]))
                        else:
                            s["EV"].inc(vector.tensor_copy(
                                Sstage[:, h * DF:(h + 1) * DF], pS[:, 0:256]))
                        scm_all.append(s["EV"].mark())

            @block.sync
            def _(sync):
                s["EV"].wait(sync)
                s["DSS"].inc(sync.dma_start(out=cc_S_i[:, :], in_=Sstage), 16)
                s["DS"].inc(sync.dma_start(out=scr_o[:, :], in_=o_full[:, :]), 16)
                s["DSQ"].wait(sync)
                s["DSK"].wait(sync)
                s["DG"].wait(sync)
                s["DS"].wait(sync)

            @block.gpsimd
            def _(gpsimd):
                s["DSS"].wait(gpsimd)
                s["CC"].inc(gpsimd.collective_compute(
                    "AllGather", ALU.bypass, replica_groups=[list(range(8))],
                    ins=[cc_S_i[:, :].opt()], outs=[cc_S_o[:, :].opt()]))

            _barrier(block, s)

    _phase2()

    def _phase1():
        NSL = FF // 256
        with (
            nc.sbuf_tensor("xnqT", [128, KD * T], BF16) as xnqT,
            nc.sbuf_tensor("wgT", [128, KD * FF], BF16) as wgT,
            nc.sbuf_tensor("wuT", [128, KD * FF], BF16) as wuT,
            nc.sbuf_tensor("sg", [128, 4 * 256], F32) as sg,
            nc.sbuf_tensor("h_sb", [128, 2 * FF], F32) as h_sb,
            nc.sbuf_tensor("fS", [128, 4 * (FF // 16)], F32) as fS,
            nc.sbuf_tensor("fB", [128, 3 * FF], F32) as fB,
            nc.sbuf_tensor("fW", [128, FF], BF16) as fW,
            nc.sbuf_tensor("hq_b", [128, 2 * FF], BF16) as hq_b,
            nc.psum_tensor("pf", [128, 4096], F32) as pf,
        ):
            NB = FF // 16
            fqb = dict(amax=fS[:, 0:NB], sc=fS[:, NB:2 * NB],
                       rec=fS[:, 2 * NB:3 * NB],
                       v=fB[:, 0:FF], w=fW[:, 0:FF],
                       M=fB[:, FF:2 * FF], tq=fB[:, 2 * FF:3 * FF])

            @block.sync
            def _(sync):
                for k in range(KD):
                    s["DS"].inc(sync.dma_start(out=xnqT[:, k * T:(k + 1) * T],
                                               in_=scr_xnq[:, k * 128:(k + 1) * 128],
                                               transpose=True), 16)
                sync.wait_ge(s["CC"].h, 1)
                for k in range(KD):
                    s["DS"].inc(sync.dma_start(out=wgT[:, k * FF:(k + 1) * FF],
                                               in_=cc_wg_o[:, k * 128:(k + 1) * 128],
                                               transpose=True), 16)
                sync.wait_ge(s["CC"].h, 2)
                for k in range(KD):
                    s["DS"].inc(sync.dma_start(out=wuT[:, k * FF:(k + 1) * FF],
                                               in_=cc_wu_o[:, k * 128:(k + 1) * 128],
                                               transpose=True), 16)

            sim1 = []
            hmm1 = []
            dsm1 = {}
            vdm1 = {}

            def q_p1(tt):
                hb = tt % 2
                srcq = h_sb[:, hb * FF:(hb + 1) * FF]

                @block.vector
                def _(vector):
                    s["VD"].wait(vector)
                    src3 = srcq.rearrange("p (nb k) -> p nb k", k=16)
                    s["VD"].inc(vector.tensor_reduce(
                        fqb["amax"][:, 0:NB].unsqueeze(2), src3, AX.X, ALU.max,
                        apply_absolute_value=True))
                    s["VD"].wait(vector)
                    s["VD"].inc(vector.tensor_scalar(
                        fqb["sc"][:, 0:NB], fqb["amax"][:, 0:NB],
                        1.0 / 6.0, 1e-12, ALU.mult, ALU.max))
                    s["VD"].wait(vector)
                    s["VD"].inc(vector.reciprocal(fqb["rec"][:, 0:NB],
                                                  fqb["sc"][:, 0:NB]))
                    s["VD"].wait(vector)
                    rbc = fqb["rec"][:, 0:NB].unsqueeze(2).broadcast_to([128, NB, 16])
                    s["VD"].inc(vector.tensor_tensor(
                        fqb["v"][:, 0:FF].rearrange("p (nb k) -> p nb k", k=16),
                        src3, rbc, ALU.mult))

                @block.gpsimd
                def _(gpsimd):
                    s["VD"].wait(gpsimd)
                    s["GD"].inc(gpsimd.tensor_tensor(
                        fqb["w"][:, 0:FF], fqb["v"][:, 0:FF], fqb["v"][:, 0:FF],
                        ALU.mult))

            def q_p2(tt):
                gm = s["GD"].mark()

                @block.vector
                def _(vector):
                    s["GD"].wait_n(vector, gm)
                    s["VD"].inc(vector.tensor_scalar(
                        fqb["M"][:, 0:FF], fqb["w"][:, 0:FF], QC1, QC0,
                        ALU.mult, ALU.add))

                @block.gpsimd
                def _(gpsimd):
                    s["VD"].wait(gpsimd)
                    s["GD"].inc(gpsimd.tensor_tensor(
                        fqb["tq"][:, 0:FF], fqb["v"][:, 0:FF], fqb["M"][:, 0:FF],
                        ALU.add))

            def q_p3(tt):
                hb = tt % 2
                gm = s["GD"].mark()

                @block.vector
                def _(vector, tt=tt, hb=hb):
                    s["GD"].wait_n(vector, gm)
                    if tt >= 2:
                        s["DS"].wait_n(vector, dsm1[tt - 1])
                    s["VD"].inc(vector.tensor_tensor(
                        fqb["tq"][:, 0:FF], fqb["tq"][:, 0:FF], fqb["M"][:, 0:FF],
                        ALU.subtract))
                    s["VD"].wait(vector)
                    sbc = fqb["sc"][:, 0:NB].unsqueeze(2).broadcast_to([128, NB, 16])
                    s["VD"].inc(vector.tensor_tensor(
                        hq_b[:, hb * FF:(hb + 1) * FF].rearrange(
                            "p (nb k) -> p nb k", k=16),
                        fqb["tq"][:, 0:FF].rearrange("p (nb k) -> p nb k", k=16),
                        sbc, ALU.mult))
                vdm1[tt] = s["VD"].mark()

            def q_stash(tt):
                hb = tt % 2

                @block.sync
                def _(sync, tt=tt, hb=hb):
                    s["VD"].wait_n(sync, vdm1[tt])
                    s["DS"].inc(sync.dma_start(out=scr_h[tt * 128:(tt + 1) * 128, :],
                                               in_=hq_b[:, hb * FF:(hb + 1) * FF]), 16)
                dsm1[tt] = s["DS"].mark()

            for t in range(NT):
                hb = t % 2
                for sl in range(NSL):
                    S1 = t * NSL + sl
                    bk = S1 % 8
                    rb = S1 % 4
                    c0 = bk * 512

                    @block.tensor
                    def _(tensor, t=t, sl=sl, S1=S1, c0=c0):
                        if S1 == 0:
                            s["DS"].wait(tensor)
                        if S1 >= 8:
                            s["EV"].wait_n(tensor, sim1[S1 - 7])
                            s["VD"].wait_n(tensor, hmm1[S1 - 8])
                        n0 = sl * 256
                        for k in range(KD):
                            lhs = xnqT[:, k * T + t * 128:k * T + (t + 1) * 128]
                            tensor.matmul(pf[:, c0:c0 + 256], lhs,
                                          wgT[:, k * FF + n0:k * FF + n0 + 256],
                                          start=(k == 0), stop=False,
                                          skip_group_check=True)
                            mi = tensor.matmul(pf[:, c0 + 256:c0 + 512], lhs,
                                               wuT[:, k * FF + n0:k * FF + n0 + 256],
                                               start=False, stop=(k == KD - 1),
                                               skip_group_check=True)
                        s["PD"].inc(mi)

                    @block.scalar
                    def _(scalar, S1=S1, rb=rb, c0=c0):
                        s["PD"].wait(scalar)
                        if S1 >= 4:
                            s["VD"].wait_n(scalar, hmm1[S1 - 4])
                        s["EV"].inc(scalar.activation(sg[:, rb * 256:(rb + 1) * 256],
                                                      pf[:, c0:c0 + 256], AF.Silu))
                    sim1.append(s["EV"].mark())

                    @block.vector
                    def _(vector, t=t, sl=sl, S1=S1, rb=rb, c0=c0, hb=hb):
                        s["PD"].wait(vector)
                        s["EV"].wait_n(vector, sim1[S1])
                        s["VD"].inc(vector.tensor_tensor(
                            h_sb[:, hb * FF + sl * 256:hb * FF + (sl + 1) * 256],
                            pf[:, c0 + 256:c0 + 512],
                            sg[:, rb * 256:(rb + 1) * 256], ALU.mult))
                    hmm1.append(s["VD"].mark())

                    if t >= 1:
                        if sl == 1:
                            q_p1(t - 1)
                        elif sl == 4:
                            q_p2(t - 1)
                        elif sl == 8:
                            q_p3(t - 1)
                        elif sl == 9:
                            q_stash(t - 1)

            q_p1(NT - 1)
            q_p2(NT - 1)
            q_p3(NT - 1)
            q_stash(NT - 1)

            @block.sync
            def _(sync):
                s["DS"].wait(sync)

            _barrier(block, s)

    _phase1()


    g_sb = stack.enter_context(nc.sbuf_tensor("g_sb", [128, NT * D], BF16))
    Sg = stack.enter_context(nc.sbuf_tensor("Sg", [128, 2 * H * DF], BF16))
    S_pre = stack.enter_context(nc.sbuf_tensor("S_pre", [128, H * DF], BF16))
    woT = stack.enter_context(nc.sbuf_tensor("woT", [128, KD * D], BF16))

    def _phase4_prework():
        for r in range(8):
            rb = r % 2

            @block.sync
            def _(sync, r=r, rb=rb):
                if r == 0:
                    sync.wait_ge(s["CC"].h, 4)
                if r >= 2:
                    s["VD"].wait(sync)
                s["DSS"].inc(sync.dma_start(
                    out=Sg[:, rb * H * DF:(rb + 1) * H * DF],
                    in_=cc_S_o[r * 128:(r + 1) * 128, :]), 16)

            @block.vector
            def _(vector, r=r, rb=rb):
                s["DSS"].wait(vector)
                s["VD"].wait(vector)
                if r == 0:
                    li = vector.tensor_scalar(S_pre[:, :], Sg[:, 0:H * DF],
                                              smask_sb[:, 0:1], None, ALU.mult)
                else:
                    li = vector.scalar_tensor_tensor(
                        S_pre[:, :], Sg[:, rb * H * DF:(rb + 1) * H * DF],
                        smask_sb[:, r:r + 1], S_pre[:, :], ALU.mult, ALU.add)
                s["VD"].inc(li)

        @block.gpsimd
        def _(gpsimd):
            for k in range(KD):
                s["DG"].inc(gpsimd.dma_start(out=woT[:, k * D:(k + 1) * D],
                                             in_=wo_T[k * 128:(k + 1) * 128, :]), 16)

        @block.vector
        def _(vector):
            s["DG"].wait(vector)
            li = None
            for k in range(KD):
                li = vector.tensor_scalar(woT[:, k * D:(k + 1) * D],
                                          woT[:, k * D:(k + 1) * D],
                                          bc(RMS), None, ALU.mult)
            s["VD"].inc(li)

    def _phase3():
        with (
            nc.sbuf_tensor("wdT", [128, NFK * D], BF16) as wdT,
            nc.sbuf_tensor("hqT", [128, 2 * NFK * 512], BF16) as hqT,
            nc.psum_tensor("pD", [128, 3072], F32) as pDm,
        ):
            pD = (pDm[:, 0:1024], pDm[:, 1024:2048], pDm[:, 2048:3072])

            @block.sync
            def _(sync):
                sync.wait_ge(s["CC"].h, 3)
                for n in range(NFK):
                    s["DS"].inc(sync.dma_start(out=wdT[:, n * D:(n + 1) * D],
                                               in_=cc_wd_o[:, n * 128:(n + 1) * 128],
                                               transpose=True), 16)

            p3m = []
            pdm3 = []
            for tb in range(4):
                hb = tb % 2

                @block.sync
                def _(sync, tb=tb, hb=hb):
                    if tb >= 2:
                        s["PD"].wait_n(sync, pdm3[tb - 2])
                    for n in range(NFK):
                        s["DA"].inc(sync.dma_start(
                            out=hqT[:, (hb * NFK + n) * 512:(hb * NFK + n + 1) * 512],
                            in_=scr_h[tb * 512:(tb + 1) * 512, n * 128:(n + 1) * 128],
                            transpose=True), 16)

                for ti in range(4):
                    t = tb * 4 + ti
                    pb = t % 3

                    @block.tensor
                    def _(tensor, hb=hb, ti=ti, t=t, pb=pb):
                        if t == 0:
                            s["DS"].wait(tensor)
                        if ti == 0:
                            s["DA"].wait(tensor)
                        if t >= 2:
                            s["EV"].wait_n(tensor, p3m[t - 2])
                        for n in range(NFK):
                            lhs = hqT[:, (hb * NFK + n) * 512 + ti * 128:
                                      (hb * NFK + n) * 512 + (ti + 1) * 128]
                            tensor.matmul(pD[pb][:, 0:512], lhs, wdT[:, n * D:n * D + 512],
                                          start=(n == 0), stop=(n == NFK - 1),
                                          skip_group_check=True)
                            mi = tensor.matmul(pD[pb][:, 512:1024], lhs,
                                               wdT[:, n * D + 512:(n + 1) * D],
                                               start=(n == 0), stop=(n == NFK - 1),
                                               skip_group_check=True)
                        s["PD"].inc(mi)

                    @block.scalar
                    def _(scalar, t=t, pb=pb):
                        s["PD"].wait(scalar)
                        s["EV"].inc(scalar.activation(g_sb[:, t * D:(t + 1) * D],
                                                      pD[pb][:, :], AF.Silu))
                    p3m.append(s["EV"].mark())
                pdm3.append(s["PD"].mark())

            _barrier(block, s)

    _phase4_prework()
    _phase3()

    s_sb = stack.enter_context(nc.sbuf_tensor("s_sb", [128, NT * D], BF16))

    def _phase4():
        with (
            nc.sbuf_tensor("qf2", [128, 2 * T], BF16) as qf2,
            nc.sbuf_tensor("o2", [128, T], BF16) as o2,
            nc.sbuf_tensor("lsum", [128, T], F32) as lsum,
            nc.sbuf_tensor("rstd", [128, T], BF16) as rstd,
            nc.sbuf_tensor("o_nrm", [128, H * T], BF16) as o_nrm,
            nc.sbuf_tensor("o_full4", [128, H * T], BF16) as o_full,
            nc.sbuf_tensor("x2", [128, 2 * D], F32) as x2,
            nc.sbuf_tensor("sgp", [128, 2 * D], F32) as sgp,
            nc.psum_tensor("pX", [128, 4096], F32) as pX,
        ):
            qfp2, qfm2 = qf2[:, 0:T], qf2[:, T:2 * T]
            pC = (pX[:, 0:512], pX[:, 512:1024])
            pR = (pX[:, 1024:1536], pX[:, 1536:2048])
            pW = (pX[:, 2048:3072], pX[:, 3072:4096])

            @block.sync
            def _(sync):
                s["DS"].inc(sync.dma_start(out=o_full[:, :], in_=scr_o[:, :]), 16)

            cpm = []
            for h in range(H):
                @block.sync
                def _(sync, h=h):
                    if h >= 1:
                        s["PD"].wait(sync)
                    s["DS"].inc(sync.dma_start(out=qfp2,
                                               in_=scr_qf[h * DF:h * DF + 128, :]), 16)
                    s["DS"].inc(sync.dma_start(out=qfm2,
                                               in_=scr_qf[h * DF + 128:(h + 1) * DF, :]), 16)

                for tc in range(4):
                    pb = tc % 2

                    @block.tensor
                    def _(tensor, h=h, tc=tc, pb=pb):
                        if tc == 0:
                            s["DS"].wait(tensor)
                            if h == 0:
                                s["VD"].wait(tensor)
                        if h * 4 + tc >= 2:
                            s["EV"].wait_n(tensor, cpm[h * 4 + tc - 2])
                        tensor.matmul(pC[pb], S_pre[:, h * DF:h * DF + 128],
                                      qfp2[:, tc * 512:(tc + 1) * 512],
                                      start=True, stop=False, skip_group_check=True)
                        mi = tensor.matmul(pC[pb], S_pre[:, h * DF + 128:(h + 1) * DF],
                                           qfm2[:, tc * 512:(tc + 1) * 512],
                                           start=False, stop=True, skip_group_check=True)
                        s["PD"].inc(mi)

                    @block.vector
                    def _(vector, h=h, tc=tc, pb=pb):
                        s["PD"].wait(vector)
                        if h == 0 and tc == 0:
                            s["DS"].wait(vector)
                        sl = slice(h * T + tc * 512, h * T + (tc + 1) * 512)
                        s["EV"].inc(vector.tensor_tensor(o_full[:, sl], o_full[:, sl],
                                                         pC[pb], ALU.add))
                    cpm.append(s["EV"].mark())

                @block.vector
                def _(vector, h=h):
                    s["EV"].wait(vector)
                    s["PD"].wait(vector)
                    s["EV"].inc(vector.scalar_tensor_tensor(
                        o2[:, :], o_full[:, h * T:(h + 1) * T], 2.0 ** -16,
                        o_full[:, h * T:(h + 1) * T], ALU.mult, ALU.mult))

                for tc in range(4):
                    pb = tc % 2

                    @block.tensor
                    def _(tensor, tc=tc, pb=pb):
                        s["EV"].wait(tensor)
                        mi = tensor.matmul(pR[pb], ones_b[:, :],
                                           o2[:, tc * 512:(tc + 1) * 512],
                                           start=True, stop=True, skip_group_check=True)
                        s["PD"].inc(mi)

                    @block.scalar
                    def _(scalar, tc=tc, pb=pb):
                        s["PD"].wait(scalar)
                        s["EV"].inc(scalar.activation(lsum[:, tc * 512:(tc + 1) * 512],
                                                      pR[pb], AF.Ln, bias=eps2_sb[:, 0:1],
                                                      scale=2.0 ** -48 / DV))

                @block.scalar
                def _(scalar, h=h):
                    li = None
                    for tc in range(4):
                        li = scalar.activation(rstd[:, tc * 512:(tc + 1) * 512],
                                               lsum[:, tc * 512:(tc + 1) * 512],
                                               AF.Exp, scale=-0.5, bias=nl2_sb[:, 0:1])
                    s["EV"].inc(li)

                @block.vector
                def _(vector, h=h):
                    s["EV"].wait(vector)
                    s["EV"].inc(vector.tensor_tensor(o_nrm[:, h * T:(h + 1) * T],
                                                     o_full[:, h * T:(h + 1) * T],
                                                     rstd[:, :], ALU.mult))

            wom = []
            vm5 = []
            ssm5 = []

            def emit_y(t):
                pb5 = t % 2

                @block.sync
                def _(sync, t=t, pb5=pb5):
                    if t >= 2:
                        s["VD"].wait_n(sync, vm5[t - 2])
                    s["DS"].inc(sync.dma_start(out=x2[:, pb5 * D:(pb5 + 1) * D],
                                               in_=x_in[t * 128:(t + 1) * 128, :]), 16)

                @block.vector
                def _(vector, t=t, pb5=pb5):
                    s["DS"].wait(vector)
                    s["EV"].wait_n(vector, wom[t])
                    if t >= 2:
                        s["DSS"].wait_n(vector, ssm5[t - 2])
                    s["VD"].inc(vector.tensor_tensor(sgp[:, pb5 * D:(pb5 + 1) * D],
                                                     s_sb[:, t * D:(t + 1) * D],
                                                     g_sb[:, t * D:(t + 1) * D],
                                                     ALU.mult))
                    s["VD"].wait(vector)
                    s["VD"].inc(vector.tensor_tensor(sgp[:, pb5 * D:(pb5 + 1) * D],
                                                     x2[:, pb5 * D:(pb5 + 1) * D],
                                                     sgp[:, pb5 * D:(pb5 + 1) * D],
                                                     ALU.add))
                vm5.append(s["VD"].mark())

                @block.sync
                def _(sync, t=t, pb5=pb5):
                    s["VD"].wait_n(sync, vm5[t])
                    s["DSS"].inc(sync.dma_start(out=y_out[t * 128:(t + 1) * 128, :],
                                                in_=sgp[:, pb5 * D:(pb5 + 1) * D]), 16)
                ssm5.append(s["DSS"].mark())

            for t in range(NT):
                pb = t % 2

                @block.tensor
                def _(tensor, t=t, pb=pb):
                    if t == 0:
                        s["VD"].wait(tensor)
                    if t < 1:
                        s["EV"].wait(tensor)
                    elif t >= 2:
                        s["EV"].wait_n(tensor, wom[t - 2])
                    for k in range(KD):
                        lhs = o_nrm[:, k * T + t * 128:k * T + (t + 1) * 128]
                        tensor.matmul(pW[pb][:, 0:512], lhs, woT[:, k * D:k * D + 512],
                                      start=(k == 0), stop=(k == KD - 1),
                                      skip_group_check=True)
                        mi = tensor.matmul(pW[pb][:, 512:1024], lhs,
                                           woT[:, k * D + 512:(k + 1) * D],
                                           start=(k == 0), stop=(k == KD - 1),
                                           skip_group_check=True)
                    s["PD"].inc(mi)

                @block.vector
                def _(vector, t=t, pb=pb):
                    s["PD"].wait(vector)
                    s["EV"].inc(vector.tensor_copy(s_sb[:, t * D:(t + 1) * D],
                                                   pW[pb][:, :]))
                wom.append(s["EV"].mark())
                emit_y(t)

            @block.sync
            def _(sync):
                s["DSS"].wait(sync)

    _phase4()


    if DEBUG_TAPS:
        taps = {
            "dbg_xn": (scr_xn, [T, D]), "dbg_xnq": (scr_xnq, [T, D]),
            "dbg_h": (scr_h, [T, FF]), "dbg_qf": (scr_qf, [H * DF, T]),
            "dbg_kf": (scr_kf, [H * DF, T]),
            "dbg_wg": (cc_wg_o, [FF, D]), "dbg_wu": (cc_wu_o, [FF, D]),
            "dbg_wd": (cc_wd_o, [D, FF]),
            "dbg_Si": (cc_S_i, [128, H * DF]), "dbg_So": (cc_S_o, [8 * 128, H * DF]),
        }
        taps["dbg_ofull"] = (scr_o, [128, H * T])
        dbg_outs = {k: nc.dram_tensor(k, shp, BF16, kind="ExternalOutput").ap()
                    for k, (_, shp) in taps.items()}
        sb_taps = {"dbg_g": (g_sb, [128, NT * D]), "dbg_s": (s_sb, [128, NT * D])}
        for k, (_, shp) in sb_taps.items():
            dbg_outs[k] = nc.dram_tensor(k, shp, BF16, kind="ExternalOutput").ap()

        @block.sync
        def _(sync):
            for k, (src_t, shp) in taps.items():
                s["DS"].inc(sync.dma_start(out=dbg_outs[k][:, :], in_=src_t[:, :]), 16)
            for k, (src_t, shp) in sb_taps.items():
                s["DS"].inc(sync.dma_start(out=dbg_outs[k][:, :], in_=src_t[:, :]), 16)
            s["DS"].wait(sync)

    stack.close()
    return nc


def kernel(L0_attn_x, L0_attn_Wq, L0_attn_Wk, L0_attn_Wv, L0_attn_Wo,
           L0_attn_fmq_w, L0_attn_fmq_b, L0_attn_fmk_w, L0_attn_fmk_b,
           L0_attn_rms_w, L0_ffn_w_gate, L0_ffn_w_up, L0_ffn_w_down):
    f32 = np.float32
    x = np.ascontiguousarray(np.asarray(L0_attn_x, dtype=f32))
    Wq = np.asarray(L0_attn_Wq, dtype=f32)
    Wk = np.asarray(L0_attn_Wk, dtype=f32)
    Wv = np.asarray(L0_attn_Wv, dtype=f32)
    Wo = np.asarray(L0_attn_Wo, dtype=f32)
    fmq_w = np.asarray(L0_attn_fmq_w, dtype=f32)
    fmk_w = np.asarray(L0_attn_fmk_w, dtype=f32)
    fmq_b = np.asarray(L0_attn_fmq_b, dtype=f32)
    fmk_b = np.asarray(L0_attn_fmk_b, dtype=f32)
    rms_w = np.asarray(L0_attn_rms_w, dtype=f32)
    w_gate = np.asarray(L0_ffn_w_gate, dtype=f32)
    w_up = np.asarray(L0_ffn_w_up, dtype=f32)
    w_down = np.asarray(L0_ffn_w_down, dtype=f32)

    wq_t = np.ascontiguousarray(Wq.T.reshape(D, H, 128).transpose(1, 0, 2))
    wk_t = np.ascontiguousarray(Wk.T.reshape(D, H, 128).transpose(1, 0, 2))
    wv_Tc = np.ascontiguousarray(Wv.T)
    wo_Tc = np.ascontiguousarray(Wo.T)
    fmq_Tc = np.ascontiguousarray(fmq_w.T)
    fmk_Tc = np.ascontiguousarray(fmk_w.T)
    fmq_bc = np.ascontiguousarray(fmq_b.reshape(128, 1))
    fmk_bc = np.ascontiguousarray(fmk_b.reshape(128, 1))
    rms_cc = np.ascontiguousarray(rms_w.reshape(128, 1))

    nc = build_nc()

    in_maps = []
    for c in range(8):
        b, q = c // 4, c % 4
        msk = np.zeros((128, 8), dtype=np.float32)
        msk[:, 4 * b:4 * b + q] = 1.0
        in_maps.append({
            "x": np.ascontiguousarray(x[b, q * T:(q + 1) * T, :]),
            "wq_t": wq_t, "wk_t": wk_t, "wv_T": wv_Tc, "wo_T": wo_Tc,
            "fmq_T": fmq_Tc, "fmk_T": fmk_Tc, "fmq_b": fmq_bc, "fmk_b": fmk_bc,
            "rms_c": rms_cc,
            "wg_sl": np.ascontiguousarray(w_gate[c * WSL:(c + 1) * WSL, :]),
            "wu_sl": np.ascontiguousarray(w_up[c * WSL:(c + 1) * WSL, :]),
            "wd_sl": np.ascontiguousarray(w_down[c * 128:(c + 1) * 128, :]),
            "smask": msk,
        })

    res = run_bass_kernel_spmd(nc, in_maps, core_ids=list(range(8)))
    globals()["_last_results"] = res.results

    out = np.empty((2, 8192, D), dtype=f32)
    for c in range(8):
        b, q = c // 4, c % 4
        out[b, q * T:(q + 1) * T, :] = np.asarray(res.results[c]["y"])
    return out


# version 22
# speedup vs baseline: 1.0261x; 1.0025x over previous
import numpy as np
import ml_dtypes
import concourse.bass as bass
import concourse.mybir as mybir
from concourse.bass_utils import run_bass_kernel_spmd

F32 = mybir.dt.float32
BF16 = mybir.dt.bfloat16
AF = mybir.ActivationFunctionType
ALU = mybir.AluOpType
AX = mybir.AxisListType

T = 2048
D = 1024
NT = 16
KD = 8
H = 8
DV = 128
DF = 256
NCH = 16
FF = 2816
NFK = 22
QW = 704
WSL = FF // 8
LN16 = float(np.log(16.0))
QC0 = float(np.float32((4.0 / 3.0) * 2 ** 22))
QC1 = float(np.float32(2 ** 22 / 6.0))

DEBUG_TAPS = False


class SemC:
    def __init__(self, stack, nc, name):
        self.h = stack.enter_context(nc.semaphore(name))
        self.n = 0

    def inc(self, instr, k=1):
        instr.then_inc(self.h, k)
        self.n += k
        return instr

    def wait(self, eng):
        eng.wait_ge(self.h, self.n)

    def mark(self):
        return self.n

    def wait_n(self, eng, n):
        eng.wait_ge(self.h, n)


def _quant_tile(block, s, bufs, src_ap, n, dq_ap, pre=("VD",)):
    nb = n // 16
    amax, sc, rec, v, w, M, tq = (bufs[k] for k in
                                  ("amax", "sc", "rec", "v", "w", "M", "tq"))
    p = src_ap.shape[0]

    @block.vector
    def _(vector):
        for k in pre:
            s[k].wait(vector)
        src3 = src_ap.rearrange("p (nb k) -> p nb k", k=16)
        s["VD"].inc(vector.tensor_reduce(amax[0:p, 0:nb].unsqueeze(2), src3, AX.X,
                                         ALU.max, apply_absolute_value=True))
        s["VD"].wait(vector)
        s["VD"].inc(vector.tensor_scalar(sc[0:p, 0:nb], amax[0:p, 0:nb],
                                         1.0 / 6.0, 1e-12, ALU.mult, ALU.max))
        s["VD"].wait(vector)
        s["VD"].inc(vector.reciprocal(rec[0:p, 0:nb], sc[0:p, 0:nb]))
        s["VD"].wait(vector)
        src3 = src_ap.rearrange("p (nb k) -> p nb k", k=16)
        rbc = rec[0:p, 0:nb].unsqueeze(2).broadcast_to([p, nb, 16])
        s["VD"].inc(vector.tensor_tensor(
            v[0:p, 0:n].rearrange("p (nb k) -> p nb k", k=16), src3, rbc, ALU.mult))

    @block.scalar
    def _(scalar):
        s["VD"].wait(scalar)
        s["AD"].inc(scalar.activation(w[0:p, 0:n], v[0:p, 0:n], AF.Square))

    @block.vector
    def _(vector):
        s["AD"].wait(vector)
        s["VD"].inc(vector.tensor_scalar(M[0:p, 0:n], w[0:p, 0:n], QC1, QC0,
                                         ALU.mult, ALU.add))
        s["VD"].wait(vector)
        s["VD"].inc(vector.tensor_tensor(tq[0:p, 0:n], v[0:p, 0:n], M[0:p, 0:n],
                                         ALU.add))
        s["VD"].wait(vector)
        s["VD"].inc(vector.tensor_tensor(tq[0:p, 0:n], tq[0:p, 0:n], M[0:p, 0:n],
                                         ALU.subtract))
        s["VD"].wait(vector)
        sbc = sc[0:p, 0:nb].unsqueeze(2).broadcast_to([p, nb, 16])
        s["VD"].inc(vector.tensor_tensor(
            dq_ap.rearrange("p (nb k) -> p nb k", k=16),
            tq[0:p, 0:n].rearrange("p (nb k) -> p nb k", k=16), sbc, ALU.mult))




def _quant2(block, s, bufs, src_ap, n, dq_ap, pre=("VD",)):
    nb = n // 16
    amax, sc, rec, v, w, M, tq = (bufs[k] for k in
                                  ("amax", "sc", "rec", "v", "w", "M", "tq"))
    p = src_ap.shape[0]

    @block.vector
    def _(vector):
        for k in pre:
            s[k].wait(vector)
        src3 = src_ap.rearrange("p (nb k) -> p nb k", k=16)
        s["VD"].inc(vector.tensor_reduce(amax[0:p, 0:nb].unsqueeze(2), src3, AX.X,
                                         ALU.max, apply_absolute_value=True))
        s["VD"].wait(vector)
        s["VD"].inc(vector.tensor_scalar(sc[0:p, 0:nb], amax[0:p, 0:nb],
                                         1.0 / 6.0, 1e-12, ALU.mult, ALU.max))
        s["VD"].wait(vector)
        s["VD"].inc(vector.reciprocal(rec[0:p, 0:nb], sc[0:p, 0:nb]))
        s["VD"].wait(vector)
        src3 = src_ap.rearrange("p (nb k) -> p nb k", k=16)
        rbc = rec[0:p, 0:nb].unsqueeze(2).broadcast_to([p, nb, 16])
        s["VD"].inc(vector.tensor_tensor(
            v[0:p, 0:n].rearrange("p (nb k) -> p nb k", k=16), src3, rbc, ALU.mult))

    @block.gpsimd
    def _(gpsimd):
        s["VD"].wait(gpsimd)
        s["GD"].inc(gpsimd.tensor_tensor(w[0:p, 0:n], v[0:p, 0:n], v[0:p, 0:n],
                                         ALU.mult))

    @block.vector
    def _(vector):
        s["GD"].wait(vector)
        s["VD"].inc(vector.tensor_scalar(M[0:p, 0:n], w[0:p, 0:n], QC1, QC0,
                                         ALU.mult, ALU.add))

    @block.gpsimd
    def _(gpsimd):
        s["VD"].wait(gpsimd)
        s["GD"].inc(gpsimd.tensor_tensor(tq[0:p, 0:n], v[0:p, 0:n], M[0:p, 0:n],
                                         ALU.add))

    @block.vector
    def _(vector):
        s["GD"].wait(vector)
        s["VD"].inc(vector.tensor_tensor(tq[0:p, 0:n], tq[0:p, 0:n], M[0:p, 0:n],
                                         ALU.subtract))
        s["VD"].wait(vector)
        sbc = sc[0:p, 0:nb].unsqueeze(2).broadcast_to([p, nb, 16])
        s["VD"].inc(vector.tensor_tensor(
            dq_ap.rearrange("p (nb k) -> p nb k", k=16),
            tq[0:p, 0:n].rearrange("p (nb k) -> p nb k", k=16), sbc, ALU.mult))


def _barrier(block, s):
    tgt = s["PH"].n + 5
    s["PH"].n = tgt
    for engname in ("sync", "tensor", "vector", "scalar", "gpsimd"):
        def mk():
            def _(eng):
                for k in ("VD", "AD", "PD", "GD", "EV"):
                    s[k].wait(eng)
                eng.sem_inc(s["PH"].h, 1)
                eng.wait_ge(s["PH"].h, tgt)
            return _
        getattr(block, engname)(mk())


def build_nc():
    import contextlib
    stack = contextlib.ExitStack()
    nc = bass.Bass("TRN2", target_bir_lowering=False)

    x_in = nc.dram_tensor("x", [T, D], F32, kind="ExternalInput").ap()
    y_out = nc.dram_tensor("y", [T, D], F32, kind="ExternalOutput").ap()
    wq_t = nc.dram_tensor("wq_t", [H, D, 128], F32, kind="ExternalInput").ap()
    wk_t = nc.dram_tensor("wk_t", [H, D, 128], F32, kind="ExternalInput").ap()
    wv_T = nc.dram_tensor("wv_T", [D, D], F32, kind="ExternalInput").ap()
    wo_T = nc.dram_tensor("wo_T", [D, D], F32, kind="ExternalInput").ap()
    fmq_T = nc.dram_tensor("fmq_T", [128, 128], F32, kind="ExternalInput").ap()
    fmk_T = nc.dram_tensor("fmk_T", [128, 128], F32, kind="ExternalInput").ap()
    fmq_b = nc.dram_tensor("fmq_b", [128, 1], F32, kind="ExternalInput").ap()
    fmk_b = nc.dram_tensor("fmk_b", [128, 1], F32, kind="ExternalInput").ap()
    rms_c = nc.dram_tensor("rms_c", [128, 1], F32, kind="ExternalInput").ap()
    wg_sl = nc.dram_tensor("wg_sl", [WSL, D], F32, kind="ExternalInput").ap()
    wu_sl = nc.dram_tensor("wu_sl", [WSL, D], F32, kind="ExternalInput").ap()
    wd_sl = nc.dram_tensor("wd_sl", [128, FF], F32, kind="ExternalInput").ap()
    smask = nc.dram_tensor("smask", [128, 8], F32, kind="ExternalInput").ap()

    scr_xn = nc.dram_tensor("scr_xn", [T, D], BF16).ap()
    scr_xnq = nc.dram_tensor("scr_xnq", [T, D], BF16).ap()
    scr_h = nc.dram_tensor("scr_h", [T, FF], BF16).ap()
    scr_kf = nc.dram_tensor("scr_kf", [H * DF, T], BF16).ap()
    scr_qf = nc.dram_tensor("scr_qf", [H * DF, T], BF16).ap()
    cc_wg_i = nc.dram_tensor("cc_wg_i", [WSL, D], BF16).ap()
    cc_wg_o = nc.dram_tensor("cc_wg_o", [FF, D], BF16).ap()
    cc_wu_i = nc.dram_tensor("cc_wu_i", [WSL, D], BF16).ap()
    cc_wu_o = nc.dram_tensor("cc_wu_o", [FF, D], BF16).ap()
    cc_wd_i = nc.dram_tensor("cc_wd_i", [128, FF], BF16).ap()
    cc_wd_o = nc.dram_tensor("cc_wd_o", [D, FF], BF16).ap()
    cc_S_i = nc.dram_tensor("cc_S_i", [128, H * DF], BF16).ap()
    cc_S_o = nc.dram_tensor("cc_S_o", [8 * 128, H * DF], BF16).ap()

    s = {k: SemC(stack, nc, k) for k in
         ("DS", "DA", "DG", "DSK", "DSQ", "DSS", "CC", "PH", "PD", "VD", "AD", "GD", "EV")}

    block = stack.enter_context(nc.Block())

    def alloc(name, shape, dt):
        return stack.enter_context(nc.sbuf_tensor(name, shape, dt))

    ones_b = alloc("ones_b", [128, 128], BF16)
    mask4 = alloc("mask4", [128, 512], BF16)
    fmq_sb = alloc("fmq_sb", [128, 128], BF16)
    fmk_sb = alloc("fmk_sb", [128, 128], BF16)
    bcol = alloc("bcol", [128, 8], F32)
    eps_sb = alloc("eps_sb", [128, 1], F32)
    eps2_sb = alloc("eps2_sb", [128, 1], F32)
    nl2_sb = alloc("nl2_sb", [128, 1], F32)
    smask_sb = alloc("smask_sb", [128, 8], F32)
    imask = alloc("imask", [128, 512], F32)
    ident_b = alloc("ident_b", [128, 128], BF16)
    BQP, BQM, BKP, BKM, RMS, FQB, FKB = range(7)

    def bc(i):
        return bcol[:, i:i + 1]

    @block.scalar
    def _(scalar):
        s["DA"].inc(scalar.dma_start(out=bc(FQB), in_=fmq_b[:, :]), 16)
        s["DA"].inc(scalar.dma_start(out=bc(FKB), in_=fmk_b[:, :]), 16)
        s["DA"].inc(scalar.dma_start(out=bc(RMS), in_=rms_c[:, :]), 16)
        s["DA"].inc(scalar.dma_start(out=smask_sb[:], in_=smask[:, :]), 16)

    @block.gpsimd
    def _(gpsimd):
        gpsimd.memset(ones_b[:], 1.0)
        gpsimd.memset(eps_sb[:, :], 1e-5)
        gpsimd.memset(eps2_sb[:, :], 1e-5 * 2.0 ** -64)
        gpsimd.memset(nl2_sb[:, :], -32.0 * 0.6931471805599453)
        s["GD"].inc(gpsimd.iota(imask[:], pattern=[[1, 512]],
                                channel_multiplier=-1,
                                allow_small_or_imprecise_dtypes=True))
        s["DG"].inc(gpsimd.dma_start(out=fmq_sb[:], in_=fmq_T[:, :]), 16)
        s["DG"].inc(gpsimd.dma_start(out=fmk_sb[:], in_=fmk_T[:, :]), 16)

    @block.vector
    def _(vector):
        s["GD"].wait(vector)
        s["DA"].wait(vector)
        vector.tensor_scalar(mask4[:], imask[:], 0.0, None, ALU.is_ge)
        vector.tensor_scalar(ident_b[:], imask[:, 0:128], 0.0, None, ALU.is_equal)
        vector.tensor_scalar(bc(BQP), bc(FQB), -LN16, None, ALU.add)
        vector.tensor_scalar(bc(BQM), bc(FQB), -1.0, -LN16, ALU.mult, ALU.add)
        vector.tensor_copy(bc(BKP), bc(FKB))
        s["VD"].inc(vector.tensor_scalar(bc(BKM), bc(FKB), -1.0, None, ALU.mult))

    def _phase0():
        CW = 4 * D
        with (
            nc.sbuf_tensor("x_sb", [128, 2 * CW], F32) as x_sb,
            nc.sbuf_tensor("wsl_sb", [128, 3 * D + FF], F32) as wsl_sb,
            nc.sbuf_tensor("qS", [128, 3 * 320], F32) as qS,
            nc.sbuf_tensor("qB", [128, 4 * 4880], F32) as qB,
            nc.sbuf_tensor("dq_b", [128, 2 * CW], BF16) as dq_b,
            nc.sbuf_tensor("ln_st", [128, 6 * NT], F32) as ln_st,
            nc.sbuf_tensor("xn_f", [128, 2 * CW], F32) as xn_f,
            nc.sbuf_tensor("xn_b", [128, 2 * CW], BF16) as xn_b,
        ):
            QBW = 4880
            qbufs = dict(amax=qS[:, 0:320], sc=qS[:, 320:640],
                         rec=qS[:, 640:960],
                         v=qB[:, 0:QBW], w=qB[:, QBW:2 * QBW],
                         M=qB[:, 2 * QBW:3 * QBW], tq=qB[:, 3 * QBW:4 * QBW])

            ST_SUM, ST_SQ, ST_MEAN, ST_VAR, ST_RSTD, ST_NMR = (
                0, NT, 2 * NT, 3 * NT, 4 * NT, 5 * NT)

            for ck in range(NT // 4):
                cb = ck % 2
                sqs = qB[:, (1 + cb) * CW:(2 + cb) * CW]

                @block.sync
                def _(sync, ck=ck, cb=cb):
                    if ck >= 2:
                        s["VD"].wait(sync)
                    s["DS"].inc(sync.dma_start(
                        out=x_sb[:, cb * CW:(cb + 1) * CW].rearrange(
                            "p (tt d) -> p tt d", d=D),
                        in_=x_in[ck * 512:(ck + 1) * 512, :].rearrange(
                            "(tt p) d -> p tt d", p=128)), 16)

                @block.gpsimd
                def _(gpsimd, ck=ck, cb=cb, sqs=sqs):
                    s["DS"].wait(gpsimd)
                    if ck >= 2:
                        s["VD"].wait(gpsimd)
                    s["GD"].inc(gpsimd.tensor_tensor(
                        sqs, x_sb[:, cb * CW:(cb + 1) * CW],
                        x_sb[:, cb * CW:(cb + 1) * CW], ALU.mult))

                @block.vector
                def _(vector, ck=ck, cb=cb, sqs=sqs):
                    s["DS"].wait(vector)
                    for j in range(4):
                        t = ck * 4 + j
                        vector.tensor_reduce(
                            ln_st[:, ST_SUM + t:ST_SUM + t + 1].unsqueeze(2),
                            x_sb[:, cb * CW + j * D:cb * CW + (j + 1) * D].unsqueeze(1),
                            AX.X, ALU.add)
                    s["GD"].wait(vector)
                    li = None
                    for j in range(4):
                        t = ck * 4 + j
                        li = vector.tensor_reduce(
                            ln_st[:, ST_SQ + t:ST_SQ + t + 1].unsqueeze(2),
                            sqs[:, j * D:(j + 1) * D].unsqueeze(1), AX.X, ALU.add)
                    s["VD"].inc(li)

            @block.vector
            def _(vector):
                s["VD"].wait(vector)
                s["VD"].inc(vector.tensor_scalar(ln_st[:, ST_MEAN:ST_MEAN + NT],
                                                 ln_st[:, ST_SUM:ST_SUM + NT],
                                                 1.0 / D, None, ALU.mult))
                s["VD"].inc(vector.tensor_scalar(ln_st[:, ST_VAR:ST_VAR + NT],
                                                 ln_st[:, ST_SQ:ST_SQ + NT],
                                                 1.0 / D, None, ALU.mult))
                s["VD"].wait(vector)
                s["VD"].inc(vector.tensor_tensor(ln_st[:, ST_SQ:ST_SQ + NT],
                                                 ln_st[:, ST_MEAN:ST_MEAN + NT],
                                                 ln_st[:, ST_MEAN:ST_MEAN + NT], ALU.mult))
                s["VD"].wait(vector)
                s["VD"].inc(vector.tensor_tensor(ln_st[:, ST_VAR:ST_VAR + NT],
                                                 ln_st[:, ST_VAR:ST_VAR + NT],
                                                 ln_st[:, ST_SQ:ST_SQ + NT], ALU.subtract))

            @block.scalar
            def _(scalar):
                s["VD"].wait(scalar)
                scalar.activation(ln_st[:, ST_SUM:ST_SUM + NT], ln_st[:, ST_VAR:ST_VAR + NT],
                                  AF.Ln, bias=eps_sb[:, 0:1])
                s["AD"].inc(scalar.activation(ln_st[:, ST_RSTD:ST_RSTD + NT],
                                              ln_st[:, ST_SUM:ST_SUM + NT],
                                              AF.Exp, scale=-0.5))

            @block.vector
            def _(vector):
                s["AD"].wait(vector)
                s["VD"].inc(vector.scalar_tensor_tensor(
                    ln_st[:, ST_NMR:ST_NMR + NT], ln_st[:, ST_MEAN:ST_MEAN + NT], -1.0,
                    ln_st[:, ST_RSTD:ST_RSTD + NT], ALU.mult, ALU.mult))

            def emit_xnq(cki):
                cb = cki % 2
                _quant2(block, s, qbufs, xn_f[:, cb * CW:(cb + 1) * CW], CW,
                        dq_b[:, cb * CW:(cb + 1) * CW], pre=("VD", "AD", "DS", "DG"))

                @block.sync
                def _(sync, cki=cki, cb=cb):
                    s["VD"].wait(sync)
                    s["DS"].inc(sync.dma_start(
                        out=scr_xnq[cki * 512:(cki + 1) * 512, :].rearrange(
                            "(tt p) d -> p tt d", p=128),
                        in_=dq_b[:, cb * CW:(cb + 1) * CW].rearrange(
                            "p (tt d) -> p tt d", d=D)), 16)

            for cki in range(NT // 4):
                cb = cki % 2

                @block.sync
                def _(sync, cki=cki, cb=cb):
                    s["AD"].wait(sync)
                    s["DS"].inc(sync.dma_start(
                        out=x_sb[:, cb * CW:(cb + 1) * CW].rearrange(
                            "p (tt d) -> p tt d", d=D),
                        in_=x_in[cki * 512:(cki + 1) * 512, :].rearrange(
                            "(tt p) d -> p tt d", p=128)), 16)

                @block.scalar
                def _(scalar, cki=cki, cb=cb):
                    s["DS"].wait(scalar)
                    s["VD"].wait(scalar)
                    for j in range(4):
                        t = cki * 4 + j
                        scalar.activation(xn_f[:, cb * CW + j * D:cb * CW + (j + 1) * D],
                                          x_sb[:, cb * CW + j * D:cb * CW + (j + 1) * D],
                                          AF.Identity,
                                          bias=ln_st[:, ST_NMR + t:ST_NMR + t + 1],
                                          scale=ln_st[:, ST_RSTD + t:ST_RSTD + t + 1])
                    for j in range(4):
                        t = cki * 4 + j
                        li = scalar.activation(
                            xn_b[:, cb * CW + j * D:cb * CW + (j + 1) * D],
                            x_sb[:, cb * CW + j * D:cb * CW + (j + 1) * D],
                            AF.Identity, bias=ln_st[:, ST_NMR + t:ST_NMR + t + 1],
                            scale=ln_st[:, ST_RSTD + t:ST_RSTD + t + 1])
                    s["AD"].inc(li)

                @block.sync
                def _(sync, cki=cki, cb=cb):
                    s["AD"].wait(sync)
                    s["DS"].inc(sync.dma_start(
                        out=scr_xn[cki * 512:(cki + 1) * 512, :].rearrange(
                            "(tt p) d -> p tt d", p=128),
                        in_=xn_b[:, cb * CW:(cb + 1) * CW].rearrange(
                            "p (tt d) -> p tt d", d=D)), 16)

                if cki >= 1:
                    emit_xnq(cki - 1)

            emit_xnq(NT // 4 - 1)

            @block.scalar
            def _(scalar):
                s["DA"].inc(scalar.dma_start(out=wsl_sb[:, 0:D], in_=wg_sl[0:128, :]), 16)
                s["DA"].inc(scalar.dma_start(out=wsl_sb[:, D:2 * D], in_=wu_sl[0:128, :]), 16)
                s["DA"].inc(scalar.dma_start(out=wsl_sb[:, 2 * D:3 * D],
                                             in_=wd_sl[0:128, 0:D]), 16)
                s["DA"].inc(scalar.dma_start(out=wsl_sb[:, 3 * D:3 * D + FF - D],
                                             in_=wd_sl[0:128, D:FF]), 16)

            @block.vector
            def _(vector):
                s["DA"].wait(vector)

            _quant2(block, s, qbufs, wsl_sb[:, 0:2 * D + FF], 2 * D + FF,
                    dq_b[:, 0:2 * D + FF])

            @block.gpsimd
            def _(gpsimd):
                s["VD"].wait(gpsimd)
                s["DG"].inc(gpsimd.dma_start(out=cc_wg_i[0:128, :],
                                             in_=dq_b[:, 0:D]), 16)
                s["DG"].inc(gpsimd.dma_start(out=cc_wu_i[0:128, :],
                                             in_=dq_b[:, D:2 * D]), 16)
                s["DG"].inc(gpsimd.dma_start(out=cc_wd_i[0:128, :],
                                             in_=dq_b[:, 2 * D:2 * D + FF]), 16)

            @block.scalar
            def _(scalar):
                s["VD"].wait(scalar)
                s["DG"].wait(scalar)
                s["DA"].wait(scalar)
                s["DA"].inc(scalar.dma_start(out=wsl_sb[:, 0:D], in_=wg_sl[128:256, :]), 16)
                s["DA"].inc(scalar.dma_start(out=wsl_sb[:, D:2 * D],
                                             in_=wu_sl[128:256, :]), 16)
                s["DA"].inc(scalar.dma_start(out=wsl_sb[0:96, 2 * D:3 * D],
                                             in_=wg_sl[256:352, :]), 16)
                s["DA"].inc(scalar.dma_start(out=wsl_sb[0:96, 3 * D:4 * D],
                                             in_=wu_sl[256:352, :]), 16)

            @block.vector
            def _(vector):
                s["DA"].wait(vector)

            _quant2(block, s, qbufs, wsl_sb[:, 0:2 * D], 2 * D, dq_b[:, 0:2 * D])

            @block.gpsimd
            def _(gpsimd):
                s["VD"].wait(gpsimd)
                s["DG"].inc(gpsimd.dma_start(out=cc_wg_i[128:256, :],
                                             in_=dq_b[:, 0:D]), 16)
                s["DG"].inc(gpsimd.dma_start(out=cc_wu_i[128:256, :],
                                             in_=dq_b[:, D:2 * D]), 16)

            _quant2(block, s, qbufs, wsl_sb[0:96, 2 * D:4 * D], 2 * D,
                    dq_b[0:96, 2 * D:4 * D])

            @block.gpsimd
            def _(gpsimd):
                s["VD"].wait(gpsimd)
                s["DG"].inc(gpsimd.dma_start(out=cc_wg_i[256:352, :],
                                             in_=dq_b[0:96, 2 * D:3 * D]), 16)
                s["DG"].inc(gpsimd.dma_start(out=cc_wu_i[256:352, :],
                                             in_=dq_b[0:96, 3 * D:4 * D]), 16)

            @block.gpsimd
            def _(gpsimd):
                s["DG"].wait(gpsimd)
                for ci, co in ((cc_wg_i, cc_wg_o), (cc_wu_i, cc_wu_o), (cc_wd_i, cc_wd_o)):
                    s["CC"].inc(gpsimd.collective_compute(
                        "AllGather", ALU.bypass, replica_groups=[list(range(8))],
                        ins=[ci[:, :].opt()], outs=[co[:, :].opt()]))


            @block.sync
            def _(sync):
                s["DS"].wait(sync)

            _barrier(block, s)

    _phase0()

    scr_o = nc.dram_tensor("scr_o", [128, H * T], BF16, kind="Internal").ap()

    def _phase2():
        with (
            nc.sbuf_tensor("xnT", [128, KD * T], BF16) as xnT,
            nc.sbuf_tensor("v_sb", [128, NT * D], BF16) as v_sb,
            nc.sbuf_tensor("wv_sb", [128, KD * D], BF16) as wv_sb,
            nc.sbuf_tensor("wqk", [128, 4 * D], BF16) as wqk,
            nc.sbuf_tensor("qkraw", [128, 2 * T], BF16) as qkraw,
            nc.sbuf_tensor("feat", [128, 4 * T], BF16) as feat,
            nc.sbuf_tensor("kfT", [128, NCH * DF], BF16) as kfT,
            nc.sbuf_tensor("attnT", [128, NCH * 128], BF16) as attnT,
            nc.sbuf_tensor("Sx", [128, DF + H * DF], BF16) as Sx,
            nc.sbuf_tensor("o_full", [128, H * T], BF16) as o_full,
            nc.sbuf_tensor("echo2", [128, 1], F32) as echo2,
            nc.psum_tensor("pp", [128, 3072], F32) as pp,
            nc.psum_tensor("pt", [128, 2048], BF16) as pt,
        ):
            wqk_r = ((wqk[:, 0:D], wqk[:, D:2 * D]),
                     (wqk[:, 2 * D:3 * D], wqk[:, 3 * D:4 * D]))
            qraw, kraw = qkraw[:, 0:T], qkraw[:, T:2 * T]
            qfp, qfm = feat[:, 0:T], feat[:, T:2 * T]
            kfp, kfm = feat[:, 2 * T:3 * T], feat[:, 3 * T:4 * T]
            S_sb = Sx[:, 0:DF]
            Sstage = Sx[:, DF:DF + H * DF]
            pv = (pp[:, 0:1024], pp[:, 1024:2048], pp[:, 2048:3072])
            pS = pp[:, 0:512]
            pqy = (pp[:, 1024:1536], pp[:, 1536:2048])
            pA = (pp[:, 2048:2560], pp[:, 2560:3072])
            pO = pp[:, 512:1024]

            @block.sync
            def _(sync):
                for k in range(KD):
                    s["DS"].inc(sync.dma_start(out=xnT[:, k * T:(k + 1) * T],
                                               in_=scr_xn[:, k * 128:(k + 1) * 128],
                                               transpose=True), 16)

            @block.gpsimd
            def _(gpsimd):
                for k in range(KD):
                    s["DG"].inc(gpsimd.dma_start(out=wv_sb[:, k * D:(k + 1) * D],
                                                 in_=wv_T[k * 128:(k + 1) * 128, :]), 16)

            vm2 = []
            for t in range(NT):
                pvb = t % 2

                @block.tensor
                def _(tensor, t=t, pvb=pvb):
                    if t == 0:
                        s["DS"].wait(tensor)
                        s["DG"].wait(tensor)
                    if t >= 1:
                        s["EV"].wait_n(tensor, vm2[t - 1])
                    for k in range(KD):
                        lhs = xnT[:, k * T + t * 128:k * T + (t + 1) * 128]
                        tensor.matmul(pv[pvb][:, 0:512], lhs, wv_sb[:, k * D:k * D + 512],
                                      start=(k == 0), stop=(k == KD - 1),
                                      skip_group_check=True)
                        mi = tensor.matmul(pv[pvb][:, 512:1024], lhs,
                                           wv_sb[:, k * D + 512:(k + 1) * D],
                                           start=(k == 0), stop=(k == KD - 1),
                                           skip_group_check=True)
                    s["PD"].inc(mi)

                @block.vector
                def _(vector, t=t, pvb=pvb):
                    s["PD"].wait(vector)
                    s["EV"].inc(vector.tensor_copy(v_sb[:, t * D:(t + 1) * D],
                                                   pv[pvb][:, :]))
                vm2.append(s["EV"].mark())

            pqm = [None] * (16 * H)
            emk = [None] * H
            dskm = [None] * NCH
            aam_all = []
            om_all = []
            scm_all = []
            ktm_all = []
            qrm = []
            for h in range(H):
                wqm_sb, wkm_sb = wqk_r[h % 2]

                @block.gpsimd
                def _(gpsimd, h=h):
                    if h >= 2:
                        s["PD"].wait_n(gpsimd, qrm[h - 1])
                    s["DG"].inc(gpsimd.dma_start(
                        out=wqk_r[h % 2][0].rearrange("p (k c) -> p k c", c=128),
                        in_=wq_t[h].rearrange("(k p) c -> p k c", p=128)), 16)
                    s["DG"].inc(gpsimd.dma_start(
                        out=wqk_r[h % 2][1].rearrange("p (k c) -> p k c", c=128),
                        in_=wk_t[h].rearrange("(k p) c -> p k c", p=128)), 16)

                for wi, (which, wm, rawbuf) in enumerate(
                        (("q", wqm_sb, qraw), ("k", wkm_sb, kraw))):
                    for tc in range(4):
                        pb = tc % 2
                        G2 = 16 * h + wi * 4 + tc

                        @block.tensor
                        def _(tensor, h=h, which=which, wm=wm, tc=tc, pb=pb, G2=G2):
                            if which == "q" and tc == 0:
                                s["DG"].wait(tensor)
                                if h > 0:
                                    s["EV"].wait_n(tensor, ktm_all[-1])
                            if G2 < 1:
                                s["EV"].wait(tensor)
                            else:
                                s["EV"].wait_n(tensor, pqm[G2 - 1])
                            for k in range(KD):
                                mi = tensor.matmul(
                                    pqy[pb], wm[:, k * 128:(k + 1) * 128],
                                    xnT[:, k * T + tc * 512:k * T + (tc + 1) * 512],
                                    start=(k == 0), stop=(k == KD - 1),
                                    skip_group_check=True)
                            s["PD"].inc(mi)

                        @block.vector
                        def _(vector, rawbuf=rawbuf, tc=tc, pb=pb):
                            s["PD"].wait(vector)
                            s["EV"].inc(vector.tensor_copy(
                                rawbuf[:, tc * 512:(tc + 1) * 512], pqy[pb]))
                        pqm[G2] = s["EV"].mark()
                qrm.append(s["PD"].mark())

                for wi, (which, rawbuf, fT, bpi, bmi, fp_, fm_) in enumerate((
                        ("q", qraw, fmq_sb, BQP, BQM, qfp, qfm),
                        ("k", kraw, fmk_sb, BKP, BKM, kfp, kfm))):
                    for tc in range(4):
                        pb = tc % 2
                        G2 = 16 * h + 8 + wi * 4 + tc

                        @block.tensor
                        def _(tensor, which=which, rawbuf=rawbuf, fT=fT, tc=tc,
                              pb=pb, G2=G2):
                            s["EV"].wait_n(tensor, pqm[G2 - 1])
                            if which == "q" and tc == 0:
                                s["DSQ"].wait(tensor)
                                s["DSK"].wait(tensor)
                            mi = tensor.matmul(pqy[pb], fT[:, :],
                                               rawbuf[:, tc * 512:(tc + 1) * 512],
                                               start=True, stop=True,
                                               skip_group_check=True)
                            s["PD"].inc(mi)

                        @block.scalar
                        def _(scalar, tc=tc, pb=pb, bpi=bpi, bmi=bmi, fp_=fp_, fm_=fm_):
                            s["PD"].wait(scalar)
                            scalar.activation(fp_[:, tc * 512:(tc + 1) * 512], pqy[pb],
                                              AF.Exp, bias=bc(bpi), scale=1.0)
                            s["EV"].inc(scalar.activation(
                                fm_[:, tc * 512:(tc + 1) * 512], pqy[pb],
                                AF.Exp, bias=bc(bmi), scale=-1.0))
                        pqm[G2] = s["EV"].mark()

                @block.scalar
                def _(scalar):
                    s["EV"].inc(scalar.activation(echo2[:, :], ones_b[:, 0:1], AF.Exp))
                emk[h] = s["EV"].mark()

                @block.sync
                def _(sync, h=h):
                    s["EV"].wait(sync)
                    s["PD"].wait(sync)
                    s["DSQ"].inc(sync.dma_start(out=scr_qf[h * DF:h * DF + 128, :],
                                                in_=qfp), 16)
                    s["DSQ"].inc(sync.dma_start(out=scr_qf[h * DF + 128:(h + 1) * DF, :],
                                                in_=qfm), 16)

                for b in range(4):
                    pob = 0
                    for i in range(4):
                        c = 4 * b + i
                        nq = (4 - i) * 128
                        q0 = (4 * b + i) * 128

                        @block.tensor
                        def _(tensor, h=h, b=b, i=i, c=c, nq=nq, q0=q0):
                            ai = len(aam_all)
                            if ai >= 2:
                                s["EV"].wait_n(tensor, aam_all[ai - 2])
                            if b == 0 and i == 0:
                                s["EV"].wait_n(tensor, emk[h])
                            pb = i % 2
                            tensor.matmul(pA[pb][:, 0:nq], kfp[:, c * 128:(c + 1) * 128],
                                          qfp[:, q0:q0 + nq], start=True, stop=False,
                                          skip_group_check=True)
                            mi = tensor.matmul(pA[pb][:, 0:nq],
                                               kfm[:, c * 128:(c + 1) * 128],
                                               qfm[:, q0:q0 + nq], start=False,
                                               stop=True, skip_group_check=True)
                            s["PD"].inc(mi)

                        @block.vector
                        def _(vector, i=i, nq=nq):
                            s["PD"].wait(vector)
                            pb = i % 2
                            s["EV"].inc(vector.tensor_tensor(
                                attnT[:, i * 512:i * 512 + nq],
                                pA[pb][:, 0:nq], mask4[:, 0:nq], ALU.mult))
                        aam_all.append(s["EV"].mark())

                    for i in range(4):
                        c = 4 * b + i
                        pt0 = (c % 2) * 1024

                        @block.tensor
                        def _(tensor, h=h, c=c, pt0=pt0):
                            ki = len(ktm_all)
                            if ki >= 2:
                                s["EV"].wait_n(tensor, ktm_all[ki - 2])
                            tensor.matmul(pt[:, pt0:pt0 + 128],
                                          kfp[:, c * 128:(c + 1) * 128], ident_b[:, :],
                                          start=True, stop=False, is_transpose=True,
                                          skip_group_check=True)
                            mi = tensor.matmul(pt[:, pt0 + 128:pt0 + 256],
                                               kfm[:, c * 128:(c + 1) * 128],
                                               ident_b[:, :],
                                               start=False, stop=True,
                                               is_transpose=True,
                                               skip_group_check=True)
                            s["PD"].inc(mi)

                        @block.vector
                        def _(vector, c=c, pt0=pt0):
                            s["PD"].wait(vector)
                            s["EV"].inc(vector.tensor_copy(
                                kfT[:, c * DF:(c + 1) * DF], pt[:, pt0:pt0 + 256]))
                        ktm_all.append(s["EV"].mark())

                    @block.tensor
                    def _(tensor, h=h, b=b, pob=pob):
                        s["EV"].wait_n(tensor, aam_all[-1])
                        if b >= 1:
                            s["EV"].wait_n(tensor, om_all[-1])
                        elif h > 0:
                            s["EV"].wait_n(tensor, om_all[-1])
                        for i in range(4):
                            c = 4 * b + i
                            nq = (4 - i) * 128
                            vsl = v_sb[:, c * D + h * 128:c * D + (h + 1) * 128]
                            mi = tensor.matmul(pO[:, i * 128:512], vsl,
                                               attnT[:, i * 512:i * 512 + nq],
                                               start=(i == 0),
                                               stop=(b == 0 and i == 3),
                                               skip_group_check=True)
                        if b > 0:
                            tensor.matmul(pO[:, 0:512], S_sb[:, 0:128],
                                          qfp[:, b * 512:(b + 1) * 512],
                                          start=False, stop=False,
                                          skip_group_check=True)
                            mi = tensor.matmul(pO[:, 0:512], S_sb[:, 128:256],
                                               qfm[:, b * 512:(b + 1) * 512],
                                               start=False, stop=True,
                                               skip_group_check=True)
                        s["EV"].wait_n(tensor, ktm_all[-1])
                        if h > 0 and b == 0:
                            s["EV"].wait_n(tensor, scm_all[-1])
                        for i in range(4):
                            c = 4 * b + i
                            vsl = v_sb[:, c * D + h * 128:c * D + (h + 1) * 128]
                            tensor.matmul(pS[:, 0:128], kfT[:, c * DF:c * DF + 128],
                                          vsl, start=(b == 0 and i == 0), stop=False,
                                          skip_group_check=True)
                            mi = tensor.matmul(pS[:, 128:256],
                                               kfT[:, c * DF + 128:c * DF + 256], vsl,
                                               start=False,
                                               stop=(b == 3 and i == 3),
                                               skip_group_check=True)
                        s["PD"].inc(mi)

                    @block.vector
                    def _(vector, h=h, b=b):
                        s["PD"].wait(vector)
                        s["EV"].inc(vector.tensor_copy(
                            o_full[:, h * T + b * 512:h * T + (b + 1) * 512],
                            pO[:, 0:512]))
                        om_all.append(s["EV"].mark())
                        if b < 3:
                            s["EV"].inc(vector.tensor_copy(S_sb[:, :], pS[:, 0:256]))
                        else:
                            s["EV"].inc(vector.tensor_copy(
                                Sstage[:, h * DF:(h + 1) * DF], pS[:, 0:256]))
                        scm_all.append(s["EV"].mark())

            @block.sync
            def _(sync):
                s["EV"].wait(sync)
                s["DSS"].inc(sync.dma_start(out=cc_S_i[:, :], in_=Sstage), 16)
                s["DS"].inc(sync.dma_start(out=scr_o[:, :], in_=o_full[:, :]), 16)
                s["DSQ"].wait(sync)
                s["DSK"].wait(sync)
                s["DG"].wait(sync)
                s["DS"].wait(sync)

            @block.gpsimd
            def _(gpsimd):
                s["DSS"].wait(gpsimd)
                s["CC"].inc(gpsimd.collective_compute(
                    "AllGather", ALU.bypass, replica_groups=[list(range(8))],
                    ins=[cc_S_i[:, :].opt()], outs=[cc_S_o[:, :].opt()]))

            _barrier(block, s)

    _phase2()

    def _phase1():
        NSL = FF // 256
        with (
            nc.sbuf_tensor("xnqT", [128, KD * T], BF16) as xnqT,
            nc.sbuf_tensor("wgT", [128, KD * FF], BF16) as wgT,
            nc.sbuf_tensor("wuT", [128, KD * FF], BF16) as wuT,
            nc.sbuf_tensor("sg", [128, 4 * 256], F32) as sg,
            nc.sbuf_tensor("h_sb", [128, 2 * FF], F32) as h_sb,
            nc.sbuf_tensor("fS", [128, 4 * (FF // 16)], F32) as fS,
            nc.sbuf_tensor("fB", [128, 3 * FF], F32) as fB,
            nc.sbuf_tensor("fW", [128, FF], BF16) as fW,
            nc.sbuf_tensor("hq_b", [128, 2 * FF], BF16) as hq_b,
            nc.psum_tensor("pf", [128, 4096], F32) as pf,
        ):
            NB = FF // 16
            fqb = dict(amax=fS[:, 0:NB], sc=fS[:, NB:2 * NB],
                       rec=fS[:, 2 * NB:3 * NB],
                       v=fB[:, 0:FF], w=fW[:, 0:FF],
                       M=fB[:, FF:2 * FF], tq=fB[:, 2 * FF:3 * FF])

            @block.sync
            def _(sync):
                for k in range(KD):
                    s["DS"].inc(sync.dma_start(out=xnqT[:, k * T:(k + 1) * T],
                                               in_=scr_xnq[:, k * 128:(k + 1) * 128],
                                               transpose=True), 16)
                sync.wait_ge(s["CC"].h, 1)
                for k in range(KD):
                    s["DS"].inc(sync.dma_start(out=wgT[:, k * FF:(k + 1) * FF],
                                               in_=cc_wg_o[:, k * 128:(k + 1) * 128],
                                               transpose=True), 16)
                sync.wait_ge(s["CC"].h, 2)
                for k in range(KD):
                    s["DS"].inc(sync.dma_start(out=wuT[:, k * FF:(k + 1) * FF],
                                               in_=cc_wu_o[:, k * 128:(k + 1) * 128],
                                               transpose=True), 16)

            sim1 = []
            hmm1 = []
            dsm1 = {}
            vdm1 = {}

            def q_p1(tt):
                hb = tt % 2
                srcq = h_sb[:, hb * FF:(hb + 1) * FF]

                @block.vector
                def _(vector):
                    s["VD"].wait(vector)
                    src3 = srcq.rearrange("p (nb k) -> p nb k", k=16)
                    s["VD"].inc(vector.tensor_reduce(
                        fqb["amax"][:, 0:NB].unsqueeze(2), src3, AX.X, ALU.max,
                        apply_absolute_value=True))
                    s["VD"].wait(vector)
                    s["VD"].inc(vector.tensor_scalar(
                        fqb["sc"][:, 0:NB], fqb["amax"][:, 0:NB],
                        1.0 / 6.0, 1e-12, ALU.mult, ALU.max))
                    s["VD"].wait(vector)
                    s["VD"].inc(vector.reciprocal(fqb["rec"][:, 0:NB],
                                                  fqb["sc"][:, 0:NB]))
                    s["VD"].wait(vector)
                    rbc = fqb["rec"][:, 0:NB].unsqueeze(2).broadcast_to([128, NB, 16])
                    s["VD"].inc(vector.tensor_tensor(
                        fqb["v"][:, 0:FF].rearrange("p (nb k) -> p nb k", k=16),
                        src3, rbc, ALU.mult))

                @block.gpsimd
                def _(gpsimd):
                    s["VD"].wait(gpsimd)
                    s["GD"].inc(gpsimd.tensor_tensor(
                        fqb["w"][:, 0:FF], fqb["v"][:, 0:FF], fqb["v"][:, 0:FF],
                        ALU.mult))

            def q_p2(tt):
                gm = s["GD"].mark()

                @block.vector
                def _(vector):
                    s["GD"].wait_n(vector, gm)
                    s["VD"].inc(vector.tensor_scalar(
                        fqb["M"][:, 0:FF], fqb["w"][:, 0:FF], QC1, QC0,
                        ALU.mult, ALU.add))

                @block.gpsimd
                def _(gpsimd):
                    s["VD"].wait(gpsimd)
                    s["GD"].inc(gpsimd.tensor_tensor(
                        fqb["tq"][:, 0:FF], fqb["v"][:, 0:FF], fqb["M"][:, 0:FF],
                        ALU.add))

            def q_p3(tt):
                hb = tt % 2
                gm = s["GD"].mark()

                @block.vector
                def _(vector, tt=tt, hb=hb):
                    s["GD"].wait_n(vector, gm)
                    if tt >= 2:
                        s["DS"].wait_n(vector, dsm1[tt - 1])
                    s["VD"].inc(vector.tensor_tensor(
                        fqb["tq"][:, 0:FF], fqb["tq"][:, 0:FF], fqb["M"][:, 0:FF],
                        ALU.subtract))
                    s["VD"].wait(vector)
                    sbc = fqb["sc"][:, 0:NB].unsqueeze(2).broadcast_to([128, NB, 16])
                    s["VD"].inc(vector.tensor_tensor(
                        hq_b[:, hb * FF:(hb + 1) * FF].rearrange(
                            "p (nb k) -> p nb k", k=16),
                        fqb["tq"][:, 0:FF].rearrange("p (nb k) -> p nb k", k=16),
                        sbc, ALU.mult))
                vdm1[tt] = s["VD"].mark()

            def q_stash(tt):
                hb = tt % 2

                @block.sync
                def _(sync, tt=tt, hb=hb):
                    s["VD"].wait_n(sync, vdm1[tt])
                    s["DS"].inc(sync.dma_start(out=scr_h[tt * 128:(tt + 1) * 128, :],
                                               in_=hq_b[:, hb * FF:(hb + 1) * FF]), 16)
                dsm1[tt] = s["DS"].mark()

            for t in range(NT):
                hb = t % 2
                for sl in range(NSL):
                    S1 = t * NSL + sl
                    bk = S1 % 8
                    rb = S1 % 4
                    c0 = bk * 512

                    @block.tensor
                    def _(tensor, t=t, sl=sl, S1=S1, c0=c0):
                        if S1 == 0:
                            s["DS"].wait(tensor)
                        if S1 >= 8:
                            s["EV"].wait_n(tensor, sim1[S1 - 7])
                            s["VD"].wait_n(tensor, hmm1[S1 - 8])
                        n0 = sl * 256
                        for k in range(KD):
                            lhs = xnqT[:, k * T + t * 128:k * T + (t + 1) * 128]
                            tensor.matmul(pf[:, c0:c0 + 256], lhs,
                                          wgT[:, k * FF + n0:k * FF + n0 + 256],
                                          start=(k == 0), stop=False,
                                          skip_group_check=True)
                            mi = tensor.matmul(pf[:, c0 + 256:c0 + 512], lhs,
                                               wuT[:, k * FF + n0:k * FF + n0 + 256],
                                               start=False, stop=(k == KD - 1),
                                               skip_group_check=True)
                        s["PD"].inc(mi)

                    @block.scalar
                    def _(scalar, S1=S1, rb=rb, c0=c0):
                        s["PD"].wait(scalar)
                        if S1 >= 4:
                            s["VD"].wait_n(scalar, hmm1[S1 - 4])
                        s["EV"].inc(scalar.activation(sg[:, rb * 256:(rb + 1) * 256],
                                                      pf[:, c0:c0 + 256], AF.Silu))
                    sim1.append(s["EV"].mark())

                    @block.vector
                    def _(vector, t=t, sl=sl, S1=S1, rb=rb, c0=c0, hb=hb):
                        s["PD"].wait(vector)
                        s["EV"].wait_n(vector, sim1[S1])
                        s["VD"].inc(vector.tensor_tensor(
                            h_sb[:, hb * FF + sl * 256:hb * FF + (sl + 1) * 256],
                            pf[:, c0 + 256:c0 + 512],
                            sg[:, rb * 256:(rb + 1) * 256], ALU.mult))
                    hmm1.append(s["VD"].mark())

                    if t >= 1:
                        if sl == 1:
                            q_p1(t - 1)
                        elif sl == 4:
                            q_p2(t - 1)
                        elif sl == 8:
                            q_p3(t - 1)
                        elif sl == 9:
                            q_stash(t - 1)

            q_p1(NT - 1)
            q_p2(NT - 1)
            q_p3(NT - 1)
            q_stash(NT - 1)

            @block.sync
            def _(sync):
                s["DS"].wait(sync)

            _barrier(block, s)

    _phase1()


    g_sb = stack.enter_context(nc.sbuf_tensor("g_sb", [128, NT * D], BF16))
    Sg = stack.enter_context(nc.sbuf_tensor("Sg", [128, 2 * H * DF], BF16))
    S_pre = stack.enter_context(nc.sbuf_tensor("S_pre", [128, H * DF], BF16))
    woT = stack.enter_context(nc.sbuf_tensor("woT", [128, KD * D], BF16))

    def _phase4_prework():
        for r in range(8):
            rb = r % 2

            @block.sync
            def _(sync, r=r, rb=rb):
                if r == 0:
                    sync.wait_ge(s["CC"].h, 4)
                if r >= 2:
                    s["VD"].wait(sync)
                s["DSS"].inc(sync.dma_start(
                    out=Sg[:, rb * H * DF:(rb + 1) * H * DF],
                    in_=cc_S_o[r * 128:(r + 1) * 128, :]), 16)

            @block.vector
            def _(vector, r=r, rb=rb):
                s["DSS"].wait(vector)
                s["VD"].wait(vector)
                if r == 0:
                    li = vector.tensor_scalar(S_pre[:, :], Sg[:, 0:H * DF],
                                              smask_sb[:, 0:1], None, ALU.mult)
                else:
                    li = vector.scalar_tensor_tensor(
                        S_pre[:, :], Sg[:, rb * H * DF:(rb + 1) * H * DF],
                        smask_sb[:, r:r + 1], S_pre[:, :], ALU.mult, ALU.add)
                s["VD"].inc(li)

        @block.gpsimd
        def _(gpsimd):
            for k in range(KD):
                s["DG"].inc(gpsimd.dma_start(out=woT[:, k * D:(k + 1) * D],
                                             in_=wo_T[k * 128:(k + 1) * 128, :]), 16)

        @block.vector
        def _(vector):
            s["DG"].wait(vector)
            li = None
            for k in range(KD):
                li = vector.tensor_scalar(woT[:, k * D:(k + 1) * D],
                                          woT[:, k * D:(k + 1) * D],
                                          bc(RMS), None, ALU.mult)
            s["VD"].inc(li)

    def _phase3():
        with (
            nc.sbuf_tensor("wdT", [128, NFK * D], BF16) as wdT,
            nc.sbuf_tensor("hqT", [128, 2 * NFK * 512], BF16) as hqT,
            nc.psum_tensor("pD", [128, 3072], F32) as pDm,
        ):
            pD = (pDm[:, 0:1024], pDm[:, 1024:2048], pDm[:, 2048:3072])

            @block.sync
            def _(sync):
                sync.wait_ge(s["CC"].h, 3)
                for n in range(NFK):
                    s["DS"].inc(sync.dma_start(out=wdT[:, n * D:(n + 1) * D],
                                               in_=cc_wd_o[:, n * 128:(n + 1) * 128],
                                               transpose=True), 16)

            p3m = []
            pdm3 = []
            for tb in range(4):
                hb = tb % 2

                @block.sync
                def _(sync, tb=tb, hb=hb):
                    if tb >= 2:
                        s["PD"].wait_n(sync, pdm3[tb - 2])
                    for n in range(NFK):
                        s["DA"].inc(sync.dma_start(
                            out=hqT[:, (hb * NFK + n) * 512:(hb * NFK + n + 1) * 512],
                            in_=scr_h[tb * 512:(tb + 1) * 512, n * 128:(n + 1) * 128],
                            transpose=True), 16)

                for ti in range(4):
                    t = tb * 4 + ti
                    pb = t % 3

                    @block.tensor
                    def _(tensor, hb=hb, ti=ti, t=t, pb=pb):
                        if t == 0:
                            s["DS"].wait(tensor)
                        if ti == 0:
                            s["DA"].wait(tensor)
                        if t >= 2:
                            s["EV"].wait_n(tensor, p3m[t - 2])
                        for n in range(NFK):
                            lhs = hqT[:, (hb * NFK + n) * 512 + ti * 128:
                                      (hb * NFK + n) * 512 + (ti + 1) * 128]
                            tensor.matmul(pD[pb][:, 0:512], lhs, wdT[:, n * D:n * D + 512],
                                          start=(n == 0), stop=(n == NFK - 1),
                                          skip_group_check=True)
                            mi = tensor.matmul(pD[pb][:, 512:1024], lhs,
                                               wdT[:, n * D + 512:(n + 1) * D],
                                               start=(n == 0), stop=(n == NFK - 1),
                                               skip_group_check=True)
                        s["PD"].inc(mi)

                    @block.scalar
                    def _(scalar, t=t, pb=pb):
                        s["PD"].wait(scalar)
                        s["EV"].inc(scalar.activation(g_sb[:, t * D:(t + 1) * D],
                                                      pD[pb][:, :], AF.Silu))
                    p3m.append(s["EV"].mark())
                pdm3.append(s["PD"].mark())

            _barrier(block, s)

    _phase4_prework()
    _phase3()

    s_sb = stack.enter_context(nc.sbuf_tensor("s_sb", [128, NT * D], BF16))

    def _phase4():
        with (
            nc.sbuf_tensor("qf2", [128, 2 * T], BF16) as qf2,
            nc.sbuf_tensor("o2", [128, T], BF16) as o2,
            nc.sbuf_tensor("lsum", [128, T], F32) as lsum,
            nc.sbuf_tensor("rstd", [128, T], BF16) as rstd,
            nc.sbuf_tensor("o_nrm", [128, H * T], BF16) as o_nrm,
            nc.sbuf_tensor("o_full4", [128, H * T], BF16) as o_full,
            nc.sbuf_tensor("x2", [128, 2 * D], F32) as x2,
            nc.sbuf_tensor("sgp", [128, 2 * D], F32) as sgp,
            nc.psum_tensor("pX", [128, 4096], F32) as pX,
        ):
            qfp2, qfm2 = qf2[:, 0:T], qf2[:, T:2 * T]
            pC = (pX[:, 0:512], pX[:, 512:1024], pX[:, 2048:2560])
            pR = (pX[:, 1024:1536], pX[:, 1536:2048])
            pW = (pX[:, 2048:3072], pX[:, 3072:4096])

            @block.sync
            def _(sync):
                s["DS"].inc(sync.dma_start(out=o_full[:, :], in_=scr_o[:, :]), 16)

            cpm = []
            for h in range(H):
                @block.sync
                def _(sync, h=h):
                    if h >= 1:
                        s["PD"].wait(sync)
                    s["DS"].inc(sync.dma_start(out=qfp2,
                                               in_=scr_qf[h * DF:h * DF + 128, :]), 16)
                    s["DS"].inc(sync.dma_start(out=qfm2,
                                               in_=scr_qf[h * DF + 128:(h + 1) * DF, :]), 16)

                for tc in range(4):
                    pb = (h * 4 + tc) % 3

                    @block.tensor
                    def _(tensor, h=h, tc=tc, pb=pb):
                        if tc == 0:
                            s["DS"].wait(tensor)
                            if h == 0:
                                s["VD"].wait(tensor)
                        if h * 4 + tc >= 3:
                            s["EV"].wait_n(tensor, cpm[h * 4 + tc - 3])
                        tensor.matmul(pC[pb], S_pre[:, h * DF:h * DF + 128],
                                      qfp2[:, tc * 512:(tc + 1) * 512],
                                      start=True, stop=False, skip_group_check=True)
                        mi = tensor.matmul(pC[pb], S_pre[:, h * DF + 128:(h + 1) * DF],
                                           qfm2[:, tc * 512:(tc + 1) * 512],
                                           start=False, stop=True, skip_group_check=True)
                        s["PD"].inc(mi)

                    @block.vector
                    def _(vector, h=h, tc=tc, pb=pb):
                        s["PD"].wait(vector)
                        if h == 0 and tc == 0:
                            s["DS"].wait(vector)
                        sl = slice(h * T + tc * 512, h * T + (tc + 1) * 512)
                        s["EV"].inc(vector.tensor_tensor(o_full[:, sl], o_full[:, sl],
                                                         pC[pb], ALU.add))
                    cpm.append(s["EV"].mark())

                @block.vector
                def _(vector, h=h):
                    s["EV"].wait(vector)
                    s["PD"].wait(vector)
                    s["EV"].inc(vector.scalar_tensor_tensor(
                        o2[:, :], o_full[:, h * T:(h + 1) * T], 2.0 ** -16,
                        o_full[:, h * T:(h + 1) * T], ALU.mult, ALU.mult))

                for tc in range(4):
                    pb = tc % 2

                    @block.tensor
                    def _(tensor, tc=tc, pb=pb):
                        s["EV"].wait(tensor)
                        mi = tensor.matmul(pR[pb], ones_b[:, :],
                                           o2[:, tc * 512:(tc + 1) * 512],
                                           start=True, stop=True, skip_group_check=True)
                        s["PD"].inc(mi)

                    @block.scalar
                    def _(scalar, tc=tc, pb=pb):
                        s["PD"].wait(scalar)
                        s["EV"].inc(scalar.activation(lsum[:, tc * 512:(tc + 1) * 512],
                                                      pR[pb], AF.Ln, bias=eps2_sb[:, 0:1],
                                                      scale=2.0 ** -48 / DV))

                @block.scalar
                def _(scalar, h=h):
                    li = None
                    for tc in range(4):
                        li = scalar.activation(rstd[:, tc * 512:(tc + 1) * 512],
                                               lsum[:, tc * 512:(tc + 1) * 512],
                                               AF.Exp, scale=-0.5, bias=nl2_sb[:, 0:1])
                    s["EV"].inc(li)

                @block.vector
                def _(vector, h=h):
                    s["EV"].wait(vector)
                    s["EV"].inc(vector.tensor_tensor(o_nrm[:, h * T:(h + 1) * T],
                                                     o_full[:, h * T:(h + 1) * T],
                                                     rstd[:, :], ALU.mult))

            wom = []
            vm5 = []
            ssm5 = []

            def emit_y(t):
                pb5 = t % 2

                @block.sync
                def _(sync, t=t, pb5=pb5):
                    if t >= 2:
                        s["VD"].wait_n(sync, vm5[t - 2])
                    s["DS"].inc(sync.dma_start(out=x2[:, pb5 * D:(pb5 + 1) * D],
                                               in_=x_in[t * 128:(t + 1) * 128, :]), 16)

                @block.vector
                def _(vector, t=t, pb5=pb5):
                    s["DS"].wait(vector)
                    s["EV"].wait_n(vector, wom[t])
                    if t >= 2:
                        s["DSS"].wait_n(vector, ssm5[t - 2])
                    s["VD"].inc(vector.tensor_tensor(sgp[:, pb5 * D:(pb5 + 1) * D],
                                                     s_sb[:, t * D:(t + 1) * D],
                                                     g_sb[:, t * D:(t + 1) * D],
                                                     ALU.mult))
                    s["VD"].wait(vector)
                    s["VD"].inc(vector.tensor_tensor(sgp[:, pb5 * D:(pb5 + 1) * D],
                                                     x2[:, pb5 * D:(pb5 + 1) * D],
                                                     sgp[:, pb5 * D:(pb5 + 1) * D],
                                                     ALU.add))
                vm5.append(s["VD"].mark())

                @block.sync
                def _(sync, t=t, pb5=pb5):
                    s["VD"].wait_n(sync, vm5[t])
                    s["DSS"].inc(sync.dma_start(out=y_out[t * 128:(t + 1) * 128, :],
                                                in_=sgp[:, pb5 * D:(pb5 + 1) * D]), 16)
                ssm5.append(s["DSS"].mark())

            for t in range(NT):
                pb = t % 2

                @block.tensor
                def _(tensor, t=t, pb=pb):
                    if t == 0:
                        s["VD"].wait(tensor)
                    if t < 1:
                        s["EV"].wait(tensor)
                    elif t >= 2:
                        s["EV"].wait_n(tensor, wom[t - 2])
                    for k in range(KD):
                        lhs = o_nrm[:, k * T + t * 128:k * T + (t + 1) * 128]
                        tensor.matmul(pW[pb][:, 0:512], lhs, woT[:, k * D:k * D + 512],
                                      start=(k == 0), stop=(k == KD - 1),
                                      skip_group_check=True)
                        mi = tensor.matmul(pW[pb][:, 512:1024], lhs,
                                           woT[:, k * D + 512:(k + 1) * D],
                                           start=(k == 0), stop=(k == KD - 1),
                                           skip_group_check=True)
                    s["PD"].inc(mi)

                @block.vector
                def _(vector, t=t, pb=pb):
                    s["PD"].wait(vector)
                    s["EV"].inc(vector.tensor_copy(s_sb[:, t * D:(t + 1) * D],
                                                   pW[pb][:, :]))
                wom.append(s["EV"].mark())
                emit_y(t)

            @block.sync
            def _(sync):
                s["DSS"].wait(sync)

    _phase4()


    if DEBUG_TAPS:
        taps = {
            "dbg_xn": (scr_xn, [T, D]), "dbg_xnq": (scr_xnq, [T, D]),
            "dbg_h": (scr_h, [T, FF]), "dbg_qf": (scr_qf, [H * DF, T]),
            "dbg_kf": (scr_kf, [H * DF, T]),
            "dbg_wg": (cc_wg_o, [FF, D]), "dbg_wu": (cc_wu_o, [FF, D]),
            "dbg_wd": (cc_wd_o, [D, FF]),
            "dbg_Si": (cc_S_i, [128, H * DF]), "dbg_So": (cc_S_o, [8 * 128, H * DF]),
        }
        taps["dbg_ofull"] = (scr_o, [128, H * T])
        dbg_outs = {k: nc.dram_tensor(k, shp, BF16, kind="ExternalOutput").ap()
                    for k, (_, shp) in taps.items()}
        sb_taps = {"dbg_g": (g_sb, [128, NT * D]), "dbg_s": (s_sb, [128, NT * D])}
        for k, (_, shp) in sb_taps.items():
            dbg_outs[k] = nc.dram_tensor(k, shp, BF16, kind="ExternalOutput").ap()

        @block.sync
        def _(sync):
            for k, (src_t, shp) in taps.items():
                s["DS"].inc(sync.dma_start(out=dbg_outs[k][:, :], in_=src_t[:, :]), 16)
            for k, (src_t, shp) in sb_taps.items():
                s["DS"].inc(sync.dma_start(out=dbg_outs[k][:, :], in_=src_t[:, :]), 16)
            s["DS"].wait(sync)

    stack.close()
    return nc


def kernel(L0_attn_x, L0_attn_Wq, L0_attn_Wk, L0_attn_Wv, L0_attn_Wo,
           L0_attn_fmq_w, L0_attn_fmq_b, L0_attn_fmk_w, L0_attn_fmk_b,
           L0_attn_rms_w, L0_ffn_w_gate, L0_ffn_w_up, L0_ffn_w_down):
    f32 = np.float32
    x = np.ascontiguousarray(np.asarray(L0_attn_x, dtype=f32))
    Wq = np.asarray(L0_attn_Wq, dtype=f32)
    Wk = np.asarray(L0_attn_Wk, dtype=f32)
    Wv = np.asarray(L0_attn_Wv, dtype=f32)
    Wo = np.asarray(L0_attn_Wo, dtype=f32)
    fmq_w = np.asarray(L0_attn_fmq_w, dtype=f32)
    fmk_w = np.asarray(L0_attn_fmk_w, dtype=f32)
    fmq_b = np.asarray(L0_attn_fmq_b, dtype=f32)
    fmk_b = np.asarray(L0_attn_fmk_b, dtype=f32)
    rms_w = np.asarray(L0_attn_rms_w, dtype=f32)
    w_gate = np.asarray(L0_ffn_w_gate, dtype=f32)
    w_up = np.asarray(L0_ffn_w_up, dtype=f32)
    w_down = np.asarray(L0_ffn_w_down, dtype=f32)

    wq_t = np.ascontiguousarray(Wq.T.reshape(D, H, 128).transpose(1, 0, 2))
    wk_t = np.ascontiguousarray(Wk.T.reshape(D, H, 128).transpose(1, 0, 2))
    wv_Tc = np.ascontiguousarray(Wv.T)
    wo_Tc = np.ascontiguousarray(Wo.T)
    fmq_Tc = np.ascontiguousarray(fmq_w.T)
    fmk_Tc = np.ascontiguousarray(fmk_w.T)
    fmq_bc = np.ascontiguousarray(fmq_b.reshape(128, 1))
    fmk_bc = np.ascontiguousarray(fmk_b.reshape(128, 1))
    rms_cc = np.ascontiguousarray(rms_w.reshape(128, 1))

    nc = build_nc()

    in_maps = []
    for c in range(8):
        b, q = c // 4, c % 4
        msk = np.zeros((128, 8), dtype=np.float32)
        msk[:, 4 * b:4 * b + q] = 1.0
        in_maps.append({
            "x": np.ascontiguousarray(x[b, q * T:(q + 1) * T, :]),
            "wq_t": wq_t, "wk_t": wk_t, "wv_T": wv_Tc, "wo_T": wo_Tc,
            "fmq_T": fmq_Tc, "fmk_T": fmk_Tc, "fmq_b": fmq_bc, "fmk_b": fmk_bc,
            "rms_c": rms_cc,
            "wg_sl": np.ascontiguousarray(w_gate[c * WSL:(c + 1) * WSL, :]),
            "wu_sl": np.ascontiguousarray(w_up[c * WSL:(c + 1) * WSL, :]),
            "wd_sl": np.ascontiguousarray(w_down[c * 128:(c + 1) * 128, :]),
            "smask": msk,
        })

    res = run_bass_kernel_spmd(nc, in_maps, core_ids=list(range(8)))
    globals()["_last_results"] = res.results

    out = np.empty((2, 8192, D), dtype=f32)
    for c in range(8):
        b, q = c // 4, c % 4
        out[b, q * T:(q + 1) * T, :] = np.asarray(res.results[c]["y"])
    return out
